# version 15
# baseline (speedup 1.0000x reference)
import math
import sys as _sys
import numpy as np
import concourse.bass as bass
import concourse.mybir as mybir
from concourse.bass_utils import run_bass_kernel_spmd
from contextlib import ExitStack

F32 = mybir.dt.float32
BF16 = mybir.dt.bfloat16
AF = mybir.ActivationFunctionType
ALU = mybir.AluOpType
AX = mybir.AxisListType

B, T, D = 16, 2048, 1024
NCORES = 8
SEQ_PER_CORE = B // NCORES
TC = 256
NCHUNK = T // TC
R = SEQ_PER_CORE * TC
WBLK = 516
H_GLA, DK, DV = 4, 256, 256
FFN_HEADS, DIM_HEAD, WINDOW, INNER = 8, 64, 512, 512
HID = 2730
MLP_H = 3072
NEG = -1.0e30

_DBG = {"n_chunks": NCHUNK, "dumps": []}


class Em:
    CENG = ("tensor", "vector", "scalar", "gpsimd", "sync")
    QUEUES = {"qs": "sync", "qa": "scalar", "qg": "gpsimd"}

    def __init__(self, nc):
        self.nc = nc
        self.ops = {e: [] for e in self.CENG}
        self.nq = {q: 0 for q in self.QUEUES}
        self._g = 0
        self.regions = {}
        self._rid = 0
        self.stable = set()
        self.junk = {}

    def region(self, name=None, excl=False):
        name = name or f"r{self._rid}"
        self._rid += 1
        self.regions[name] = {"w": None, "r": [], "x": excl}
        return name

    def _need(self, dep, eng, deps):
        kind = dep[0]
        if kind == "c" and dep[1] == eng:
            return
        key = (kind, dep[1])
        deps[key] = max(deps.get(key, -1), dep[2])

    def op(self, eng, fn, reads=(), writes=(), scalar_reads=(), tag="",
           tail=False, cost=500, actset=None):
        deps = {}
        hazard = 0
        hsrc = []
        idx = len(self.ops[eng])
        for r in list(reads) + list(scalar_reads):
            st = self.regions[r]
            if st.get("x"):
                for rd in st["r"]:
                    if rd[0] != "c" or rd[1] != eng:
                        self._need(rd, eng, deps)
            if st["w"] is not None:
                self._need(st["w"], eng, deps)
                w = st["w"]
                if st.get("wt") and w[0] == "c" and w[1] == eng:
                    hsrc.append(("t", w[1], w[2]))
                    if w[2] >= idx - 2:
                        hazard = max(hazard, 1)
                if r in scalar_reads and r not in self.stable:
                    hsrc.append(("s", None, None))
                    if w[0] != "c" or w[1] != eng or w[2] >= idx - 2:
                        hazard = max(hazard, 1)
        for r in writes:
            st = self.regions[r]
            if st["w"] is not None:
                self._need(st["w"], eng, deps)
            for rd in st["r"]:
                self._need(rd, eng, deps)
        rec = {"fn": fn, "deps": deps, "inc": False, "dma": None,
               "hazard": hazard, "eng": eng, "idx": idx, "tail": tail,
               "cost": cost, "hsrc": hsrc, "g": self._g, "actset": actset,
               "rd": list(reads) + list(scalar_reads), "sr": list(scalar_reads),
               "wr": list(writes),
               "tag": tag or f"L{_sys._getframe(2).f_lineno}"}
        self._g += 1
        self.ops[eng].append(rec)
        me = ("c", eng, idx)
        for r in reads:
            self.regions[r]["r"].append(me)
        for r in scalar_reads:
            self.regions[r]["r"].append(me)
        for r in writes:
            self.regions[r] = {"w": me, "r": [], "wt": tail,
                               "x": self.regions[r].get("x", False)}
        return me

    def dma(self, queue, fn, reads=(), writes=(), tag="", cost=3000):
        eng = self.QUEUES[queue]
        deps = {}
        idx = len(self.ops[eng])
        for r in reads:
            st = self.regions[r]
            if st["w"] is not None:
                self._need(st["w"], eng, deps)
        for r in writes:
            st = self.regions[r]
            if st["w"] is not None:
                self._need(st["w"], eng, deps)
            for rd in st["r"]:
                self._need(rd, eng, deps)
        ordinal = self.nq[queue]
        self.nq[queue] += 1
        rec = {"fn": fn, "deps": deps, "inc": True, "dma": queue, "ord": ordinal,
               "hazard": 0, "eng": eng, "idx": idx, "tail": False,
               "cost": cost, "hsrc": [], "g": self._g,
               "rd": list(reads), "sr": [], "wr": list(writes),
               "tag": tag or f"L{_sys._getframe(2).f_lineno}"}
        self._g += 1
        self.ops[eng].append(rec)
        me = ("q", queue, ordinal)
        for r in reads:
            self.regions[r]["r"].append(me)
        for r in writes:
            self.regions[r] = {"w": me, "r": [], "wt": False,
                               "x": self.regions[r].get("x", False)}
        return me

    def barrier_all(self, region_names):
        pass


    def _reschedule(self):
        allops = sorted((r for ops in self.ops.values() for r in ops),
                        key=lambda r: r["g"])
        nid = {r["g"]: i for i, r in enumerate(allops)}
        edges = [set() for _ in allops]

        regs = {name: {"w": None, "r": [], "x": st.get("x", False),
                       "wt": False}
                for name, st in self.regions.items()}
        for i, r in enumerate(allops):
            for rg in r["rd"]:
                st = regs[rg]
                if st["x"]:
                    for (rdid, rdeng) in st["r"]:
                        if rdeng != r["eng"]:
                            edges[i].add(rdid)
                if st["w"] is not None:
                    edges[i].add(st["w"])
            for rg in r["wr"]:
                st = regs[rg]
                if st["w"] is not None:
                    edges[i].add(st["w"])
                for (rdid, _e) in st["r"]:
                    edges[i].add(rdid)
            edges[i].discard(i)
            for rg in r["rd"]:
                regs[rg]["r"].append((i, r["eng"]))
            for rg in r["wr"]:
                regs[rg] = {"w": i, "r": [], "x": regs[rg]["x"],
                            "wt": r["tail"]}

        ndep = [len(e) for e in edges]
        consumers = [[] for _ in allops]
        for i, e in enumerate(edges):
            for p in e:
                consumers[p].append(i)
        pend = {e: [] for e in self.CENG}
        import heapq
        wait_cnt = list(ndep)
        for i, r in enumerate(allops):
            if wait_cnt[i] == 0:
                heapq.heappush(pend[r["eng"]], (r["g"], i))
        etime = {e: 0.0 for e in self.CENG}
        qtime = {q: 0.0 for q in self.QUEUES}
        fin = [0.0] * len(allops)
        order = {e: [] for e in self.CENG}
        nsched = 0
        import itertools
        cur_set = None
        GWIN = 1200
        while nsched < len(allops):
            cands = [e for e in self.CENG if pend[e]]
            assert cands, "scheduler cycle"
            eng = min(cands, key=lambda e: etime[e])
            if eng == "scalar":
                tmp = []
                best = None
                while pend[eng]:
                    it = heapq.heappop(pend[eng])
                    tmp.append(it)
                    aset = allops[it[1]].get("actset")
                    if aset is None or aset == cur_set:
                        best = it
                        break
                if best is None or best[0] > tmp[0][0] + GWIN:
                    best = tmp[0]
                for it in tmp:
                    if it is not best:
                        heapq.heappush(pend[eng], it)
                g, i = best
            else:
                g, i = heapq.heappop(pend[eng])
            r = allops[i]
            start = max([etime[eng]] + [fin[p] for p in edges[i]])
            cost = r["cost"]
            aset = r.get("actset")
            if aset is not None and aset != cur_set and eng == "scalar":
                cur_set = aset
                cost += 1280.0
            if r["dma"] is not None:
                qs = max(start, qtime[r["dma"]])
                qtime[r["dma"]] = qs + cost
                fin[i] = qs + cost
                etime[eng] = start + 100.0
            else:
                fin[i] = start + cost
                etime[eng] = fin[i]
            order[eng].append(r)
            nsched += 1
            for c in consumers[i]:
                wait_cnt[c] -= 1
                if wait_cnt[c] == 0:
                    heapq.heappush(pend[allops[c]["eng"]],
                                   (allops[c]["g"], c))
        return allops, nid, edges, order

    def finalize(self):
        nc = self.nc
        allops, nid, edges, order = self._reschedule()
        pos = {}
        for e, lst in order.items():
            for k, r in enumerate(lst):
                pos[r["g"]] = k
        gid = {r["g"]: i for i, r in enumerate(allops)}
        qord = {}
        cnt = {q: 0 for q in self.QUEUES}
        for e in self.CENG:
            pass
        for e, lst in order.items():
            for r in lst:
                if r["dma"] is not None:
                    qord[r["g"]] = cnt[r["dma"]]
                    cnt[r["dma"]] += 1
        node_of = {r["g"]: i for i, r in enumerate(allops)}
        need_inc = set()
        dep_nodes = [[] for _ in allops]
        for i, e in enumerate(edges):
            r = allops[i]
            for p in e:
                pr = allops[p]
                if pr["eng"] != r["eng"] or pr["dma"] is not None:
                    dep_nodes[i].append(p)
                    if pr["dma"] is None:
                        need_inc.add(p)
        cum = {}
        for e, lst in order.items():
            c = 0
            cc = []
            for r in lst:
                if r["dma"] is None and node_of[r["g"]] in need_inc:
                    r["inc"] = True
                if r["inc"] and r["dma"] is None:
                    c += 1
                cc.append(c)
            cum[e] = cc
        for e, lst in order.items():
            lastw = {}
            for k, r in enumerate(lst):
                r["hazard"] = 0
                if any(x not in self.stable for x in r["sr"]):
                    r["hazard"] = 1
                for (kind, heng, hidx) in r["hsrc"]:
                    if kind != "t":
                        continue
                for p in dep_nodes[node_of[r["g"]]]:
                    pass
            posk = {r["g"]: k for k, r in enumerate(lst)}
            for k, r in enumerate(lst):
                i = node_of[r["g"]]
                for p in edges[i]:
                    pr = allops[p]
                    if (pr["eng"] == e and pr["dma"] is None and pr["tail"]
                            and k - posk[pr["g"]] <= 2):
                        r["hazard"] = max(r["hazard"], 1)

        es = ExitStack()
        KPOOL = 8
        sems = {}
        pools = {}
        for e in self.CENG:
            sems[e] = es.enter_context(nc.semaphore(f"sem_{e}"))
        for q in self.QUEUES:
            pools[q] = [es.enter_context(nc.semaphore(f"sem_{q}{i}"))
                        for i in range(KPOOL)]
        allsems = list(sems.values()) + [s for p in pools.values() for s in p]
        nums = sorted(h.num for h in allsems)
        rng = range(nums[0], nums[-1] + 1)

        with es:
            nc.all_engine_barrier()
            nc.gpsimd.dma_reset(rng)
            nc.gpsimd.sem_clear(rng)
            nc.all_engine_barrier()
            with nc.Block() as block:
                def section(eng):
                    def body(engobj):
                        water = {}
                        for k, r in enumerate(order[eng]):
                            i = node_of[r["g"]]
                            wts = []
                            for p in dep_nodes[i]:
                                pr = allops[p]
                                if pr["dma"] is not None:
                                    oq = qord[pr["g"]]
                                    lane = oq % KPOOL
                                    val = 16 * (oq // KPOOL + 1)
                                    key = ("q", pr["dma"], lane)
                                    if water.get(key, -1) < val:
                                        wts.append((pools[pr["dma"]][lane],
                                                    val))
                                        water[key] = val
                                else:
                                    val = cum[pr["eng"]][pos[pr["g"]]]
                                    key = ("c", pr["eng"])
                                    if water.get(key, -1) < val:
                                        wts.append((sems[pr["eng"]], val))
                                        water[key] = val
                            for s_, v_ in wts:
                                engobj.wait_ge(s_, v_)
                            if r["hazard"]:
                                j = self.junk.get(eng)
                                for _ in range(2):
                                    if j is None:
                                        engobj.nop()
                                    elif eng == "scalar":
                                        engobj.copy(j, j)
                                    else:
                                        engobj.tensor_copy(j, j)
                            ins = r["fn"](engobj)
                            if r["inc"]:
                                if ins is None:
                                    ins = engobj.nop()
                                if r["dma"]:
                                    q = r["dma"]
                                    lane = qord[r["g"]] % KPOOL
                                    ins.then_inc(pools[q][lane], 16)
                                else:
                                    ins.then_inc(sems[eng], 1)
                        for q, qeng in self.QUEUES.items():
                            if qeng == eng and self.nq[q]:
                                for lane in range(min(KPOOL, self.nq[q])):
                                    n = (self.nq[q] - 1 - lane) // KPOOL + 1
                                    engobj.wait_ge(pools[q][lane], 16 * n)
                    return body

                block.tensor(section("tensor"))
                block.vector(section("vector"))
                block.scalar(section("scalar"))
                block.gpsimd(section("gpsimd"))
                block.sync(section("sync"))
            nc.all_engine_barrier()
            nc.gpsimd.dma_reset(rng)
            nc.gpsimd.sem_clear(rng)
            nc.all_engine_barrier()


def _host_consts():
    c = {}
    c["ident_f32"] = np.eye(128, dtype=np.float32)
    c["ident_bf"] = np.eye(128, dtype=np.float32)
    c["ones_col_bf"] = np.ones((128, 1), dtype=np.float32)
    c["ones_row_bf"] = np.ones((1, 512), dtype=np.float32)
    c["ones_1x128_f32"] = np.ones((1, 128), dtype=np.float32)
    m0 = np.zeros((128, 256), dtype=np.float32)
    m1 = np.zeros((128, 256), dtype=np.float32)
    s_idx = np.arange(128)[:, None]
    t_idx = np.arange(256)[None, :]
    m0[:, :] = (s_idx <= t_idx)
    m1[:, :] = (s_idx + 128 <= t_idx)
    c["gla_mask0"] = m0
    c["gla_mask1"] = m1
    q_i = np.arange(128)[:, None]
    k_j = np.arange(128)[None, :]
    c["mask_causal"] = np.where(k_j <= q_i, 0.0, NEG).astype(np.float32)
    c["mask_edge"] = np.where(k_j >= q_i, 0.0, NEG).astype(np.float32)
    half = np.arange(0, DIM_HEAD, 2, dtype=np.float64)
    inv_freq = 1.0 / 10000.0 ** (half / DIM_HEAD)
    t = np.arange(T, dtype=np.float64)
    freqs = t[:, None] * inv_freq[None, :]
    freqs = np.concatenate([freqs, freqs], axis=-1)
    base = (half + 0.4 * DIM_HEAD) / (1.4 * DIM_HEAD)
    power = (t - T // 2) / float(WINDOW // 2)
    scale = base[None, :] ** power[:, None]
    scale = np.concatenate([scale, scale], axis=-1)
    cos, sin = np.cos(freqs), np.sin(freqs)
    c["tab_qc"] = (8.0 * cos * scale).astype(np.float32)
    c["tab_qs"] = (8.0 * sin * scale).astype(np.float32)
    c["tab_kc"] = (cos / scale).astype(np.float32)
    c["tab_ks"] = (sin / scale).astype(np.float32)
    fc = np.zeros((128, 8), dtype=np.float32)
    fc[:, 0] = 1e-5
    fc[:, 1] = 1.0
    fc[:, 2] = 0.5
    fc[:, 3] = 0.01
    c["fconst"] = fc
    return c


_CONSTS = _host_consts()


def _pp_layout():
    cols = {}
    cur = 0
    for name, n in [("ln1_w", 8), ("ln1_b", 8), ("ln2_w", 8), ("ln2_b", 8),
                    ("bgk2", 8), ("b1a", 24), ("b2a", 8),
                    ("flng0", 8), ("flnb0", 8), ("flng1", 8), ("flnb1", 8),
                    ("fb2_0", 8), ("fb2_1", 8),
                    ("fb1a0", 22), ("fb1g0", 22), ("fb1a1", 22), ("fb1g1", 22),
                    ("fb1gs0", 22), ("fb1gs1", 22)]:
        cols[name] = cur
        cur += n
    assert cur <= 256
    return cols


_PPCOLS = _pp_layout()


def _pack_pp(ins):
    pp = np.zeros((128, 256), dtype=np.float32)

    def put(name, vec):
        c = _PPCOLS[name]
        v = np.asarray(vec, dtype=np.float32).reshape(-1)
        nt = (len(v) + 127) // 128
        for t in range(nt):
            seg = v[t * 128:(t + 1) * 128]
            pp[0:len(seg), c + t] = seg
    put("ln1_w", ins["L0_attn_ln1_w"]); put("ln1_b", ins["L0_attn_ln1_b"])
    put("ln2_w", ins["L0_attn_ln2_w"]); put("ln2_b", ins["L0_attn_ln2_b"])
    put("bgk2", ins["L0_attn_bgk2"])
    put("b1a", ins["L0_attn_b1"]); put("b2a", ins["L0_attn_b2"])
    for l in range(2):
        put(f"flng{l}", ins["L0_ffn_ln_g"][l]); put(f"flnb{l}", ins["L0_ffn_ln_b"][l])
        put(f"fb2_{l}", ins["L0_ffn_b2"][l])
        put(f"fb1a{l}", ins["L0_ffn_b1"][l][:HID])
        put(f"fb1g{l}", ins["L0_ffn_b1"][l][HID:])
    return pp


def _pack_rows(ins):
    r = np.zeros((1, 512), dtype=np.float32)
    r[0, 0:256] = np.asarray(ins["L0_attn_rms_w"], np.float32)
    r[0, 256:320] = np.asarray(ins["L0_ffn_q_scale"][0], np.float32)
    r[0, 320:384] = np.asarray(ins["L0_ffn_k_scale"][0], np.float32)
    r[0, 384:448] = np.asarray(ins["L0_ffn_q_scale"][1], np.float32)
    r[0, 448:512] = np.asarray(ins["L0_ffn_k_scale"][1], np.float32)
    return r


def build_program():
    nc = bass.Bass("TRN2", target_bir_lowering=False)
    em = Em(nc)
    es = ExitStack()
    NCH = _DBG["n_chunks"]

    def din(name, shape, dt=F32):
        return nc.dram_tensor(name, list(shape), dt, kind="ExternalInput").ap()

    x_h = din("x", (SEQ_PER_CORE, T, D))
    WH = {}
    for nm, shp in [
        ("ln1_w", (D,)), ("ln1_b", (D,)), ("Wq", (D, D)), ("Wk", (D, D)),
        ("Wv", (D, D)), ("Wg", (D, D)), ("Wgk1", (D, 16)), ("Wgk2", (16, D)),
        ("bgk2", (D,)), ("rms_w", (DV,)), ("Wo", (D, D)), ("ln2_w", (D,)),
        ("ln2_b", (D,)), ("W1", (D, MLP_H)), ("b1", (MLP_H,)),
        ("W2", (MLP_H, D)), ("b2", (D,)),
        ("fwqkv", (2, D, 3 * INNER)), ("fqs", (2, DIM_HEAD)), ("fks", (2, DIM_HEAD)),
        ("fwo", (2, INNER, D)), ("flng", (2, D)), ("flnb", (2, D)),
        ("fw1", (2, D, 2 * HID)), ("fb1", (2, 2 * HID)), ("fw2", (2, HID, D)),
        ("fb2", (2, D)),
    ]:
        WH[nm] = din(nm, shp)
    CH = {k: din("c_" + k, v.shape) for k, v in _CONSTS.items()}
    PPH = din("ppk", (128, 256))
    ROWH = din("rowk", (1, 512))
    FW2P = din("fw2p", (2, 2816, 1024))
    WSC = nc.dram_tensor("wscratch", [128 * 400000], BF16,
                         kind="Internal").ap()
    wsc_reg = em.region("wscratch")
    out_h = nc.dram_tensor("out", [SEQ_PER_CORE, T, D], F32,
                           kind="ExternalOutput").ap()
    dbg_h = {nm: nc.dram_tensor("dbg_" + nm, list(shape), F32,
                                kind="ExternalOutput").ap()
             for nm, shape in _DBG["dumps"]}

    def S(name, p, f, dt=F32):
        t = es.enter_context(nc.sbuf_tensor(name, [p, f], dt))
        return (t, em.region(name))

    x_tm = S("x_tm", 128, 4 * 1024)
    xnxf = S("xnxf", 128, 8 * WBLK)
    x1 = S("x1", 128, 8 * 512)
    abuf = S("abuf", 128, 8 * 512, BF16)
    ybuf = S("ybuf", 128, 8 * 512, BF16)
    ofm = S("ofm", 128, 8 * 256, BF16)
    offm = S("offm", 128, 4 * 512, BF16)
    hst = [S(f"hst{s}", 128, 8 * 256, BF16) for s in range(2)]
    kca = [[S(f"kc{l}{s}", 128, 4 * 768, BF16) for s in range(2)] for l in range(2)]
    vca = [[S(f"vc{l}{s}", 128, 6 * 512, BF16) for s in range(2)] for l in range(2)]
    g_q = S("g_q", 128, 512); g_k = S("g_k", 128, 512)
    g_m = S("g_m", 128, 512); g_l = S("g_l", 128, 512); g_s = S("g_s", 128, 512)
    g_s2 = S("g_s2", 128, 512)
    g_qb = S("g_qb", 128, 512, BF16); g_qt = S("g_qt", 128, 512, BF16)
    g_kh = S("g_kh", 128, 512, BF16); g_kt = S("g_kt", 128, 512, BF16)
    g_gg = S("g_gg", 128, 512); g_sg = S("g_sg", 128, 512)
    g_v = S("g_v", 128, 512, BF16); g_at = S("g_at", 128, 512, BF16)
    g_ob = S("g_ob", 128, 512, BF16)
    big = S("big", 128, 2048)
    bigB = (big[0], em.region("bigB"))
    xntt = S("xntt", 128, 1024)
    xnrtt = S("xnrtt", 128, 1024, BF16)
    ftq = S("ftq", 128, 4 * 512, BF16)
    ftk = S("ftk", 128, 4 * 512, BF16)
    fqf = S("fqf", 128, 4 * 512, BF16)
    atn = S("atn", 128, 640, BF16)
    atn2 = S("atn2", 128, 640, BF16)
    rowr = S("rowr", 1, 512)
    rowm = S("rowm", 1, 512)
    stat = S("stat", 128, 64)
    sch = S("sch", 128, 32)
    carry = S("carry", 128, 64)
    ones_rf = S("ones_rf", 1, 128)
    pp = S("pp", 128, 256)
    rmswb = S("rmswb", 128, 256)
    tabs = S("tabs", 128, 512)
    qscb = S("qscb", 128, 256)
    identf = S("identf", 128, 128)
    identb = S("identb", 128, 128, BF16)
    onescb = S("onescb", 128, 1, BF16)
    onesrb = S("onesrb", 1, 512, BF16)
    glam = S("glam", 128, 512, BF16)
    ffnm = S("ffnm", 128, 256, BF16)
    u_bf = S("u_bf", 16, 512, BF16)
    fcv = S("fcv", 128, 8)
    junkt = S("junkt", 128, 4)
    em.junk["vector"] = junkt[0][:, 0:2]
    em.junk["scalar"] = junkt[0][:, 2:4]
    NSLOT = 4
    wsl = [S(f"wsl{i}", 128, 1024, BF16) for i in range(NSLOT)]
    pb = []
    for i in range(8):
        t = es.enter_context(nc.psum_tensor(f"pb{i}", [128, 512], F32))
        pb.append((t, em.region(f"pb{i}", excl=True)))

    def pbf(i):
        return pb[i][0][:, :].bitcast(BF16)

    def TT(eng, out, in0, in1, op, rd, wr, dt_note=""):
        em.op(eng, lambda e, o=out, a=in0, b=in1, p=op:
              e.tensor_tensor(out=o, in0=a, in1=b, op=p), rd, wr)

    def TS(eng, out, in0, s1, s2, op0, op1, rd, wr, sr=()):
        if op1 is None:
            em.op(eng, lambda e, o=out, a=in0, x=s1, p0=op0:
                  e.tensor_scalar(out=o, in0=a, scalar1=x, scalar2=None, op0=p0),
                  rd, wr, sr)
        else:
            em.op(eng, lambda e, o=out, a=in0, x=s1, y=s2, p0=op0, p1=op1:
                  e.tensor_scalar(out=o, in0=a, scalar1=x, scalar2=y,
                                  op0=p0, op1=p1), rd, wr, sr)

    def STT(eng, out, in0, sc, in1, op0, op1, rd, wr, sr=()):
        em.op(eng, lambda e, o=out, a=in0, x=sc, b=in1, p0=op0, p1=op1:
              e.scalar_tensor_tensor(out=o, in0=a, scalar=x, in1=b,
                                     op0=p0, op1=p1), rd, wr, sr)

    _ASET = {AF.Exp: "E", AF.Ln: "E", AF.Erf: "S", AF.Sigmoid: "S"}

    def ACT(out, in_, func, rd, wr, sr=(), accum=None, **kw):
        aset = _ASET.get(func)
        if accum is not None:
            em.op("scalar", lambda e, o=out, i=in_, f=func, k=dict(kw), a=accum:
                  e.activation(o, i, f, accum_out=a, **k), rd, wr, sr,
                  tail=True, actset=aset)
        else:
            em.op("scalar", lambda e, o=out, i=in_, f=func, k=dict(kw):
                  e.activation(o, i, f, **k), rd, wr, sr, actset=aset)

    def SCOPY(out, in_, rd, wr):
        em.op("scalar", lambda e, o=out, i=in_: e.copy(o, i), rd, wr)

    def VCOPY(out, in_, rd, wr):
        em.op("vector", lambda e, o=out, i=in_: e.tensor_copy(o, i), rd, wr)

    def RED(out, in_, op, rd, wr, **kw):
        if op == "sum":
            em.op("vector", lambda e, o=out, i=in_, k=dict(kw):
                  e.reduce_sum(o, i, axis=AX.X, **k), rd, wr, tail=True,
                  cost=650)
        else:
            em.op("vector", lambda e, o=out, i=in_, k=dict(kw):
                  e.reduce_max(o, i, axis=AX.X, **k), rd, wr, tail=True,
                  cost=650)

    def SCAN(out, d0, d1, init, op0, op1, rd, wr, sr=()):
        em.op("vector", lambda e, o=out, a=d0, b=d1, i=init, p0=op0, p1=op1:
              e.tensor_tensor_scan(out=o, data0=a, data1=b, initial=i,
                                   op0=p0, op1=p1), rd, wr, sr, tail=True)

    def MM(out, lhsT, rhs, start, stop, rd, wr):
        em.op("tensor", lambda e, o=out, l=lhsT, r=rhs, a=start, b=stop:
              e.matmul(o, l, r, start=a, stop=b), rd, wr, cost=100)

    def TR(out, in_, ident, rd, wr):
        em.op("tensor", lambda e, o=out, i=in_, d=ident:
              e.transpose(o, i, d), rd, wr, cost=80)

    def RECIP(out, in_, rd, wr):
        em.op("vector", lambda e, o=out, i=in_: e.reciprocal(o, i), rd, wr,
              tail=True)


    def DMA(queue, out, in_, rd, wr, slow=False, cost=3000):
        em.dma(queue, lambda e, o=out, i=in_, sl=slow:
               e.dma_start(out=o, in_=i, allow_slow_non_contiguous=sl), rd, wr,
               cost=cost)

    wstate = {"i": 0, "off": 0, "offs": [], "c": 0}

    def wchunk(c):
        wstate["c"] = c
        wstate["i"] = 0
        if c == 0:
            wstate["offs"] = []
        wstate["gi"] = 0

    def wload(in_ap, cols, parts=128):
        assert parts == 128
        sl = wsl[wstate["i"] % NSLOT]
        wstate["i"] += 1
        gi = wstate["gi"]
        if wstate["c"] == 0:
            off = wstate["off"]
            greg = em.region(f"wg{gi}")
            wstate["offs"].append((off, cols, greg))
            wstate["off"] += cols
            assert wstate["off"] <= 400000
            blk = WSC[off * 128:(off + cols) * 128].rearrange(
                "(p c) -> p c", p=128)
            DMA("qg", blk, in_ap, rd=(), wr=[greg])
        off, cols0, greg = wstate["offs"][gi]
        assert cols0 == cols
        blk = WSC[off * 128:(off + cols) * 128].rearrange("(p c) -> p c", p=128)
        DMA("qs", sl[0][:, 0:cols], blk,
            rd=[greg], wr=[sl[1]], cost=4 * cols)
        wstate["gi"] = gi + 1
        return sl

    def wtiles(arr2d, r0, nr, c0, ncols):
        return arr2d[r0:r0 + nr, c0:c0 + ncols]

    def wgroup_rows(arr2d, c0, ncols):
        return arr2d[:, c0:c0 + ncols].rearrange("(f p) c -> p f c", p=128)

    def fmc(fp, s, i):
        return fp * WBLK + 1 + s * 257 + i

    def fms(fp, s, i):
        return fp * WBLK + s * 257 + i

    def plc(fp, tt, i=0):
        return fp * 512 + tt * 128 + i

    XT, XR = xnxf
    A_T, A_R = abuf
    X1T, X1R = x1
    YT, YR = ybuf
    PBT = pb

    def xnxf_seqs(fp, shifted=False):
        off = fp * WBLK + (0 if shifted else 1)
        return XT[:, off:off + 514].rearrange("p (s i) -> p s i", s=2,
                                              i=257)[:, :, 0:256]

    def dump(name, ap, rd):
        if name in dbg_h:
            DMA("qg", dbg_h[name], ap, rd=list(rd), wr=())

    DMA("qa", identf[0][:], CH["ident_f32"], (), [identf[1]])
    DMA("qg", identb[0][:], CH["ident_bf"], (), [identb[1]])
    DMA("qg", onescb[0][:], CH["ones_col_bf"], (), [onescb[1]])
    DMA("qg", onesrb[0][0:1, :], CH["ones_row_bf"], (), [onesrb[1]])
    DMA("qg", glam[0][:, 0:256], CH["gla_mask0"], (), [glam[1]])
    DMA("qg", glam[0][:, 256:512], CH["gla_mask1"], (), [glam[1]])
    DMA("qg", ffnm[0][:, 0:128], CH["mask_causal"], (), [ffnm[1]])
    DMA("qg", ffnm[0][:, 128:256], CH["mask_edge"], (), [ffnm[1]])
    DMA("qa", ones_rf[0][:], CH["ones_1x128_f32"], (), [ones_rf[1]])
    em.stable.add(ones_rf[1])

    DMA("qa", fcv[0][:], CH["fconst"], (), [fcv[1]])
    for _i, _v in enumerate([1e-5, 1.0, 0.5, 0.01]):
        nc.const_aps.register(F32, _v, fcv[0][:, _i:_i + 1])
    em.stable.add(fcv[1])
    em.op("scalar", lambda e: e.copy(sch[0][0:1, 31:32], fcv[0][0:1, 0:1]),
          [fcv[1]], [sch[1]])
    em.op("vector", lambda e: e.tensor_copy(sch[0][0:1, 30:31],
                                            fcv[0][0:1, 0:1]),
          [fcv[1]], [sch[1]])

    PPC = dict(_PPCOLS)
    DMA("qa", pp[0][:], PPH, (), [pp[1]])
    crows = S("crows", 1, 512)
    DMA("qa", crows[0][:], ROWH, (), [crows[1]])
    em.stable.add(crows[1])
    for srcn, dstn in (("fb1g0", "fb1gs0"), ("fb1g1", "fb1gs1")):
        em.op("scalar", lambda e, a=PPC[srcn], b=PPC[dstn]:
              e.mul(pp[0][:, b:b + 22], pp[0][:, a:a + 22],
                    0.7071067811865476), [pp[1]], [pp[1]])
    wgk1t = S("wgk1t", 128, 128, BF16)
    wgk2t = S("wgk2t", 16, 1024, BF16)
    DMA("qg", wgk1t[0][:], wgroup_rows(WH["Wgk1"], 0, 16), (), [wgk1t[1]])
    DMA("qa", big[0][0:16, 0:1024], WH["Wgk2"][0:16, :], (), [big[1]])
    SCOPY(wgk2t[0][0:16, :], big[0][0:16, 0:1024], [big[1]], [wgk2t[1]])
    em.stable.add(wgk1t[1]); em.stable.add(wgk2t[1])
    MM(pb[6][0][0:128, 0:256], ones_rf[0][:], crows[0][0:1, 0:256], True, True,
       [ones_rf[1], crows[1]], [pb[6][1]])
    VCOPY(rmswb[0][:], pb[6][0][0:128, 0:256], [pb[6][1]], [rmswb[1]])
    for i in range(4):
        MM(pb[7][0][0:128, 0:64], ones_rf[0][:],
           crows[0][0:1, 256 + i * 64:256 + (i + 1) * 64], True, True,
           [ones_rf[1], crows[1]], [pb[7][1]])
        VCOPY(qscb[0][:, i * 64:(i + 1) * 64], pb[7][0][0:128, 0:64],
              [pb[7][1]], [qscb[1]])

    em.op("gpsimd", lambda e: e.memset(hst[0][0][:], 0.0), (), [hst[0][1]])
    em.op("gpsimd", lambda e: e.memset(hst[1][0][:], 0.0), (), [hst[1][1]])
    em.op("gpsimd", lambda e: e.memset(carry[0][:], 0.0), (), [carry[1]])
    em.op("gpsimd", lambda e: e.memset(ybuf[0][:], 0.0), (), [ybuf[1]])
    em.op("gpsimd", lambda e: e.memset(xnxf[0][:], 0.0), (), [xnxf[1]])

    for rname in [identf[1], identb[1], onescb[1], onesrb[1], glam[1], ffnm[1],
                  pp[1], rmswb[1], qscb[1]]:
        em.stable.add(rname)
    em.op("scalar", lambda e: e.nop(),
          [], [fcv[1], identf[1], pp[1], crows[1], wgk2t[1]])
    em.op("gpsimd", lambda e: e.nop(),
          [], [identb[1], onescb[1], onesrb[1], glam[1], ffnm[1],
               wgk1t[1]])

    ALU_ = ALU

    def st_load(c):
        for tt in range(4):
            s, hi = tt // 2, tt % 2
            DMA("qa", x_tm[0][:, tt * 1024:(tt + 1) * 1024],
                x_h[s, c * TC + hi * 128: c * TC + hi * 128 + 128, :],
                (), [x_tm[1]], cost=8000)

    def st_outer(c):
        xr, sr_, br = x_tm[1], stat[1], big[1]
        for tt in range(4):
            xs = x_tm[0][:, tt * 1024:(tt + 1) * 1024]
            br_t = br if tt % 2 == 0 else bigB[1]
            ACT(big[0][:, (tt % 2) * 1024:(tt % 2) * 1024 + 1024], xs,
                AF.Square, [xr], [br_t])
            RED(stat[0][:, tt:tt + 1],
                big[0][:, (tt % 2) * 1024:(tt % 2) * 1024 + 1024], "sum",
                [br_t], [sr_])
            RED(stat[0][:, 4 + tt:5 + tt], xs, "sum", [xr], [sr_])
        TS("vector", stat[0][:, 8:12], stat[0][:, 4:8], 1.0 / D, None,
           ALU_.mult, None, [sr_], [sr_])
        ACT(stat[0][:, 24:28], stat[0][:, 8:12], AF.Square, [sr_], [sr_])
        STT("vector", stat[0][:, 28:32], stat[0][:, 0:4], 1.0 / D,
            stat[0][:, 24:28], ALU_.mult, ALU_.subtract, [sr_], [sr_])
        ACT(stat[0][:, 32:36], stat[0][:, 28:32], AF.Ln, [sr_], [sr_], bias=1e-5)
        ACT(stat[0][:, 12:16], stat[0][:, 32:36], AF.Exp, [sr_], [sr_], scale=-0.5)
        TT("vector", stat[0][:, 16:20], stat[0][:, 8:12], stat[0][:, 12:16],
           ALU_.mult, [sr_], [sr_])
        ACT(stat[0][:, 36:40], stat[0][:, 12:16], AF.Square, [sr_], [sr_])
        TT("vector", stat[0][:, 40:44], stat[0][:, 28:32], stat[0][:, 36:40],
           ALU_.mult, [sr_], [sr_])
        ACT(stat[0][:, 44:48], stat[0][:, 40:44], AF.Ln, [sr_], [sr_], bias=1e-5)
        ACT(stat[0][:, 20:24], stat[0][:, 44:48], AF.Exp, [sr_], [sr_], scale=-0.5)
        dump("statearly", stat[0][:], [sr_])
        if _DBG.get("outer_stats_only"):
            dump("xtm", x_tm[0][:], [x_tm[1]])
            return
        for s in range(2):
            for hi in range(2):
                tt = 2 * s + hi
                xs = x_tm[0][:, tt * 1024:(tt + 1) * 1024]
                br_h = br if hi == 0 else bigB[1]
                TS("vector", big[0][:, hi * 1024:(hi + 1) * 1024], xs,
                   stat[0][:, 12 + tt:13 + tt], stat[0][:, 16 + tt:17 + tt],
                   ALU_.mult, ALU_.subtract, [xr], [br_h], sr=[sr_])
                TS("vector", xnrtt[0][:], big[0][:, hi * 1024:(hi + 1) * 1024],
                   stat[0][:, 20 + tt:21 + tt], None, ALU_.mult, None,
                   [br_h], [xnrtt[1]], sr=[sr_])
                for fp in range(8):
                    bi = 2 + fp % 2
                    b = pb[bi]
                    TR(pbf(bi)[0:128, 0:128], xnrtt[0][:, fp * 128:(fp + 1) * 128],
                       identb[0][:], [xnrtt[1], identb[1]], [b[1]])
                    ACT(abuf[0][:, fp * 512 + tt * 128: fp * 512 + tt * 128 + 128],
                        pbf(bi)[0:128, 0:128], AF.Identity, [b[1], pp[1]], [abuf[1]],
                        scale=pp[0][:, PPC["ln1_w"] + fp:PPC["ln1_w"] + fp + 1],
                        bias=pp[0][:, PPC["ln1_b"] + fp:PPC["ln1_b"] + fp + 1])
            for fp in range(8):
                b = pb[fp % 2]
                for hi in range(2):
                    TR(b[0][0:128, hi * 128:hi * 128 + 128],
                       big[0][:, hi * 1024 + fp * 128: hi * 1024 + fp * 128 + 128],
                       identf[0][:], [br if hi == 0 else bigB[1], identf[1]],
                       [b[1]])
                SCOPY(xnxf[0][:, fmc(fp, s, 0):fmc(fp, s, 0) + 256],
                      b[0][0:128, 0:256], [b[1]], [xnxf[1]])
        dump("xn", xnxf[0][:], [xnxf[1]])
        dump("stat", stat[0][:], [stat[1]])
        dump("xn1", abuf[0][:], [abuf[1]])

    def st_gla(c):
        ar, ppr = abuf[1], pp[1]
        CUTS = ["u", "gk", "proj", "gate", "gv", "at", "o", "otr", "state", "wo"]
        cut = _DBG.get("gla_cut", "wo")
        nsh = _DBG.get("gla_sh", 8)

        def upto(name):
            return CUTS.index(cut) >= CUTS.index(name)
        for fin in range(8):
            MM(pb[4][0][0:16, 0:512], wgk1t[0][:, fin * 16:(fin + 1) * 16],
               abuf[0][:, fin * 512:(fin + 1) * 512], fin == 0, fin == 7,
               [wgk1t[1], ar], [pb[4][1]])
        SCOPY(u_bf[0][0:16, :], pb[4][0][0:16, 0:512], [pb[4][1]], [u_bf[1]])
        if not upto("gk"):
            return
        for s in range(2):
            for h in range(H_GLA):
                if s * 4 + h >= nsh:
                    continue
                par = (s * 4 + h) % 2
                BZ0, BZ1, BP, BG0, BG1, BO, BT1, BT2 = (
                    (4, 5, 6, 2, 3, 7, 4, 5) if par == 0 else
                    (0, 1, 2, 0, 1, 3, 0, 1))
                g_lx, g_mx = g_l, g_m
                g_sx = g_s if par == 0 else g_s2
                for kk in range(2):
                    zb = pb[BZ0 + kk * (BZ1 - BZ0)]
                    MM(zb[0][0:128, 0:256],
                       wgk2t[0][0:16, h * 256 + kk * 128:h * 256 + kk * 128 + 128],
                       u_bf[0][0:16, s * 256:(s + 1) * 256], True, True,
                       [wgk2t[1], u_bf[1]], [zb[1]])
                    ksl = slice(kk * 256, (kk + 1) * 256)
                    bg = pp[0][:, PPC["bgk2"] + 2 * h + kk:PPC["bgk2"] + 2 * h + kk + 1]
                    ACT(g_lx[0][:, ksl], zb[0][0:128, 0:256], AF.Abs,
                        [zb[1]], [g_lx[1]], sr=[ppr], bias=bg, scale=1.0)
                    ACT(g_sx[0][:, ksl], g_lx[0][:, ksl], AF.Exp,
                        [g_lx[1]], [g_sx[1]], scale=-1.0)
                    ACT(g_lx[0][:, ksl], g_sx[0][:, ksl], AF.Ln,
                        [g_sx[1]], [g_lx[1]], bias=1.0)
                    TS("vector", g_mx[0][:, ksl], zb[0][0:128, 0:256], bg, 0.0,
                       ALU_.add, ALU_.min, [zb[1]], [g_mx[1]], sr=[ppr])
                    SCAN(g_sx[0][:, ksl], g_mx[0][:, ksl], g_lx[0][:, ksl], 0.0,
                         ALU_.add, ALU_.subtract, [g_mx[1], g_lx[1]], [g_sx[1]])
                for kk in range(2):
                    VCOPY(sch[0][:, kk:kk + 1],
                          g_sx[0][:, kk * 256 + 255:kk * 256 + 256],
                          [g_sx[1]], [sch[1]])
                ACT(sch[0][:, 2:4], sch[0][:, 0:2], AF.Exp, [sch[1]], [sch[1]],
                    scale=1.0 / 16.0)
                ACT(sch[0][:, 4:6], sch[0][:, 0:2], AF.Exp, [sch[1]], [sch[1]],
                    scale=-1.0 / 16.0)
                TT("vector", g_mx[0][:].rearrange("p (c i) -> p c i", c=2),
                   g_sx[0][:].rearrange("p (c i) -> p c i", c=2),
                   sch[0][:, 0:2].unsqueeze(2).broadcast_to([128, 2, 256]),
                   ALU_.subtract, [g_sx[1], sch[1]], [g_mx[1]])
                ACT(g_lx[0][:], g_sx[0][:], AF.Exp, [g_sx[1]], [g_lx[1]],
                    scale=1.0 / 16.0)
                ACT(g_sx[0][:], g_mx[0][:], AF.Exp, [g_mx[1]], [g_sx[1]],
                    scale=-1.0 / 16.0)
                if not upto("proj"):
                    continue
                for Wn, dst in (("Wq", g_q), ("Wk", g_k)):
                    for kk in range(2):
                        grp = wload(wgroup_rows(WH[Wn], h * 256 + kk * 128, 128),
                                    1024)
                        bk = pb[BP]
                        for fin in range(8):
                            MM(bk[0][0:128, 0:256],
                               grp[0][:, fin * 128:(fin + 1) * 128],
                               abuf[0][:, fin * 512 + s * 256:
                                        fin * 512 + s * 256 + 256],
                               fin == 0, fin == 7, [grp[1], ar], [bk[1]])
                        VCOPY(dst[0][:, kk * 256:(kk + 1) * 256],
                              bk[0][0:128, 0:256], [bk[1]], [dst[1]])
                if not upto("gate"):
                    continue
                STT("vector", g_qb[0][:], g_q[0][:], 1.0 / 16.0, g_lx[0][:],
                    ALU_.mult, ALU_.mult, [g_q[1], g_lx[1]], [g_qb[1]])
                TT("vector", g_qt[0][:].rearrange("p (c i) -> p c i", c=2),
                   g_qb[0][:].rearrange("p (c i) -> p c i", c=2),
                   sch[0][:, 4:6].unsqueeze(2).broadcast_to([128, 2, 256]),
                   ALU_.mult, [g_qb[1], sch[1]], [g_qt[1]])
                TT("vector", g_kh[0][:], g_k[0][:], g_sx[0][:], ALU_.mult,
                   [g_k[1], g_sx[1]], [g_kh[1]])
                if not upto("gv"):
                    continue
                for Wn, dst in (("Wg", g_gg), ("Wv", g_v)):
                    for half in range(2):
                        grp = wload(
                            WH[Wn][half * 512:(half + 1) * 512,
                                   h * 256:(h + 1) * 256].rearrange(
                                "(f p) c -> p f c", p=128), 1024)
                        for hi in range(2):
                            tt = 2 * s + hi
                            bgv = pb[BG0 if hi == 0 else BG1]
                            for f4 in range(4):
                                fin = half * 4 + f4
                                MM(bgv[0][0:128, 256:512],
                                   abuf[0][:, fin * 512 + tt * 128:
                                            fin * 512 + tt * 128 + 128],
                                   grp[0][:, f4 * 256:(f4 + 1) * 256],
                                   half == 0 and f4 == 0, half == 1 and f4 == 3,
                                   [grp[1], ar], [bgv[1]])
                    for hi in range(2):
                        bgv = pb[BG0 if hi == 0 else BG1]
                        if dst is g_v:
                            SCOPY(dst[0][:, hi * 256:(hi + 1) * 256],
                                  bgv[0][0:128, 256:512],
                                  [bgv[1]], [dst[1]])
                        else:
                            VCOPY(dst[0][:, hi * 256:(hi + 1) * 256],
                                  bgv[0][0:128, 256:512],
                                  [bgv[1]], [dst[1]])
                if not upto("at"):
                    continue
                for sh in range(2):
                    for kk in range(2):
                        MM(pb[BP][0][0:128, sh * 256:(sh + 1) * 256],
                           g_kh[0][:, kk * 256 + sh * 128:kk * 256 + sh * 128 + 128],
                           g_qt[0][:, kk * 256:(kk + 1) * 256],
                           kk == 0, kk == 1, [g_kh[1], g_qt[1]], [pb[BP][1]])
                    TT("vector", g_at[0][:, sh * 256:(sh + 1) * 256],
                       pb[BP][0][0:128, sh * 256:(sh + 1) * 256],
                       glam[0][:, sh * 256:(sh + 1) * 256],
                       ALU_.mult, [pb[BP][1], glam[1]], [g_at[1]])
                if not upto("o"):
                    continue
                for hi in range(2):
                    ob = pb[BO]
                    MM(ob[0][0:128, 0:256],
                       g_at[0][:, 0 * 256 + hi * 128:0 * 256 + hi * 128 + 128],
                       g_v[0][:, 0:256], True, False,
                       [g_at[1], g_v[1]], [ob[1]])
                    MM(ob[0][0:128, 0:256],
                       g_at[0][:, 1 * 256 + hi * 128:1 * 256 + hi * 128 + 128],
                       g_v[0][:, 256:512], False, False,
                       [g_at[1], g_v[1]], [ob[1]])
                    MM(ob[0][0:128, 0:256],
                       g_qb[0][:, 0 * 256 + hi * 128:0 * 256 + hi * 128 + 128],
                       hst[s][0][:, (2 * h) * 256:(2 * h) * 256 + 256],
                       False, False, [g_qb[1], hst[s][1]], [ob[1]])
                    MM(ob[0][0:128, 0:256],
                       g_qb[0][:, 1 * 256 + hi * 128:1 * 256 + hi * 128 + 128],
                       hst[s][0][:, (2 * h + 1) * 256:(2 * h + 1) * 256 + 256],
                       False, True, [g_qb[1], hst[s][1]], [ob[1]])
                    hsl = slice(hi * 256, (hi + 1) * 256)
                    ACT(big[0][:, hi * 256:(hi + 1) * 256], ob[0][0:128, 0:256],
                        AF.Square, [ob[1]], [big[1]])
                    RED(sch[0][:, 8 + hi:9 + hi],
                        big[0][:, hi * 256:(hi + 1) * 256], "sum",
                        [big[1]], [sch[1]])
                    ACT(sch[0][:, 10 + hi:11 + hi], sch[0][:, 8 + hi:9 + hi],
                        AF.Ln, [sch[1]], [sch[1]], scale=1.0 / 256.0, bias=1e-5)
                    ACT(sch[0][:, 12 + hi:13 + hi], sch[0][:, 10 + hi:11 + hi],
                        AF.Exp, [sch[1]], [sch[1]], scale=-0.5)
                    ACT(g_sg[0][:, hsl], g_gg[0][:, hsl], AF.Sigmoid,
                        [g_gg[1]], [g_sg[1]])
                    TT("vector", g_sg[0][:, hsl], g_gg[0][:, hsl],
                       g_sg[0][:, hsl], ALU_.mult, [g_gg[1], g_sg[1]], [g_sg[1]])
                    STT("vector", g_gg[0][:, hsl], ob[0][0:128, 0:256],
                        sch[0][:, 12 + hi:13 + hi], g_sg[0][:, hsl],
                        ALU_.mult, ALU_.mult, [ob[1], g_sg[1]], [g_gg[1]],
                        sr=[sch[1]])
                    TT("vector", g_ob[0][:, hsl], g_gg[0][:, hsl],
                       rmswb[0][:, 0:256], ALU_.mult, [g_gg[1], rmswb[1]],
                       [g_ob[1]])
                if not upto("otr"):
                    continue
                for vv in range(2):
                    for hi in range(2):
                        TR(pbf(BT1)[0:128, 512 + hi * 128:512 + hi * 128 + 128],
                           g_ob[0][:, hi * 256 + vv * 128:hi * 256 + vv * 128 + 128],
                           identb[0][:], [g_ob[1], identb[1]], [pb[BT1][1]])
                    SCOPY(ofm[0][:, (2 * h + vv) * 256:(2 * h + vv) * 256 + 256],
                          pbf(BT1)[0:128, 512:768], [pb[BT1][1]], [ofm[1]])
                if not upto("state"):
                    continue
                for sh in range(2):
                    for kk in range(2):
                        TR(pbf(BT2)[0:128, 512 + kk * 128:512 + kk * 128 + 128],
                           g_kh[0][:, kk * 256 + sh * 128:kk * 256 + sh * 128 + 128],
                           identb[0][:], [g_kh[1], identb[1]], [pb[BT2][1]])
                    SCOPY(g_kt[0][:, sh * 256:(sh + 1) * 256],
                          pbf(BT2)[0:128, 512:768], [pb[BT2][1]], [g_kt[1]])
                for kk in range(2):
                    for sh in range(2):
                        MM(pb[BP][0][0:128, kk * 256:(kk + 1) * 256],
                           g_kt[0][:, sh * 256 + kk * 128:sh * 256 + kk * 128 + 128],
                           g_v[0][:, sh * 256:(sh + 1) * 256],
                           sh == 0, sh == 1, [g_kt[1], g_v[1]], [pb[BP][1]])
                    TS("vector", g_sg[0][:, kk * 256:(kk + 1) * 256],
                       hst[s][0][:, (2 * h + kk) * 256:(2 * h + kk) * 256 + 256],
                       sch[0][:, 2 + kk:3 + kk], None, ALU_.mult, None,
                       [hst[s][1]], [g_sg[1]], sr=[sch[1]])
                    TT("vector",
                       hst[s][0][:, (2 * h + kk) * 256:(2 * h + kk) * 256 + 256],
                       g_sg[0][:, kk * 256:(kk + 1) * 256],
                       pb[BP][0][0:128, kk * 256:(kk + 1) * 256], ALU_.add,
                       [g_sg[1], pb[BP][1]], [hst[s][1]])
            if not upto("wo") or nsh < 8:
                continue
            for vin in range(8):
                grp = wload(WH["Wo"][vin * 128:(vin + 1) * 128, :], 1024)
                for fo in range(8):
                    MM(pb[fo][0][0:128, 0:256],
                       grp[0][:, fo * 128:(fo + 1) * 128],
                       ofm[0][:, vin * 256:(vin + 1) * 256],
                       vin == 0, vin == 7, [grp[1], ofm[1]], [pb[fo][1]])
            for fo in range(8):
                TT("vector", x1[0][:, fo * 512 + s * 256:fo * 512 + s * 256 + 256],
                   pb[fo][0][0:128, 0:256],
                   xnxf[0][:, fmc(fo, s, 0):fmc(fo, s, 0) + 256],
                   ALU_.add, [pb[fo][1], xnxf[1]], [x1[1]])
        dump("x1", x1[0][:], [x1[1]])

    def st_ln2_mlp(c):
        xr = x1[1]
        SCOPY(ybuf[0][:, 0:4096], x1[0][:, 0:4096], [xr], [ybuf[1]])
        for fp in range(8):
            MM(pb[4][0][0:1, 0:512], onescb[0][:],
               ybuf[0][:, fp * 512:(fp + 1) * 512], fp == 0, fp == 7,
               [onescb[1], ybuf[1]], [pb[4][1]])
        ACT(ybuf[0][:, 0:4096], x1[0][:, 0:4096], AF.Square, [xr], [ybuf[1]])
        for fp in range(8):
            MM(pb[5][0][0:1, 0:512], onescb[0][:],
               ybuf[0][:, fp * 512:(fp + 1) * 512], fp == 0, fp == 7,
               [onescb[1], ybuf[1]], [pb[5][1]])
        _ln_rows_and_bcast(pb[4], pb[5])
        for half in range(2):
            _ln_apply_fm(x1[0][:, half * 2048:(half + 1) * 2048].rearrange(
                "p (f s i) -> p f s i", f=4, s=2), half, "ln2_w", "ln2_b", [xr],
                pb[4], pb[5])
        for half in range(3):
            for j in range(8):
                jj = half * 8 + j
                grp = wload(wgroup_rows(WH["W1"], jj * 128, 128), 1024)
                ub = pb[4 + j % 4]
                for fin in range(8):
                    MM(ub[0][0:128, 0:512], grp[0][:, fin * 128:(fin + 1) * 128],
                       abuf[0][:, fin * 512:(fin + 1) * 512], fin == 0, fin == 7,
                       [grp[1], abuf[1]], [ub[1]])
                bcol = pp[0][:, PPC["b1a"] + jj:PPC["b1a"] + jj + 1]
                uslc = slice((j % 4) * 512, (j % 4) * 512 + 512)
                ureg = big[1] if j % 4 < 2 else bigB[1]
                ACT(big[0][:, uslc], ub[0][0:128, 0:512], AF.Prelu,
                    [ub[1]], [ureg], sr=[pp[1]], bias=bcol, alpha=0.01)
                ACT(ybuf[0][:, j * 512:(j + 1) * 512],
                    big[0][:, uslc], AF.Square, [ureg], [ybuf[1]])
            for j in range(8):
                jj = half * 8 + j
                grp = wload(WH["W2"][jj * 128:(jj + 1) * 128, :], 1024)
                for fo in range(8):
                    MM(pb[fo][0][0:128, 0:512], grp[0][:, fo * 128:(fo + 1) * 128],
                       ybuf[0][:, j * 512:(j + 1) * 512], j == 0, j == 7,
                       [grp[1], ybuf[1]], [pb[fo][1]])
            for fo in range(8):
                if half < 2:
                    TT("vector", x1[0][:, fo * 512:(fo + 1) * 512],
                       x1[0][:, fo * 512:(fo + 1) * 512], pb[fo][0][0:128, 0:512],
                       ALU_.add, [pb[fo][1], xr], [xr])
                else:
                    STT("vector", x1[0][:, fo * 512:(fo + 1) * 512],
                        pb[fo][0][0:128, 0:512],
                        pp[0][:, PPC["b2a"] + fo:PPC["b2a"] + fo + 1],
                        x1[0][:, fo * 512:(fo + 1) * 512], ALU_.add, ALU_.add,
                        [pb[fo][1], xr], [xr], sr=[pp[1]])
        dump("attnres", x1[0][:], [x1[1]])

    def _ln_rows_and_bcast(sum_pb, ssq_pb):
        TS("vector", rowm[0][0:1, :], sum_pb[0][0:1, 0:512], 1.0 / D, None,
           ALU_.mult, None, [sum_pb[1]], [rowm[1]])
        ACT(rowr[0][0:1, :], rowm[0][0:1, :], AF.Square, [rowm[1]], [rowr[1]])
        STT("vector", rowr[0][0:1, :], ssq_pb[0][0:1, 0:512], 1.0 / D,
            rowr[0][0:1, :], ALU_.mult, ALU_.subtract,
            [ssq_pb[1], rowr[1]], [rowr[1]])
        ACT(rowr[0][0:1, :], rowr[0][0:1, :], AF.Ln, [rowr[1]], [rowr[1]],
            bias=1e-5)
        ACT(rowr[0][0:1, :], rowr[0][0:1, :], AF.Exp, [rowr[1]], [rowr[1]],
            scale=-0.5)
        TT("vector", rowm[0][0:1, :], rowm[0][0:1, :], rowr[0][0:1, :],
           ALU_.mult, [rowm[1], rowr[1]], [rowm[1]])
        MM(sum_pb[0][0:128, 0:512], ones_rf[0][:], rowr[0][0:1, 0:512],
           True, True, [ones_rf[1], rowr[1]], [sum_pb[1]])
        MM(ssq_pb[0][0:128, 0:512], ones_rf[0][:], rowm[0][0:1, 0:512],
           True, True, [ones_rf[1], rowm[1]], [ssq_pb[1]])

    def _ln_apply_fm(in4_ap, half, wname, bname, in_rd, rpb, mpb):
        big4 = big[0][:, 0:2048].rearrange("p (f s i) -> p f s i", f=4, s=2)
        rb4 = rpb[0][0:128, 0:512].rearrange("p (s i) -> p s i", s=2).unsqueeze(
            1).broadcast_to([128, 4, 2, 256])
        mrb4 = mpb[0][0:128, 0:512].rearrange("p (s i) -> p s i", s=2).unsqueeze(
            1).broadcast_to([128, 4, 2, 256])
        TT("vector", big4, in4_ap, rb4, ALU_.mult,
           list(in_rd) + [rpb[1]], [big[1], bigB[1]])
        TT("vector", big4, big4, mrb4, ALU_.subtract,
           [big[1], mpb[1], bigB[1]], [big[1], bigB[1]])
        for f4 in range(4):
            fp = half * 4 + f4
            ACT(abuf[0][:, fp * 512:(fp + 1) * 512],
                big[0][:, f4 * 512:(f4 + 1) * 512], AF.Identity,
                [(big[1] if f4 < 2 else bigB[1]), pp[1]], [abuf[1]],
                scale=pp[0][:, PPC[wname] + fp:PPC[wname] + fp + 1],
                bias=pp[0][:, PPC[bname] + fp:PPC[bname] + fp + 1])

    XV3 = xnxf[0][:].rearrange("p (f x) -> p f x", f=8)

    def _shift_cast(c, site):
        cr, xr = carry[1], xnxf[1]
        for s in range(2):
            VCOPY(XV3[:, :, 257 * s], carry[0][:, site * 16 + s * 8:
                                               site * 16 + s * 8 + 8],
                  [cr], [xr])
        for s in range(2):
            VCOPY(carry[0][:, site * 16 + s * 8:site * 16 + s * 8 + 8],
                  XV3[:, :, 256 + 257 * s], [xr], [cr])
        for half in range(2):
            lo = 1 if half == 0 else 0
            src = XV3[:, half * 4:(half + 1) * 4, lo:lo + 514].rearrange(
                "p f (s i) -> p f s i", s=2, i=257)[:, :, :, 0:256]
            dst = abuf[0][:, half * 2048:(half + 1) * 2048].rearrange(
                "p (f s i) -> p f s i", f=4, s=2, i=256)
            SCOPY(dst, src, [xr], [abuf[1]])

    def st_ffn_layer(c, l):
        lr = "0" if l == 0 else "1"
        ar, xr = abuf[1], xnxf[1]
        _shift_cast(c, l * 2 + 0)
        for p3, dsts in ((0, "q"), (1, "k"), (2, "v")):
            for fin in range(8):
                grp = wload(WH["fwqkv"][l, fin * 128:(fin + 1) * 128,
                                        p3 * 512:(p3 + 1) * 512], 512)
                for tt in range(4):
                    MM(pb[tt][0][0:128, 0:512],
                       abuf[0][:, fin * 512 + tt * 128:fin * 512 + tt * 128 + 128],
                       grp[0][:, 0:512], fin == 0, fin == 7,
                       [grp[1], ar], [pb[tt][1]])
            for tt in range(4):
                s, hi = tt // 2, tt % 2
                if dsts == "q":
                    SCOPY(ftq[0][:, tt * 512:(tt + 1) * 512],
                          pb[tt][0][0:128, 0:512], [pb[tt][1]], [ftq[1]])
                elif dsts == "k":
                    SCOPY(ftk[0][:, tt * 512:(tt + 1) * 512],
                          pb[tt][0][0:128, 0:512], [pb[tt][1]], [ftk[1]])
                else:
                    rr = (c % 3) * 2 + hi
                    SCOPY(vca[l][s][0][:, rr * 512:(rr + 1) * 512],
                          pb[tt][0][0:128, 0:512], [pb[tt][1]], [vca[l][s][1]])
        for qi, nm in enumerate(["tab_qc", "tab_qs", "tab_kc", "tab_ks"]):
            for hi in range(2):
                DMA("qa", tabs[0][:, qi * 128 + hi * 64:qi * 128 + hi * 64 + 64],
                    CH[nm][c * TC + hi * 128: c * TC + hi * 128 + 128, :],
                    (), [tabs[1]])
        for ft, qk in ((ftq, 0), (ftk, 1)):
            fr = ft[1]
            ACT(big[0][:, 0:2048], ft[0][:], AF.Square, [fr], [big[1]])
            RED(stat[0][:, 32:64].rearrange("p (g o) -> p g o", o=1),
                big[0][:, 0:2048].rearrange("p (g d) -> p g d", d=64),
                "sum", [big[1]], [stat[1]])
            TS("vector", stat[0][:, 32:64], stat[0][:, 32:64], 1e-24, None,
               ALU_.max, None, [stat[1]], [stat[1]])
            ACT(stat[0][:, 0:32], stat[0][:, 32:64], AF.Ln, [stat[1]], [stat[1]])
            ACT(stat[0][:, 32:64], stat[0][:, 0:32], AF.Exp, [stat[1]], [stat[1]],
                scale=-0.5)
            f3 = ft[0][:].rearrange("p (g d) -> p g d", d=64)
            TT("vector", f3, f3,
               stat[0][:, 32:64].unsqueeze(2).broadcast_to([128, 32, 64]),
               ALU_.mult, [fr, stat[1]], [fr])
            qs_off = (l * 2 + qk) * 64
            TT("vector", f3, f3,
               qscb[0][:, qs_off:qs_off + 64].unsqueeze(1).broadcast_to(
                   [128, 32, 64]), ALU_.mult, [fr, qscb[1]], [fr])
            tb = tabs[0][:, qk * 256:qk * 256 + 128]
            ts_ = tabs[0][:, qk * 256 + 128:qk * 256 + 256]
            for s in range(2):
                half_ap = ft[0][:, s * 1024:(s + 1) * 1024]
                h4 = half_ap.rearrange("p (hi g d) -> p hi g d", hi=2, d=64)
                cosb = tb.rearrange("p (hi d) -> p hi d", hi=2).unsqueeze(
                    2).broadcast_to([128, 2, 8, 64])
                sinb = ts_.rearrange("p (hi d) -> p hi d", hi=2).unsqueeze(
                    2).broadcast_to([128, 2, 8, 64])
                big4a = big[0][:, 0:1024].rearrange(
                    "p (hi g d) -> p hi g d", hi=2, d=64)
                big4b = big[0][:, 1024:2048].rearrange(
                    "p (hi g d) -> p hi g d", hi=2, d=64)
                TT("vector", big4a, h4, cosb, ALU_.mult,
                   [fr, tabs[1]], [big[1]])
                h4h = half_ap.rearrange("p (b two hd) -> p b two hd",
                                        two=2, hd=32)
                b4h = big[0][:, 1024:2048].rearrange(
                    "p (b two hd) -> p b two hd", two=2, hd=32)
                em.op("scalar", lambda e, o=b4h[:, :, 0, :],
                      i=h4h[:, :, 1, :]: e.mul(o, i, -1.0), [fr], [bigB[1]])
                em.op("scalar", lambda e, o=b4h[:, :, 1, :],
                      i=h4h[:, :, 0, :]: e.copy(o, i), [fr], [bigB[1]])
                TT("vector", big4b, big4b, sinb, ALU_.mult,
                   [bigB[1], tabs[1]], [bigB[1]])
                TT("vector", half_ap, big[0][:, 0:1024], big[0][:, 1024:2048],
                   ALU_.add, [big[1], bigB[1]], [fr])
        for dd in range(4):
            bi = 4 + dd % 2
            for tt in range(4):
                TR(pbf(bi)[0:128, tt * 128:tt * 128 + 128],
                   ftq[0][:, tt * 512 + dd * 128:tt * 512 + dd * 128 + 128],
                   identb[0][:], [ftq[1], identb[1]], [pb[bi][1]])
            SCOPY(fqf[0][:, dd * 512:(dd + 1) * 512], pbf(bi)[0:128, 0:512],
                  [pb[bi][1]], [fqf[1]])
        rslot = (c % 3) * 256
        for dd in range(4):
            bi = 4 + dd % 2
            for tt in range(4):
                TR(pbf(bi)[0:128, tt * 128:tt * 128 + 128],
                   ftk[0][:, tt * 512 + dd * 128:tt * 512 + dd * 128 + 128],
                   identb[0][:], [ftk[1], identb[1]], [pb[bi][1]])
            for s in range(2):
                SCOPY(kca[l][s][0][:, dd * 768 + rslot:dd * 768 + rslot + 256],
                      pbf(bi)[0:128, s * 256:s * 256 + 256], [pb[bi][1]],
                      [kca[l][s][1]])
        if l == 0:
            dump("fqf0", fqf[0][:], [fqf[1]])
            dump("kc00", kca[0][0][0][:], [kca[0][0][1]])
        for s in range(2):
            for hh in range(FFN_HEADS):
                dd, ph = hh // 2, 64 * (hh % 2)
                parm = (s * 8 + hh) % 2
                bA, bB = (pb[6], pb[7]) if parm == 0 else (pb[4], pb[5])
                simo = 0 if parm == 0 else 1024
                simr = big[1] if parm == 0 else bigB[1]
                atnb = atn if parm == 0 else atn2
                sco = 0 if parm == 0 else 8
                for qt in range(2):
                    gq = 2 * c + qt
                    kts = [(g, ("causal" if g == gq else
                                ("edge" if g == gq - 4 else "full")))
                           for g in range(max(0, gq - 4), gq + 1)]
                    nk = len(kts)
                    for ki, (g, kind) in enumerate(kts):
                        bank = bA if ki < 4 else bB
                        col = (ki % 4) * 128
                        rcol = (g // 2 % 3) * 256 + (g % 2) * 128
                        MM(bank[0][0:128, col:col + 128],
                           fqf[0][ph:ph + 64,
                                  dd * 512 + s * 256 + qt * 128:
                                  dd * 512 + s * 256 + qt * 128 + 128],
                           kca[l][s][0][ph:ph + 64,
                                        dd * 768 + rcol:dd * 768 + rcol + 128],
                           True, True, [fqf[1], kca[l][s][1]], [bank[1]])
                        dst = big[0][:, simo + ki * 128:simo + ki * 128 + 128]
                        if kind == "full":
                            SCOPY(dst, bank[0][0:128, col:col + 128],
                                  [bank[1]], [simr])
                        else:
                            moff = 0 if kind == "causal" else 128
                            TT("vector", dst, bank[0][0:128, col:col + 128],
                               ffnm[0][:, moff:moff + 128], ALU_.add,
                               [bank[1], ffnm[1]], [simr])
                    RED(stat[0][:, sco:sco + 1],
                        big[0][:, simo:simo + nk * 128], "max",
                        [simr], [stat[1]])
                    TS("vector", stat[0][:, sco + 1:sco + 2],
                       stat[0][:, sco:sco + 1], -1.0, None,
                       ALU_.mult, None, [stat[1]], [stat[1]])
                    ACT(atnb[0][:, 0:nk * 128], big[0][:, simo:simo + nk * 128],
                        AF.Exp, [simr, stat[1]], [atnb[1]],
                        sr=[stat[1]], bias=stat[0][:, sco + 1:sco + 2])
                    RED(stat[0][:, sco + 2:sco + 3], atnb[0][:, 0:nk * 128],
                        "sum", [atnb[1]], [stat[1]])
                    RECIP(stat[0][:, sco + 3:sco + 4],
                          stat[0][:, sco + 2:sco + 3], [stat[1]], [stat[1]])
                    TS("vector", atnb[0][:, 0:nk * 128], atnb[0][:, 0:nk * 128],
                       stat[0][:, sco + 3:sco + 4], None, ALU_.mult, None,
                       [atnb[1]], [atnb[1]], sr=[stat[1]])
                    if (l, s, hh, qt) == _DBG.get("ffn_probe", (None,) * 4):
                        dump("simdmp", big[0][:], [big[1]])
                        dump("atnpre", atn[0][:], [atn[1]])
                        dump("statdmp", stat[0][:], [stat[1]])
                    bAn = 6 if parm == 0 else 4
                    for ki in range(nk):
                        TR(pbf(bAn)[0:128, ki * 128:(ki + 1) * 128],
                           atnb[0][:, ki * 128:ki * 128 + 128], identb[0][:],
                           [atnb[1], identb[1]], [bA[1]])
                    SCOPY(atnb[0][:, 0:nk * 128], pbf(bAn)[0:128, 0:nk * 128],
                          [bA[1]], [atnb[1]])
                    for ki, (g, kind) in enumerate(kts):
                        rr = (g // 2 % 3) * 2 + (g % 2)
                        MM(bB[0][0:64, 256 + qt * 128:256 + qt * 128 + 128],
                           vca[l][s][0][:, rr * 512 + hh * 64:
                                        rr * 512 + hh * 64 + 64],
                           atnb[0][:, ki * 128:ki * 128 + 128],
                           ki == 0, ki == nk - 1,
                           [vca[l][s][1], atnb[1]], [bB[1]])
                if True:
                    SCOPY(offm[0][ph:ph + 64,
                                  dd * 512 + s * 256:dd * 512 + s * 256 + 256],
                          bB[0][0:64, 256:512], [bB[1]], [offm[1]])
        dump("fq" + lr, ftq[0][:], [ftq[1]])
        dump("fk" + lr, ftk[0][:], [ftk[1]])
        dump("offm" + lr, offm[0][:], [offm[1]])
        for vin in range(4):
            grp = wload(WH["fwo"][l, vin * 128:(vin + 1) * 128, :], 1024)
            for fo in range(8):
                MM(pb[fo][0][0:128, 0:512], grp[0][:, fo * 128:(fo + 1) * 128],
                   offm[0][:, vin * 512:(vin + 1) * 512], vin == 0, vin == 3,
                   [grp[1], offm[1]], [pb[fo][1]])
        for fo in range(8):
            TT("vector", xnxf_seqs(fo),
               pb[fo][0][0:128, 0:512].rearrange("p (s i) -> p s i", s=2),
               xnxf_seqs(fo), ALU_.add, [pb[fo][1], xr], [xr])
        dump("mid" + lr, xnxf[0][:], [xr])
        _shift_cast(c, l * 2 + 1)
        for fp in range(8):
            MM(pb[4][0][0:1, 0:512], onescb[0][:],
               abuf[0][:, fp * 512:(fp + 1) * 512], fp == 0, fp == 7,
               [onescb[1], ar], [pb[4][1]])
        ACT(ybuf[0][:, 0:4096], abuf[0][:, 0:4096], AF.Square, [ar], [ybuf[1]])
        for fp in range(8):
            MM(pb[5][0][0:1, 0:512], onescb[0][:],
               ybuf[0][:, fp * 512:(fp + 1) * 512], fp == 0, fp == 7,
               [onescb[1], ybuf[1]], [pb[5][1]])
        _ln_rows_and_bcast(pb[4], pb[5])
        for half in range(2):
            lo = 1 if half == 0 else 0
            src4 = XV3[:, half * 4:(half + 1) * 4, lo:lo + 514].rearrange(
                "p f (s i) -> p f s i", s=2, i=257)[:, :, :, 0:256]
            _ln_apply_fm(src4, half, "flng" + lr, "flnb" + lr, [xr],
                         pb[4], pb[5])
        for half in range(3):
            npass = (8, 8, 6)[half]
            for j in range(npass):
                jj = half * 8 + j
                ncc = 128 if jj < 21 else 42
                grpA = wload(WH["fw1"][l][:, jj * 128:jj * 128 + ncc].rearrange(
                    "(f p) c -> p f c", p=128), 8 * ncc)
                grpG = wload(WH["fw1"][l][:, HID + jj * 128:
                                          HID + jj * 128 + ncc].rearrange(
                    "(f p) c -> p f c", p=128), 8 * ncc)
                bA, bG = (pb[6], pb[7]) if j % 2 == 0 else (pb[4], pb[5])
                for fin in range(8):
                    MM(bA[0][0:ncc, 0:512],
                       grpA[0][:, fin * ncc:(fin + 1) * ncc],
                       abuf[0][:, fin * 512:(fin + 1) * 512], fin == 0, fin == 7,
                       [grpA[1], ar], [bA[1]])
                for fin in range(8):
                    MM(bG[0][0:ncc, 0:512],
                       grpG[0][:, fin * ncc:(fin + 1) * ncc],
                       abuf[0][:, fin * 512:(fin + 1) * 512], fin == 0, fin == 7,
                       [grpG[1], ar], [bG[1]])
                bga = pp[0][0:ncc, PPC["fb1a" + lr] + jj:PPC["fb1a" + lr] + jj + 1]
                bgg = pp[0][0:ncc, PPC["fb1g" + lr] + jj:PPC["fb1g" + lr] + jj + 1]
                bgs = pp[0][0:ncc, PPC["fb1gs" + lr] + jj:
                            PPC["fb1gs" + lr] + jj + 1]
                eslc = slice(0, 512) if j % 2 == 0 else slice(1024, 1536)
                fslc = slice(512, 1024) if j % 2 == 0 else slice(1536, 2048)
                ACT(big[0][0:ncc, eslc], bG[0][0:ncc, 0:512], AF.Erf,
                    [bG[1]], [big[1]], sr=[pp[1]],
                    scale=0.7071067811865476, bias=bgs)
                ACT(big[0][0:ncc, fslc], big[0][0:ncc, eslc], AF.Identity,
                    [big[1]], [big[1]], scale=0.5, bias=0.5)
                STT("vector", big[0][0:ncc, eslc], bG[0][0:ncc, 0:512],
                    bgg, big[0][0:ncc, fslc], ALU_.add, ALU_.mult,
                    [bG[1], big[1]], [big[1]], sr=[pp[1]])
                STT("vector", ybuf[0][0:ncc, j * 512:(j + 1) * 512],
                    bA[0][0:ncc, 0:512], bga, big[0][0:ncc, eslc],
                    ALU_.add, ALU_.mult, [bA[1], big[1]], [ybuf[1]],
                    sr=[pp[1]])
            for j in range(npass):
                jj = half * 8 + j
                grp = wload(FW2P[l][jj * 128:(jj + 1) * 128, :], 1024)
                for fo in range(8):
                    MM(pb[fo][0][0:128, 0:512],
                       grp[0][:, fo * 128:(fo + 1) * 128],
                       ybuf[0][:, j * 512:(j + 1) * 512], j == 0, j == npass - 1,
                       [grp[1], ybuf[1]], [pb[fo][1]])
            for fo in range(8):
                if half < 2:
                    TT("vector", xnxf_seqs(fo),
                       pb[fo][0][0:128, 0:512].rearrange("p (s i) -> p s i", s=2),
                       xnxf_seqs(fo), ALU_.add, [pb[fo][1], xr], [xr])
                else:
                    STT("vector", xnxf_seqs(fo),
                        pb[fo][0][0:128, 0:512].rearrange("p (s i) -> p s i",
                                                          s=2),
                        pp[0][:, PPC["fb2_" + lr] + fo:PPC["fb2_" + lr] + fo + 1],
                        xnxf_seqs(fo), ALU_.add, ALU_.add,
                        [pb[fo][1], xr], [xr], sr=[pp[1]])
        dump("ffn" + lr, xnxf[0][:], [xr])

    def st_final(c):
        for tt in range(4):
            s, hi = tt // 2, tt % 2
            for half in range(2):
                bA, bB = pb[0 + half], pb[2 + half]
                for f4 in range(4):
                    fp = half * 4 + f4
                    TR(bA[0][0:128, f4 * 128:f4 * 128 + 128],
                       x1[0][:, fp * 512 + tt * 128:fp * 512 + tt * 128 + 128],
                       identf[0][:], [x1[1], identf[1]], [bA[1]])
                    TR(bB[0][0:128, f4 * 128:f4 * 128 + 128],
                       xnxf[0][:, fmc(fp, s, hi * 128):
                               fmc(fp, s, hi * 128) + 128],
                       identf[0][:], [xnxf[1], identf[1]], [bB[1]])
                dst = x_tm[0][:, tt * 1024 + half * 512:
                              tt * 1024 + half * 512 + 512]
                TT("vector", dst, dst, bA[0][0:128, 0:512], ALU_.add,
                   [bA[1], x_tm[1]], [x_tm[1]])
                TT("vector", dst, dst, bB[0][0:128, 0:512], ALU_.add,
                   [bB[1], x_tm[1]], [x_tm[1]])
        for tt in range(4):
            s, hi = tt // 2, tt % 2
            DMA("qa", out_h[s, c * TC + hi * 128:c * TC + hi * 128 + 128, :],
                x_tm[0][:, tt * 1024:(tt + 1) * 1024], [x_tm[1]], ())

    stages = _DBG.get("stages", ("load", "outer", "gla", "mlp", "ffn", "final"))
    if "loadtest" in stages:
        st_load(0)
        for tt in range(4):
            xs = x_tm[0][:, tt * 1024:(tt + 1) * 1024]
            if _DBG.get("lt2"):
                ACT(big[0][:, (tt % 2) * 1024:(tt % 2) * 1024 + 1024], xs,
                    AF.Square, [x_tm[1]], [big[1]])
                RED(stat[0][:, tt:tt + 1],
                    big[0][:, (tt % 2) * 1024:(tt % 2) * 1024 + 1024], "sum",
                    [big[1]], [stat[1]])
            RED(stat[0][:, 4 + tt:5 + tt], xs, "sum",
                [x_tm[1]], [stat[1]])
        dump("statearly", stat[0][:], [stat[1]])
        dump("xtm", x_tm[0][:], [x_tm[1]])
        with es:
            em.finalize()
        return nc
    for c in range(NCH):
        wchunk(c)
        if "load" in stages:
            st_load(c)
        if "outer" in stages:
            st_outer(c)
        if "gla" in stages:
            st_gla(c)
        if "mlp" in stages:
            st_ln2_mlp(c)
        if "ffn" in stages:
            st_ffn_layer(c, 0)
            st_ffn_layer(c, 1)
        if "final" in stages:
            st_final(c)

    with es:
        em.finalize()
    return nc


def kernel(**inputs):
    nmap = {
        "x": None,
        "ln1_w": "L0_attn_ln1_w", "ln1_b": "L0_attn_ln1_b",
        "Wq": "L0_attn_Wq", "Wk": "L0_attn_Wk", "Wv": "L0_attn_Wv",
        "Wg": "L0_attn_Wg", "Wgk1": "L0_attn_Wgk1", "Wgk2": "L0_attn_Wgk2",
        "bgk2": "L0_attn_bgk2", "rms_w": "L0_attn_rms_w", "Wo": "L0_attn_Wo",
        "ln2_w": "L0_attn_ln2_w", "ln2_b": "L0_attn_ln2_b",
        "W1": "L0_attn_W1", "b1": "L0_attn_b1", "W2": "L0_attn_W2",
        "b2": "L0_attn_b2",
        "fwqkv": "L0_ffn_wqkv", "fqs": "L0_ffn_q_scale", "fks": "L0_ffn_k_scale",
        "fwo": "L0_ffn_wo", "flng": "L0_ffn_ln_g", "flnb": "L0_ffn_ln_b",
        "fw1": "L0_ffn_w1", "fb1": "L0_ffn_b1", "fw2": "L0_ffn_w2",
        "fb2": "L0_ffn_b2",
    }
    x = np.ascontiguousarray(np.asarray(inputs["L0_attn_x"], dtype=np.float32))
    shared = {k: np.ascontiguousarray(np.asarray(inputs[v], dtype=np.float32))
              for k, v in nmap.items() if v is not None}
    for k, v in _CONSTS.items():
        shared["c_" + k] = v
    shared["ppk"] = _pack_pp(inputs)
    shared["rowk"] = _pack_rows(inputs)
    fw2 = np.asarray(inputs["L0_ffn_w2"], dtype=np.float32)
    shared["fw2p"] = np.concatenate(
        [fw2, np.zeros((2, 2816 - HID, 1024), np.float32)], axis=1)
    nc = build_program()
    in_maps = []
    for core in range(NCORES):
        m = dict(shared)
        m["x"] = x[core * SEQ_PER_CORE:(core + 1) * SEQ_PER_CORE]
        in_maps.append(m)
    res = run_bass_kernel_spmd(nc, in_maps, core_ids=list(range(NCORES)))
    out = np.concatenate([np.asarray(r["out"]) for r in res.results], axis=0)
    if _DBG["dumps"]:
        kernel._dbg_out = [{k: np.asarray(v) for k, v in r.items()}
                           for r in res.results]
    return out


# version 16 (final)
# speedup vs baseline: 1.0119x; 1.0119x over previous
import math
import sys as _sys
import numpy as np
import concourse.bass as bass
import concourse.mybir as mybir
from concourse.bass_utils import run_bass_kernel_spmd
from contextlib import ExitStack

F32 = mybir.dt.float32
BF16 = mybir.dt.bfloat16
AF = mybir.ActivationFunctionType
ALU = mybir.AluOpType
AX = mybir.AxisListType

B, T, D = 16, 2048, 1024
NCORES = 8
SEQ_PER_CORE = B // NCORES
TC = 256
NCHUNK = T // TC
R = SEQ_PER_CORE * TC
WBLK = 516
H_GLA, DK, DV = 4, 256, 256
FFN_HEADS, DIM_HEAD, WINDOW, INNER = 8, 64, 512, 512
HID = 2730
MLP_H = 3072
NEG = -1.0e30

_DBG = {"n_chunks": NCHUNK, "dumps": []}


class Em:
    CENG = ("tensor", "vector", "scalar", "gpsimd", "sync")
    QUEUES = {"qs": "sync", "qa": "scalar", "qg": "gpsimd"}

    def __init__(self, nc):
        self.nc = nc
        self.ops = {e: [] for e in self.CENG}
        self.nq = {q: 0 for q in self.QUEUES}
        self._g = 0
        self.regions = {}
        self._rid = 0
        self.stable = set()
        self.junk = {}

    def region(self, name=None, excl=False):
        name = name or f"r{self._rid}"
        self._rid += 1
        self.regions[name] = {"w": None, "r": [], "x": excl}
        return name

    def _need(self, dep, eng, deps):
        kind = dep[0]
        if kind == "c" and dep[1] == eng:
            return
        key = (kind, dep[1])
        deps[key] = max(deps.get(key, -1), dep[2])

    def op(self, eng, fn, reads=(), writes=(), scalar_reads=(), tag="",
           tail=False, cost=500, actset=None):
        deps = {}
        hazard = 0
        hsrc = []
        idx = len(self.ops[eng])
        for r in list(reads) + list(scalar_reads):
            st = self.regions[r]
            if st.get("x"):
                for rd in st["r"]:
                    if rd[0] != "c" or rd[1] != eng:
                        self._need(rd, eng, deps)
            if st["w"] is not None:
                self._need(st["w"], eng, deps)
                w = st["w"]
                if st.get("wt") and w[0] == "c" and w[1] == eng:
                    hsrc.append(("t", w[1], w[2]))
                    if w[2] >= idx - 2:
                        hazard = max(hazard, 1)
                if r in scalar_reads and r not in self.stable:
                    hsrc.append(("s", None, None))
                    if w[0] != "c" or w[1] != eng or w[2] >= idx - 2:
                        hazard = max(hazard, 1)
        for r in writes:
            st = self.regions[r]
            if st["w"] is not None:
                self._need(st["w"], eng, deps)
            for rd in st["r"]:
                self._need(rd, eng, deps)
        rec = {"fn": fn, "deps": deps, "inc": False, "dma": None,
               "hazard": hazard, "eng": eng, "idx": idx, "tail": tail,
               "cost": cost, "hsrc": hsrc, "g": self._g, "actset": actset,
               "rd": list(reads) + list(scalar_reads), "sr": list(scalar_reads),
               "wr": list(writes),
               "tag": tag or f"L{_sys._getframe(2).f_lineno}"}
        self._g += 1
        self.ops[eng].append(rec)
        me = ("c", eng, idx)
        for r in reads:
            self.regions[r]["r"].append(me)
        for r in scalar_reads:
            self.regions[r]["r"].append(me)
        for r in writes:
            self.regions[r] = {"w": me, "r": [], "wt": tail,
                               "x": self.regions[r].get("x", False)}
        return me

    def dma(self, queue, fn, reads=(), writes=(), tag="", cost=3000):
        eng = self.QUEUES[queue]
        deps = {}
        idx = len(self.ops[eng])
        for r in reads:
            st = self.regions[r]
            if st["w"] is not None:
                self._need(st["w"], eng, deps)
        for r in writes:
            st = self.regions[r]
            if st["w"] is not None:
                self._need(st["w"], eng, deps)
            for rd in st["r"]:
                self._need(rd, eng, deps)
        ordinal = self.nq[queue]
        self.nq[queue] += 1
        rec = {"fn": fn, "deps": deps, "inc": True, "dma": queue, "ord": ordinal,
               "hazard": 0, "eng": eng, "idx": idx, "tail": False,
               "cost": cost, "hsrc": [], "g": self._g,
               "rd": list(reads), "sr": [], "wr": list(writes),
               "tag": tag or f"L{_sys._getframe(2).f_lineno}"}
        self._g += 1
        self.ops[eng].append(rec)
        me = ("q", queue, ordinal)
        for r in reads:
            self.regions[r]["r"].append(me)
        for r in writes:
            self.regions[r] = {"w": me, "r": [], "wt": False,
                               "x": self.regions[r].get("x", False)}
        return me

    def barrier_all(self, region_names):
        pass


    def _reschedule(self):
        allops = sorted((r for ops in self.ops.values() for r in ops),
                        key=lambda r: r["g"])
        nid = {r["g"]: i for i, r in enumerate(allops)}
        edges = [set() for _ in allops]

        regs = {name: {"w": None, "r": [], "x": st.get("x", False),
                       "wt": False}
                for name, st in self.regions.items()}
        for i, r in enumerate(allops):
            for rg in r["rd"]:
                st = regs[rg]
                if st["x"]:
                    for (rdid, rdeng) in st["r"]:
                        if rdeng != r["eng"]:
                            edges[i].add(rdid)
                if st["w"] is not None:
                    edges[i].add(st["w"])
            for rg in r["wr"]:
                st = regs[rg]
                if st["w"] is not None:
                    edges[i].add(st["w"])
                for (rdid, _e) in st["r"]:
                    edges[i].add(rdid)
            edges[i].discard(i)
            for rg in r["rd"]:
                regs[rg]["r"].append((i, r["eng"]))
            for rg in r["wr"]:
                regs[rg] = {"w": i, "r": [], "x": regs[rg]["x"],
                            "wt": r["tail"]}

        ndep = [len(e) for e in edges]
        consumers = [[] for _ in allops]
        for i, e in enumerate(edges):
            for p in e:
                consumers[p].append(i)
        pend = {e: [] for e in self.CENG}
        import heapq
        wait_cnt = list(ndep)
        for i, r in enumerate(allops):
            if wait_cnt[i] == 0:
                heapq.heappush(pend[r["eng"]], (r["g"], i))
        etime = {e: 0.0 for e in self.CENG}
        qtime = {q: 0.0 for q in self.QUEUES}
        fin = [0.0] * len(allops)
        order = {e: [] for e in self.CENG}
        nsched = 0
        import itertools
        cur_set = None
        GWIN = 1200
        while nsched < len(allops):
            cands = [e for e in self.CENG if pend[e]]
            assert cands, "scheduler cycle"
            eng = min(cands, key=lambda e: etime[e])
            if eng == "scalar":
                tmp = []
                best = None
                while pend[eng]:
                    it = heapq.heappop(pend[eng])
                    tmp.append(it)
                    aset = allops[it[1]].get("actset")
                    if aset is None or aset == cur_set:
                        best = it
                        break
                if best is None or best[0] > tmp[0][0] + GWIN:
                    best = tmp[0]
                for it in tmp:
                    if it is not best:
                        heapq.heappush(pend[eng], it)
                g, i = best
            else:
                g, i = heapq.heappop(pend[eng])
            r = allops[i]
            start = max([etime[eng]] + [fin[p] for p in edges[i]])
            cost = r["cost"]
            aset = r.get("actset")
            if aset is not None and aset != cur_set and eng == "scalar":
                cur_set = aset
                cost += 1280.0
            if r["dma"] is not None:
                qs = max(start, qtime[r["dma"]])
                qtime[r["dma"]] = qs + cost
                fin[i] = qs + cost
                etime[eng] = start + 100.0
            else:
                fin[i] = start + cost
                etime[eng] = fin[i]
            order[eng].append(r)
            nsched += 1
            for c in consumers[i]:
                wait_cnt[c] -= 1
                if wait_cnt[c] == 0:
                    heapq.heappush(pend[allops[c]["eng"]],
                                   (allops[c]["g"], c))
        return allops, nid, edges, order

    def finalize(self):
        nc = self.nc
        allops, nid, edges, order = self._reschedule()
        pos = {}
        for e, lst in order.items():
            for k, r in enumerate(lst):
                pos[r["g"]] = k
        gid = {r["g"]: i for i, r in enumerate(allops)}
        qord = {}
        cnt = {q: 0 for q in self.QUEUES}
        for e in self.CENG:
            pass
        for e, lst in order.items():
            for r in lst:
                if r["dma"] is not None:
                    qord[r["g"]] = cnt[r["dma"]]
                    cnt[r["dma"]] += 1
        node_of = {r["g"]: i for i, r in enumerate(allops)}
        need_inc = set()
        dep_nodes = [[] for _ in allops]
        for i, e in enumerate(edges):
            r = allops[i]
            for p in e:
                pr = allops[p]
                if pr["eng"] != r["eng"] or pr["dma"] is not None:
                    dep_nodes[i].append(p)
                    if pr["dma"] is None:
                        need_inc.add(p)
        cum = {}
        for e, lst in order.items():
            c = 0
            cc = []
            for r in lst:
                if r["dma"] is None and node_of[r["g"]] in need_inc:
                    r["inc"] = True
                if r["inc"] and r["dma"] is None:
                    c += 1
                cc.append(c)
            cum[e] = cc
        for e, lst in order.items():
            lastw = {}
            for k, r in enumerate(lst):
                r["hazard"] = 0
                if any(x not in self.stable for x in r["sr"]):
                    r["hazard"] = 1
                for (kind, heng, hidx) in r["hsrc"]:
                    if kind != "t":
                        continue
                for p in dep_nodes[node_of[r["g"]]]:
                    pass
            posk = {r["g"]: k for k, r in enumerate(lst)}
            for k, r in enumerate(lst):
                i = node_of[r["g"]]
                for p in edges[i]:
                    pr = allops[p]
                    if (pr["eng"] == e and pr["dma"] is None and pr["tail"]
                            and k - posk[pr["g"]] <= 2):
                        r["hazard"] = max(r["hazard"], 1)

        es = ExitStack()
        KPOOL = 8
        sems = {}
        pools = {}
        for e in self.CENG:
            sems[e] = es.enter_context(nc.semaphore(f"sem_{e}"))
        for q in self.QUEUES:
            pools[q] = [es.enter_context(nc.semaphore(f"sem_{q}{i}"))
                        for i in range(KPOOL)]
        allsems = list(sems.values()) + [s for p in pools.values() for s in p]
        nums = sorted(h.num for h in allsems)
        rng = range(nums[0], nums[-1] + 1)

        with es:
            nc.all_engine_barrier()
            nc.gpsimd.dma_reset(rng)
            nc.gpsimd.sem_clear(rng)
            nc.all_engine_barrier()
            with nc.Block() as block:
                def section(eng):
                    def body(engobj):
                        water = {}
                        for k, r in enumerate(order[eng]):
                            i = node_of[r["g"]]
                            wts = []
                            for p in dep_nodes[i]:
                                pr = allops[p]
                                if pr["dma"] is not None:
                                    oq = qord[pr["g"]]
                                    lane = oq % KPOOL
                                    val = 16 * (oq // KPOOL + 1)
                                    key = ("q", pr["dma"], lane)
                                    if water.get(key, -1) < val:
                                        wts.append((pools[pr["dma"]][lane],
                                                    val))
                                        water[key] = val
                                else:
                                    val = cum[pr["eng"]][pos[pr["g"]]]
                                    key = ("c", pr["eng"])
                                    if water.get(key, -1) < val:
                                        wts.append((sems[pr["eng"]], val))
                                        water[key] = val
                            for s_, v_ in wts:
                                engobj.wait_ge(s_, v_)
                            if r["hazard"]:
                                j = self.junk.get(eng)
                                for _ in range(2):
                                    if j is None:
                                        engobj.nop()
                                    elif eng == "scalar":
                                        engobj.copy(j, j)
                                    else:
                                        engobj.tensor_copy(j, j)
                            ins = r["fn"](engobj)
                            if r["inc"]:
                                if ins is None:
                                    ins = engobj.nop()
                                if r["dma"]:
                                    q = r["dma"]
                                    lane = qord[r["g"]] % KPOOL
                                    ins.then_inc(pools[q][lane], 16)
                                else:
                                    ins.then_inc(sems[eng], 1)
                        for q, qeng in self.QUEUES.items():
                            if qeng == eng and self.nq[q]:
                                for lane in range(min(KPOOL, self.nq[q])):
                                    n = (self.nq[q] - 1 - lane) // KPOOL + 1
                                    engobj.wait_ge(pools[q][lane], 16 * n)
                    return body

                block.tensor(section("tensor"))
                block.vector(section("vector"))
                block.scalar(section("scalar"))
                block.gpsimd(section("gpsimd"))
                block.sync(section("sync"))
            nc.all_engine_barrier()
            nc.gpsimd.dma_reset(rng)
            nc.gpsimd.sem_clear(rng)
            nc.all_engine_barrier()


def _host_consts():
    c = {}
    c["ident_f32"] = np.eye(128, dtype=np.float32)
    c["ident_bf"] = np.eye(128, dtype=np.float32)
    c["ones_col_bf"] = np.ones((128, 1), dtype=np.float32)
    c["ones_row_bf"] = np.ones((1, 512), dtype=np.float32)
    c["ones_1x128_f32"] = np.ones((1, 128), dtype=np.float32)
    m0 = np.zeros((128, 256), dtype=np.float32)
    m1 = np.zeros((128, 256), dtype=np.float32)
    s_idx = np.arange(128)[:, None]
    t_idx = np.arange(256)[None, :]
    m0[:, :] = (s_idx <= t_idx)
    m1[:, :] = (s_idx + 128 <= t_idx)
    c["gla_mask0"] = m0
    c["gla_mask1"] = m1
    q_i = np.arange(128)[:, None]
    k_j = np.arange(128)[None, :]
    c["mask_causal"] = np.where(k_j <= q_i, 0.0, NEG).astype(np.float32)
    c["mask_edge"] = np.where(k_j >= q_i, 0.0, NEG).astype(np.float32)
    half = np.arange(0, DIM_HEAD, 2, dtype=np.float64)
    inv_freq = 1.0 / 10000.0 ** (half / DIM_HEAD)
    t = np.arange(T, dtype=np.float64)
    freqs = t[:, None] * inv_freq[None, :]
    freqs = np.concatenate([freqs, freqs], axis=-1)
    base = (half + 0.4 * DIM_HEAD) / (1.4 * DIM_HEAD)
    power = (t - T // 2) / float(WINDOW // 2)
    scale = base[None, :] ** power[:, None]
    scale = np.concatenate([scale, scale], axis=-1)
    cos, sin = np.cos(freqs), np.sin(freqs)
    c["tab_qc"] = (8.0 * cos * scale).astype(np.float32)
    c["tab_qs"] = (8.0 * sin * scale).astype(np.float32)
    c["tab_kc"] = (cos / scale).astype(np.float32)
    c["tab_ks"] = (sin / scale).astype(np.float32)
    fc = np.zeros((128, 8), dtype=np.float32)
    fc[:, 0] = 1e-5
    fc[:, 1] = 1.0
    fc[:, 2] = 0.5
    fc[:, 3] = 0.01
    c["fconst"] = fc
    return c


_CONSTS = _host_consts()


def _pp_layout():
    cols = {}
    cur = 0
    for name, n in [("ln1_w", 8), ("ln1_b", 8), ("ln2_w", 8), ("ln2_b", 8),
                    ("bgk2", 8), ("b1a", 24), ("b2a", 8),
                    ("flng0", 8), ("flnb0", 8), ("flng1", 8), ("flnb1", 8),
                    ("fb2_0", 8), ("fb2_1", 8),
                    ("fb1a0", 22), ("fb1g0", 22), ("fb1a1", 22), ("fb1g1", 22),
                    ("fb1gs0", 22), ("fb1gs1", 22)]:
        cols[name] = cur
        cur += n
    assert cur <= 256
    return cols


_PPCOLS = _pp_layout()


def _pack_pp(ins):
    pp = np.zeros((128, 256), dtype=np.float32)

    def put(name, vec):
        c = _PPCOLS[name]
        v = np.asarray(vec, dtype=np.float32).reshape(-1)
        nt = (len(v) + 127) // 128
        for t in range(nt):
            seg = v[t * 128:(t + 1) * 128]
            pp[0:len(seg), c + t] = seg
    put("ln1_w", ins["L0_attn_ln1_w"]); put("ln1_b", ins["L0_attn_ln1_b"])
    put("ln2_w", ins["L0_attn_ln2_w"]); put("ln2_b", ins["L0_attn_ln2_b"])
    put("bgk2", ins["L0_attn_bgk2"])
    put("b1a", ins["L0_attn_b1"]); put("b2a", ins["L0_attn_b2"])
    for l in range(2):
        put(f"flng{l}", ins["L0_ffn_ln_g"][l]); put(f"flnb{l}", ins["L0_ffn_ln_b"][l])
        put(f"fb2_{l}", ins["L0_ffn_b2"][l])
        put(f"fb1a{l}", ins["L0_ffn_b1"][l][:HID])
        put(f"fb1g{l}", ins["L0_ffn_b1"][l][HID:])
    return pp


def _pack_rows(ins):
    r = np.zeros((1, 512), dtype=np.float32)
    r[0, 0:256] = np.asarray(ins["L0_attn_rms_w"], np.float32)
    r[0, 256:320] = np.asarray(ins["L0_ffn_q_scale"][0], np.float32)
    r[0, 320:384] = np.asarray(ins["L0_ffn_k_scale"][0], np.float32)
    r[0, 384:448] = np.asarray(ins["L0_ffn_q_scale"][1], np.float32)
    r[0, 448:512] = np.asarray(ins["L0_ffn_k_scale"][1], np.float32)
    return r


def build_program():
    nc = bass.Bass("TRN2", target_bir_lowering=False)
    em = Em(nc)
    es = ExitStack()
    NCH = _DBG["n_chunks"]

    def din(name, shape, dt=F32):
        return nc.dram_tensor(name, list(shape), dt, kind="ExternalInput").ap()

    x_h = din("x", (SEQ_PER_CORE, T, D))
    WH = {}
    for nm, shp in [
        ("ln1_w", (D,)), ("ln1_b", (D,)), ("Wq", (D, D)), ("Wk", (D, D)),
        ("Wv", (D, D)), ("Wg", (D, D)), ("Wgk1", (D, 16)), ("Wgk2", (16, D)),
        ("bgk2", (D,)), ("rms_w", (DV,)), ("Wo", (D, D)), ("ln2_w", (D,)),
        ("ln2_b", (D,)), ("W1", (D, MLP_H)), ("b1", (MLP_H,)),
        ("W2", (MLP_H, D)), ("b2", (D,)),
        ("fwqkv", (2, D, 3 * INNER)), ("fqs", (2, DIM_HEAD)), ("fks", (2, DIM_HEAD)),
        ("fwo", (2, INNER, D)), ("flng", (2, D)), ("flnb", (2, D)),
        ("fw1", (2, D, 2 * HID)), ("fb1", (2, 2 * HID)), ("fw2", (2, HID, D)),
        ("fb2", (2, D)),
    ]:
        WH[nm] = din(nm, shp)
    CH = {k: din("c_" + k, v.shape) for k, v in _CONSTS.items()}
    PPH = din("ppk", (128, 256))
    ROWH = din("rowk", (1, 512))
    FW2P = din("fw2p", (2, 2816, 1024))
    WSC = nc.dram_tensor("wscratch", [128 * 400000], BF16,
                         kind="Internal").ap()
    wsc_reg = em.region("wscratch")
    out_h = nc.dram_tensor("out", [SEQ_PER_CORE, T, D], F32,
                           kind="ExternalOutput").ap()
    dbg_h = {nm: nc.dram_tensor("dbg_" + nm, list(shape), F32,
                                kind="ExternalOutput").ap()
             for nm, shape in _DBG["dumps"]}

    def S(name, p, f, dt=F32):
        t = es.enter_context(nc.sbuf_tensor(name, [p, f], dt))
        return (t, em.region(name))

    x_tm = S("x_tm", 128, 4 * 1024)
    xnxf = S("xnxf", 128, 8 * WBLK)
    x1 = S("x1", 128, 8 * 512)
    abuf = S("abuf", 128, 8 * 512, BF16)
    ybuf = S("ybuf", 128, 8 * 512, BF16)
    ofm = S("ofm", 128, 8 * 256, BF16)
    offm = S("offm", 128, 4 * 512, BF16)
    hst = [S(f"hst{s}", 128, 8 * 256, BF16) for s in range(2)]
    kca = [[S(f"kc{l}{s}", 128, 4 * 768, BF16) for s in range(2)] for l in range(2)]
    vca = [[S(f"vc{l}{s}", 128, 6 * 512, BF16) for s in range(2)] for l in range(2)]
    g_q = S("g_q", 128, 512); g_k = S("g_k", 128, 512)
    g_m = S("g_m", 128, 512); g_l = S("g_l", 128, 512); g_s = S("g_s", 128, 512)
    g_s2 = S("g_s2", 128, 512)
    g_qb = S("g_qb", 128, 512, BF16); g_qt = S("g_qt", 128, 512, BF16)
    g_kh = S("g_kh", 128, 512, BF16); g_kt = S("g_kt", 128, 512, BF16)
    g_gg = S("g_gg", 128, 512); g_sg = S("g_sg", 128, 512)
    g_v = S("g_v", 128, 512, BF16); g_at = S("g_at", 128, 512, BF16)
    g_ob = S("g_ob", 128, 512, BF16)
    big = S("big", 128, 2048)
    bigB = (big[0], em.region("bigB"))
    xntt = S("xntt", 128, 1024)
    xnrtt = S("xnrtt", 128, 1024, BF16)
    ftq = S("ftq", 128, 4 * 512, BF16)
    ftk = S("ftk", 128, 4 * 512, BF16)
    fqf = S("fqf", 128, 4 * 512, BF16)
    atn = S("atn", 128, 640, BF16)
    atn2 = S("atn2", 128, 640, BF16)
    rowr = S("rowr", 1, 512)
    rowm = S("rowm", 1, 512)
    stat = S("stat", 128, 64)
    sch = S("sch", 128, 32)
    carry = S("carry", 128, 64)
    ones_rf = S("ones_rf", 1, 128)
    pp = S("pp", 128, 256)
    rmswb = S("rmswb", 128, 256)
    tabs = S("tabs", 128, 512)
    qscb = S("qscb", 128, 256)
    identf = S("identf", 128, 128)
    identb = S("identb", 128, 128, BF16)
    onescb = S("onescb", 128, 1, BF16)
    onesrb = S("onesrb", 1, 512, BF16)
    glam = S("glam", 128, 512, BF16)
    ffnm = S("ffnm", 128, 256, BF16)
    u_bf = S("u_bf", 16, 512, BF16)
    fcv = S("fcv", 128, 8)
    junkt = S("junkt", 128, 4)
    em.junk["vector"] = junkt[0][:, 0:2]
    em.junk["scalar"] = junkt[0][:, 2:4]
    NSLOT = 4
    wsl = [S(f"wsl{i}", 128, 1024, BF16) for i in range(NSLOT)]
    pb = []
    for i in range(8):
        t = es.enter_context(nc.psum_tensor(f"pb{i}", [128, 512], F32))
        pb.append((t, em.region(f"pb{i}", excl=True)))

    def pbf(i):
        return pb[i][0][:, :].bitcast(BF16)

    def TT(eng, out, in0, in1, op, rd, wr, dt_note=""):
        em.op(eng, lambda e, o=out, a=in0, b=in1, p=op:
              e.tensor_tensor(out=o, in0=a, in1=b, op=p), rd, wr)

    def TS(eng, out, in0, s1, s2, op0, op1, rd, wr, sr=()):
        if op1 is None:
            em.op(eng, lambda e, o=out, a=in0, x=s1, p0=op0:
                  e.tensor_scalar(out=o, in0=a, scalar1=x, scalar2=None, op0=p0),
                  rd, wr, sr)
        else:
            em.op(eng, lambda e, o=out, a=in0, x=s1, y=s2, p0=op0, p1=op1:
                  e.tensor_scalar(out=o, in0=a, scalar1=x, scalar2=y,
                                  op0=p0, op1=p1), rd, wr, sr)

    def STT(eng, out, in0, sc, in1, op0, op1, rd, wr, sr=()):
        em.op(eng, lambda e, o=out, a=in0, x=sc, b=in1, p0=op0, p1=op1:
              e.scalar_tensor_tensor(out=o, in0=a, scalar=x, in1=b,
                                     op0=p0, op1=p1), rd, wr, sr)

    _ASET = {AF.Exp: "E", AF.Ln: "E", AF.Erf: "S", AF.Sigmoid: "S"}

    def ACT(out, in_, func, rd, wr, sr=(), accum=None, **kw):
        aset = _ASET.get(func)
        if accum is not None:
            em.op("scalar", lambda e, o=out, i=in_, f=func, k=dict(kw), a=accum:
                  e.activation(o, i, f, accum_out=a, **k), rd, wr, sr,
                  tail=True, actset=aset)
        else:
            em.op("scalar", lambda e, o=out, i=in_, f=func, k=dict(kw):
                  e.activation(o, i, f, **k), rd, wr, sr, actset=aset)

    def SCOPY(out, in_, rd, wr):
        em.op("scalar", lambda e, o=out, i=in_: e.copy(o, i), rd, wr)

    def VCOPY(out, in_, rd, wr):
        em.op("vector", lambda e, o=out, i=in_: e.tensor_copy(o, i), rd, wr,
              cost=150)

    def RED(out, in_, op, rd, wr, **kw):
        if op == "sum":
            em.op("vector", lambda e, o=out, i=in_, k=dict(kw):
                  e.reduce_sum(o, i, axis=AX.X, **k), rd, wr, tail=True,
                  cost=650)
        else:
            em.op("vector", lambda e, o=out, i=in_, k=dict(kw):
                  e.reduce_max(o, i, axis=AX.X, **k), rd, wr, tail=True,
                  cost=650)

    def SCAN(out, d0, d1, init, op0, op1, rd, wr, sr=()):
        em.op("vector", lambda e, o=out, a=d0, b=d1, i=init, p0=op0, p1=op1:
              e.tensor_tensor_scan(out=o, data0=a, data1=b, initial=i,
                                   op0=p0, op1=p1), rd, wr, sr, tail=True)

    def MM(out, lhsT, rhs, start, stop, rd, wr):
        em.op("tensor", lambda e, o=out, l=lhsT, r=rhs, a=start, b=stop:
              e.matmul(o, l, r, start=a, stop=b), rd, wr, cost=150)

    def TR(out, in_, ident, rd, wr):
        em.op("tensor", lambda e, o=out, i=in_, d=ident:
              e.transpose(o, i, d), rd, wr, cost=120)

    def RECIP(out, in_, rd, wr):
        em.op("vector", lambda e, o=out, i=in_: e.reciprocal(o, i), rd, wr,
              tail=True, cost=150)


    def DMA(queue, out, in_, rd, wr, slow=False, cost=3000):
        em.dma(queue, lambda e, o=out, i=in_, sl=slow:
               e.dma_start(out=o, in_=i, allow_slow_non_contiguous=sl), rd, wr,
               cost=cost)

    wstate = {"i": 0, "off": 0, "offs": [], "c": 0}

    def wchunk(c):
        wstate["c"] = c
        wstate["i"] = 0
        if c == 0:
            wstate["offs"] = []
        wstate["gi"] = 0

    def wload(in_ap, cols, parts=128):
        assert parts == 128
        sl = wsl[wstate["i"] % NSLOT]
        wstate["i"] += 1
        gi = wstate["gi"]
        if wstate["c"] == 0:
            off = wstate["off"]
            greg = em.region(f"wg{gi}")
            wstate["offs"].append((off, cols, greg))
            wstate["off"] += cols
            assert wstate["off"] <= 400000
            blk = WSC[off * 128:(off + cols) * 128].rearrange(
                "(p c) -> p c", p=128)
            DMA("qg", blk, in_ap, rd=(), wr=[greg])
        off, cols0, greg = wstate["offs"][gi]
        assert cols0 == cols
        blk = WSC[off * 128:(off + cols) * 128].rearrange("(p c) -> p c", p=128)
        DMA("qs", sl[0][:, 0:cols], blk,
            rd=[greg], wr=[sl[1]], cost=4 * cols)
        wstate["gi"] = gi + 1
        return sl

    def wtiles(arr2d, r0, nr, c0, ncols):
        return arr2d[r0:r0 + nr, c0:c0 + ncols]

    def wgroup_rows(arr2d, c0, ncols):
        return arr2d[:, c0:c0 + ncols].rearrange("(f p) c -> p f c", p=128)

    def fmc(fp, s, i):
        return fp * WBLK + 1 + s * 257 + i

    def fms(fp, s, i):
        return fp * WBLK + s * 257 + i

    def plc(fp, tt, i=0):
        return fp * 512 + tt * 128 + i

    XT, XR = xnxf
    A_T, A_R = abuf
    X1T, X1R = x1
    YT, YR = ybuf
    PBT = pb

    def xnxf_seqs(fp, shifted=False):
        off = fp * WBLK + (0 if shifted else 1)
        return XT[:, off:off + 514].rearrange("p (s i) -> p s i", s=2,
                                              i=257)[:, :, 0:256]

    def dump(name, ap, rd):
        if name in dbg_h:
            DMA("qg", dbg_h[name], ap, rd=list(rd), wr=())

    DMA("qa", identf[0][:], CH["ident_f32"], (), [identf[1]])
    DMA("qg", identb[0][:], CH["ident_bf"], (), [identb[1]])
    DMA("qg", onescb[0][:], CH["ones_col_bf"], (), [onescb[1]])
    DMA("qg", onesrb[0][0:1, :], CH["ones_row_bf"], (), [onesrb[1]])
    DMA("qg", glam[0][:, 0:256], CH["gla_mask0"], (), [glam[1]])
    DMA("qg", glam[0][:, 256:512], CH["gla_mask1"], (), [glam[1]])
    DMA("qg", ffnm[0][:, 0:128], CH["mask_causal"], (), [ffnm[1]])
    DMA("qg", ffnm[0][:, 128:256], CH["mask_edge"], (), [ffnm[1]])
    DMA("qa", ones_rf[0][:], CH["ones_1x128_f32"], (), [ones_rf[1]])
    em.stable.add(ones_rf[1])

    DMA("qa", fcv[0][:], CH["fconst"], (), [fcv[1]])
    for _i, _v in enumerate([1e-5, 1.0, 0.5, 0.01]):
        nc.const_aps.register(F32, _v, fcv[0][:, _i:_i + 1])
    em.stable.add(fcv[1])
    em.op("scalar", lambda e: e.copy(sch[0][0:1, 31:32], fcv[0][0:1, 0:1]),
          [fcv[1]], [sch[1]])
    em.op("vector", lambda e: e.tensor_copy(sch[0][0:1, 30:31],
                                            fcv[0][0:1, 0:1]),
          [fcv[1]], [sch[1]])

    PPC = dict(_PPCOLS)
    DMA("qa", pp[0][:], PPH, (), [pp[1]])
    crows = S("crows", 1, 512)
    DMA("qa", crows[0][:], ROWH, (), [crows[1]])
    em.stable.add(crows[1])
    for srcn, dstn in (("fb1g0", "fb1gs0"), ("fb1g1", "fb1gs1")):
        em.op("scalar", lambda e, a=PPC[srcn], b=PPC[dstn]:
              e.mul(pp[0][:, b:b + 22], pp[0][:, a:a + 22],
                    0.7071067811865476), [pp[1]], [pp[1]])
    wgk1t = S("wgk1t", 128, 128, BF16)
    wgk2t = S("wgk2t", 16, 1024, BF16)
    DMA("qg", wgk1t[0][:], wgroup_rows(WH["Wgk1"], 0, 16), (), [wgk1t[1]])
    DMA("qa", big[0][0:16, 0:1024], WH["Wgk2"][0:16, :], (), [big[1]])
    SCOPY(wgk2t[0][0:16, :], big[0][0:16, 0:1024], [big[1]], [wgk2t[1]])
    em.stable.add(wgk1t[1]); em.stable.add(wgk2t[1])
    MM(pb[6][0][0:128, 0:256], ones_rf[0][:], crows[0][0:1, 0:256], True, True,
       [ones_rf[1], crows[1]], [pb[6][1]])
    VCOPY(rmswb[0][:], pb[6][0][0:128, 0:256], [pb[6][1]], [rmswb[1]])
    for i in range(4):
        MM(pb[7][0][0:128, 0:64], ones_rf[0][:],
           crows[0][0:1, 256 + i * 64:256 + (i + 1) * 64], True, True,
           [ones_rf[1], crows[1]], [pb[7][1]])
        VCOPY(qscb[0][:, i * 64:(i + 1) * 64], pb[7][0][0:128, 0:64],
              [pb[7][1]], [qscb[1]])

    em.op("gpsimd", lambda e: e.memset(hst[0][0][:], 0.0), (), [hst[0][1]])
    em.op("gpsimd", lambda e: e.memset(hst[1][0][:], 0.0), (), [hst[1][1]])
    em.op("gpsimd", lambda e: e.memset(carry[0][:], 0.0), (), [carry[1]])
    em.op("gpsimd", lambda e: e.memset(ybuf[0][:], 0.0), (), [ybuf[1]])
    em.op("gpsimd", lambda e: e.memset(xnxf[0][:], 0.0), (), [xnxf[1]])

    for rname in [identf[1], identb[1], onescb[1], onesrb[1], glam[1], ffnm[1],
                  pp[1], rmswb[1], qscb[1]]:
        em.stable.add(rname)
    em.op("scalar", lambda e: e.nop(),
          [], [fcv[1], identf[1], pp[1], crows[1], wgk2t[1]])
    em.op("gpsimd", lambda e: e.nop(),
          [], [identb[1], onescb[1], onesrb[1], glam[1], ffnm[1],
               wgk1t[1]])

    ALU_ = ALU

    def st_load(c):
        for tt in range(4):
            s, hi = tt // 2, tt % 2
            DMA("qa", x_tm[0][:, tt * 1024:(tt + 1) * 1024],
                x_h[s, c * TC + hi * 128: c * TC + hi * 128 + 128, :],
                (), [x_tm[1]], cost=8000)

    def st_outer(c):
        xr, sr_, br = x_tm[1], stat[1], big[1]
        for tt in range(4):
            xs = x_tm[0][:, tt * 1024:(tt + 1) * 1024]
            br_t = br if tt % 2 == 0 else bigB[1]
            ACT(big[0][:, (tt % 2) * 1024:(tt % 2) * 1024 + 1024], xs,
                AF.Square, [xr], [br_t])
            RED(stat[0][:, tt:tt + 1],
                big[0][:, (tt % 2) * 1024:(tt % 2) * 1024 + 1024], "sum",
                [br_t], [sr_])
            RED(stat[0][:, 4 + tt:5 + tt], xs, "sum", [xr], [sr_])
        TS("vector", stat[0][:, 8:12], stat[0][:, 4:8], 1.0 / D, None,
           ALU_.mult, None, [sr_], [sr_])
        ACT(stat[0][:, 24:28], stat[0][:, 8:12], AF.Square, [sr_], [sr_])
        STT("vector", stat[0][:, 28:32], stat[0][:, 0:4], 1.0 / D,
            stat[0][:, 24:28], ALU_.mult, ALU_.subtract, [sr_], [sr_])
        ACT(stat[0][:, 32:36], stat[0][:, 28:32], AF.Ln, [sr_], [sr_], bias=1e-5)
        ACT(stat[0][:, 12:16], stat[0][:, 32:36], AF.Exp, [sr_], [sr_], scale=-0.5)
        TT("vector", stat[0][:, 16:20], stat[0][:, 8:12], stat[0][:, 12:16],
           ALU_.mult, [sr_], [sr_])
        ACT(stat[0][:, 36:40], stat[0][:, 12:16], AF.Square, [sr_], [sr_])
        TT("vector", stat[0][:, 40:44], stat[0][:, 28:32], stat[0][:, 36:40],
           ALU_.mult, [sr_], [sr_])
        ACT(stat[0][:, 44:48], stat[0][:, 40:44], AF.Ln, [sr_], [sr_], bias=1e-5)
        ACT(stat[0][:, 20:24], stat[0][:, 44:48], AF.Exp, [sr_], [sr_], scale=-0.5)
        dump("statearly", stat[0][:], [sr_])
        if _DBG.get("outer_stats_only"):
            dump("xtm", x_tm[0][:], [x_tm[1]])
            return
        for s in range(2):
            for hi in range(2):
                tt = 2 * s + hi
                xs = x_tm[0][:, tt * 1024:(tt + 1) * 1024]
                br_h = br if hi == 0 else bigB[1]
                TS("vector", big[0][:, hi * 1024:(hi + 1) * 1024], xs,
                   stat[0][:, 12 + tt:13 + tt], stat[0][:, 16 + tt:17 + tt],
                   ALU_.mult, ALU_.subtract, [xr], [br_h], sr=[sr_])
                TS("vector", xnrtt[0][:], big[0][:, hi * 1024:(hi + 1) * 1024],
                   stat[0][:, 20 + tt:21 + tt], None, ALU_.mult, None,
                   [br_h], [xnrtt[1]], sr=[sr_])
                for fp in range(8):
                    bi = 2 + fp % 2
                    b = pb[bi]
                    TR(pbf(bi)[0:128, 0:128], xnrtt[0][:, fp * 128:(fp + 1) * 128],
                       identb[0][:], [xnrtt[1], identb[1]], [b[1]])
                    ACT(abuf[0][:, fp * 512 + tt * 128: fp * 512 + tt * 128 + 128],
                        pbf(bi)[0:128, 0:128], AF.Identity, [b[1], pp[1]], [abuf[1]],
                        scale=pp[0][:, PPC["ln1_w"] + fp:PPC["ln1_w"] + fp + 1],
                        bias=pp[0][:, PPC["ln1_b"] + fp:PPC["ln1_b"] + fp + 1])
            for fp in range(8):
                b = pb[fp % 2]
                for hi in range(2):
                    TR(b[0][0:128, hi * 128:hi * 128 + 128],
                       big[0][:, hi * 1024 + fp * 128: hi * 1024 + fp * 128 + 128],
                       identf[0][:], [br if hi == 0 else bigB[1], identf[1]],
                       [b[1]])
                SCOPY(xnxf[0][:, fmc(fp, s, 0):fmc(fp, s, 0) + 256],
                      b[0][0:128, 0:256], [b[1]], [xnxf[1]])
        dump("xn", xnxf[0][:], [xnxf[1]])
        dump("stat", stat[0][:], [stat[1]])
        dump("xn1", abuf[0][:], [abuf[1]])

    def st_gla(c):
        ar, ppr = abuf[1], pp[1]
        CUTS = ["u", "gk", "proj", "gate", "gv", "at", "o", "otr", "state", "wo"]
        cut = _DBG.get("gla_cut", "wo")
        nsh = _DBG.get("gla_sh", 8)

        def upto(name):
            return CUTS.index(cut) >= CUTS.index(name)
        for fin in range(8):
            MM(pb[4][0][0:16, 0:512], wgk1t[0][:, fin * 16:(fin + 1) * 16],
               abuf[0][:, fin * 512:(fin + 1) * 512], fin == 0, fin == 7,
               [wgk1t[1], ar], [pb[4][1]])
        SCOPY(u_bf[0][0:16, :], pb[4][0][0:16, 0:512], [pb[4][1]], [u_bf[1]])
        if not upto("gk"):
            return
        for s in range(2):
            for h in range(H_GLA):
                if s * 4 + h >= nsh:
                    continue
                par = (s * 4 + h) % 2
                BZ0, BZ1, BP, BG0, BG1, BO, BT1, BT2 = (
                    (4, 5, 6, 2, 3, 7, 4, 5) if par == 0 else
                    (0, 1, 2, 0, 1, 3, 0, 1))
                g_lx, g_mx = g_l, g_m
                g_sx = g_s if par == 0 else g_s2
                for kk in range(2):
                    zb = pb[BZ0 + kk * (BZ1 - BZ0)]
                    MM(zb[0][0:128, 0:256],
                       wgk2t[0][0:16, h * 256 + kk * 128:h * 256 + kk * 128 + 128],
                       u_bf[0][0:16, s * 256:(s + 1) * 256], True, True,
                       [wgk2t[1], u_bf[1]], [zb[1]])
                    ksl = slice(kk * 256, (kk + 1) * 256)
                    bg = pp[0][:, PPC["bgk2"] + 2 * h + kk:PPC["bgk2"] + 2 * h + kk + 1]
                    ACT(g_lx[0][:, ksl], zb[0][0:128, 0:256], AF.Abs,
                        [zb[1]], [g_lx[1]], sr=[ppr], bias=bg, scale=1.0)
                    ACT(g_sx[0][:, ksl], g_lx[0][:, ksl], AF.Exp,
                        [g_lx[1]], [g_sx[1]], scale=-1.0)
                    ACT(g_lx[0][:, ksl], g_sx[0][:, ksl], AF.Ln,
                        [g_sx[1]], [g_lx[1]], bias=1.0)
                    TS("vector", g_mx[0][:, ksl], zb[0][0:128, 0:256], bg, 0.0,
                       ALU_.add, ALU_.min, [zb[1]], [g_mx[1]], sr=[ppr])
                    SCAN(g_sx[0][:, ksl], g_mx[0][:, ksl], g_lx[0][:, ksl], 0.0,
                         ALU_.add, ALU_.subtract, [g_mx[1], g_lx[1]], [g_sx[1]])
                for kk in range(2):
                    VCOPY(sch[0][:, kk:kk + 1],
                          g_sx[0][:, kk * 256 + 255:kk * 256 + 256],
                          [g_sx[1]], [sch[1]])
                ACT(sch[0][:, 2:4], sch[0][:, 0:2], AF.Exp, [sch[1]], [sch[1]],
                    scale=1.0 / 16.0)
                ACT(sch[0][:, 4:6], sch[0][:, 0:2], AF.Exp, [sch[1]], [sch[1]],
                    scale=-1.0 / 16.0)
                TT("vector", g_mx[0][:].rearrange("p (c i) -> p c i", c=2),
                   g_sx[0][:].rearrange("p (c i) -> p c i", c=2),
                   sch[0][:, 0:2].unsqueeze(2).broadcast_to([128, 2, 256]),
                   ALU_.subtract, [g_sx[1], sch[1]], [g_mx[1]])
                ACT(g_lx[0][:], g_sx[0][:], AF.Exp, [g_sx[1]], [g_lx[1]],
                    scale=1.0 / 16.0)
                ACT(g_sx[0][:], g_mx[0][:], AF.Exp, [g_mx[1]], [g_sx[1]],
                    scale=-1.0 / 16.0)
                if not upto("proj"):
                    continue
                for Wn, dst in (("Wq", g_q), ("Wk", g_k)):
                    for kk in range(2):
                        grp = wload(wgroup_rows(WH[Wn], h * 256 + kk * 128, 128),
                                    1024)
                        bk = pb[BP]
                        for fin in range(8):
                            MM(bk[0][0:128, 0:256],
                               grp[0][:, fin * 128:(fin + 1) * 128],
                               abuf[0][:, fin * 512 + s * 256:
                                        fin * 512 + s * 256 + 256],
                               fin == 0, fin == 7, [grp[1], ar], [bk[1]])
                        VCOPY(dst[0][:, kk * 256:(kk + 1) * 256],
                              bk[0][0:128, 0:256], [bk[1]], [dst[1]])
                if not upto("gate"):
                    continue
                STT("vector", g_qb[0][:], g_q[0][:], 1.0 / 16.0, g_lx[0][:],
                    ALU_.mult, ALU_.mult, [g_q[1], g_lx[1]], [g_qb[1]])
                TT("vector", g_qt[0][:].rearrange("p (c i) -> p c i", c=2),
                   g_qb[0][:].rearrange("p (c i) -> p c i", c=2),
                   sch[0][:, 4:6].unsqueeze(2).broadcast_to([128, 2, 256]),
                   ALU_.mult, [g_qb[1], sch[1]], [g_qt[1]])
                TT("vector", g_kh[0][:], g_k[0][:], g_sx[0][:], ALU_.mult,
                   [g_k[1], g_sx[1]], [g_kh[1]])
                if not upto("gv"):
                    continue
                for Wn, dst in (("Wg", g_gg), ("Wv", g_v)):
                    for half in range(2):
                        grp = wload(
                            WH[Wn][half * 512:(half + 1) * 512,
                                   h * 256:(h + 1) * 256].rearrange(
                                "(f p) c -> p f c", p=128), 1024)
                        for hi in range(2):
                            tt = 2 * s + hi
                            bgv = pb[BG0 if hi == 0 else BG1]
                            for f4 in range(4):
                                fin = half * 4 + f4
                                MM(bgv[0][0:128, 256:512],
                                   abuf[0][:, fin * 512 + tt * 128:
                                            fin * 512 + tt * 128 + 128],
                                   grp[0][:, f4 * 256:(f4 + 1) * 256],
                                   half == 0 and f4 == 0, half == 1 and f4 == 3,
                                   [grp[1], ar], [bgv[1]])
                    for hi in range(2):
                        bgv = pb[BG0 if hi == 0 else BG1]
                        if dst is g_v:
                            SCOPY(dst[0][:, hi * 256:(hi + 1) * 256],
                                  bgv[0][0:128, 256:512],
                                  [bgv[1]], [dst[1]])
                        else:
                            VCOPY(dst[0][:, hi * 256:(hi + 1) * 256],
                                  bgv[0][0:128, 256:512],
                                  [bgv[1]], [dst[1]])
                if not upto("at"):
                    continue
                for sh in range(2):
                    for kk in range(2):
                        MM(pb[BP][0][0:128, sh * 256:(sh + 1) * 256],
                           g_kh[0][:, kk * 256 + sh * 128:kk * 256 + sh * 128 + 128],
                           g_qt[0][:, kk * 256:(kk + 1) * 256],
                           kk == 0, kk == 1, [g_kh[1], g_qt[1]], [pb[BP][1]])
                    TT("vector", g_at[0][:, sh * 256:(sh + 1) * 256],
                       pb[BP][0][0:128, sh * 256:(sh + 1) * 256],
                       glam[0][:, sh * 256:(sh + 1) * 256],
                       ALU_.mult, [pb[BP][1], glam[1]], [g_at[1]])
                if not upto("o"):
                    continue
                for hi in range(2):
                    ob = pb[BO]
                    MM(ob[0][0:128, 0:256],
                       g_at[0][:, 0 * 256 + hi * 128:0 * 256 + hi * 128 + 128],
                       g_v[0][:, 0:256], True, False,
                       [g_at[1], g_v[1]], [ob[1]])
                    MM(ob[0][0:128, 0:256],
                       g_at[0][:, 1 * 256 + hi * 128:1 * 256 + hi * 128 + 128],
                       g_v[0][:, 256:512], False, False,
                       [g_at[1], g_v[1]], [ob[1]])
                    MM(ob[0][0:128, 0:256],
                       g_qb[0][:, 0 * 256 + hi * 128:0 * 256 + hi * 128 + 128],
                       hst[s][0][:, (2 * h) * 256:(2 * h) * 256 + 256],
                       False, False, [g_qb[1], hst[s][1]], [ob[1]])
                    MM(ob[0][0:128, 0:256],
                       g_qb[0][:, 1 * 256 + hi * 128:1 * 256 + hi * 128 + 128],
                       hst[s][0][:, (2 * h + 1) * 256:(2 * h + 1) * 256 + 256],
                       False, True, [g_qb[1], hst[s][1]], [ob[1]])
                    hsl = slice(hi * 256, (hi + 1) * 256)
                    ACT(big[0][:, hi * 256:(hi + 1) * 256], ob[0][0:128, 0:256],
                        AF.Square, [ob[1]], [big[1]])
                    RED(sch[0][:, 8 + hi:9 + hi],
                        big[0][:, hi * 256:(hi + 1) * 256], "sum",
                        [big[1]], [sch[1]])
                    ACT(sch[0][:, 10 + hi:11 + hi], sch[0][:, 8 + hi:9 + hi],
                        AF.Ln, [sch[1]], [sch[1]], scale=1.0 / 256.0, bias=1e-5)
                    ACT(sch[0][:, 12 + hi:13 + hi], sch[0][:, 10 + hi:11 + hi],
                        AF.Exp, [sch[1]], [sch[1]], scale=-0.5)
                    ACT(g_sg[0][:, hsl], g_gg[0][:, hsl], AF.Sigmoid,
                        [g_gg[1]], [g_sg[1]])
                    TT("vector", g_sg[0][:, hsl], g_gg[0][:, hsl],
                       g_sg[0][:, hsl], ALU_.mult, [g_gg[1], g_sg[1]], [g_sg[1]])
                    STT("vector", g_gg[0][:, hsl], ob[0][0:128, 0:256],
                        sch[0][:, 12 + hi:13 + hi], g_sg[0][:, hsl],
                        ALU_.mult, ALU_.mult, [ob[1], g_sg[1]], [g_gg[1]],
                        sr=[sch[1]])
                    TT("vector", g_ob[0][:, hsl], g_gg[0][:, hsl],
                       rmswb[0][:, 0:256], ALU_.mult, [g_gg[1], rmswb[1]],
                       [g_ob[1]])
                if not upto("otr"):
                    continue
                for vv in range(2):
                    for hi in range(2):
                        TR(pbf(BT1)[0:128, 512 + hi * 128:512 + hi * 128 + 128],
                           g_ob[0][:, hi * 256 + vv * 128:hi * 256 + vv * 128 + 128],
                           identb[0][:], [g_ob[1], identb[1]], [pb[BT1][1]])
                    SCOPY(ofm[0][:, (2 * h + vv) * 256:(2 * h + vv) * 256 + 256],
                          pbf(BT1)[0:128, 512:768], [pb[BT1][1]], [ofm[1]])
                if not upto("state"):
                    continue
                for sh in range(2):
                    for kk in range(2):
                        TR(pbf(BT2)[0:128, 512 + kk * 128:512 + kk * 128 + 128],
                           g_kh[0][:, kk * 256 + sh * 128:kk * 256 + sh * 128 + 128],
                           identb[0][:], [g_kh[1], identb[1]], [pb[BT2][1]])
                    SCOPY(g_kt[0][:, sh * 256:(sh + 1) * 256],
                          pbf(BT2)[0:128, 512:768], [pb[BT2][1]], [g_kt[1]])
                for kk in range(2):
                    for sh in range(2):
                        MM(pb[BP][0][0:128, kk * 256:(kk + 1) * 256],
                           g_kt[0][:, sh * 256 + kk * 128:sh * 256 + kk * 128 + 128],
                           g_v[0][:, sh * 256:(sh + 1) * 256],
                           sh == 0, sh == 1, [g_kt[1], g_v[1]], [pb[BP][1]])
                    TS("vector", g_sg[0][:, kk * 256:(kk + 1) * 256],
                       hst[s][0][:, (2 * h + kk) * 256:(2 * h + kk) * 256 + 256],
                       sch[0][:, 2 + kk:3 + kk], None, ALU_.mult, None,
                       [hst[s][1]], [g_sg[1]], sr=[sch[1]])
                    TT("vector",
                       hst[s][0][:, (2 * h + kk) * 256:(2 * h + kk) * 256 + 256],
                       g_sg[0][:, kk * 256:(kk + 1) * 256],
                       pb[BP][0][0:128, kk * 256:(kk + 1) * 256], ALU_.add,
                       [g_sg[1], pb[BP][1]], [hst[s][1]])
            if not upto("wo") or nsh < 8:
                continue
            for vin in range(8):
                grp = wload(WH["Wo"][vin * 128:(vin + 1) * 128, :], 1024)
                for fo in range(8):
                    MM(pb[fo][0][0:128, 0:256],
                       grp[0][:, fo * 128:(fo + 1) * 128],
                       ofm[0][:, vin * 256:(vin + 1) * 256],
                       vin == 0, vin == 7, [grp[1], ofm[1]], [pb[fo][1]])
            for fo in range(8):
                TT("vector", x1[0][:, fo * 512 + s * 256:fo * 512 + s * 256 + 256],
                   pb[fo][0][0:128, 0:256],
                   xnxf[0][:, fmc(fo, s, 0):fmc(fo, s, 0) + 256],
                   ALU_.add, [pb[fo][1], xnxf[1]], [x1[1]])
        dump("x1", x1[0][:], [x1[1]])

    def st_ln2_mlp(c):
        xr = x1[1]
        SCOPY(ybuf[0][:, 0:4096], x1[0][:, 0:4096], [xr], [ybuf[1]])
        for fp in range(8):
            MM(pb[4][0][0:1, 0:512], onescb[0][:],
               ybuf[0][:, fp * 512:(fp + 1) * 512], fp == 0, fp == 7,
               [onescb[1], ybuf[1]], [pb[4][1]])
        ACT(ybuf[0][:, 0:4096], x1[0][:, 0:4096], AF.Square, [xr], [ybuf[1]])
        for fp in range(8):
            MM(pb[5][0][0:1, 0:512], onescb[0][:],
               ybuf[0][:, fp * 512:(fp + 1) * 512], fp == 0, fp == 7,
               [onescb[1], ybuf[1]], [pb[5][1]])
        _ln_rows_and_bcast(pb[4], pb[5])
        for half in range(2):
            _ln_apply_fm(x1[0][:, half * 2048:(half + 1) * 2048].rearrange(
                "p (f s i) -> p f s i", f=4, s=2), half, "ln2_w", "ln2_b", [xr],
                pb[4], pb[5])
        for half in range(3):
            for j in range(8):
                jj = half * 8 + j
                grp = wload(wgroup_rows(WH["W1"], jj * 128, 128), 1024)
                ub = pb[4 + j % 4]
                for fin in range(8):
                    MM(ub[0][0:128, 0:512], grp[0][:, fin * 128:(fin + 1) * 128],
                       abuf[0][:, fin * 512:(fin + 1) * 512], fin == 0, fin == 7,
                       [grp[1], abuf[1]], [ub[1]])
                bcol = pp[0][:, PPC["b1a"] + jj:PPC["b1a"] + jj + 1]
                uslc = slice((j % 4) * 512, (j % 4) * 512 + 512)
                ureg = big[1] if j % 4 < 2 else bigB[1]
                ACT(big[0][:, uslc], ub[0][0:128, 0:512], AF.Prelu,
                    [ub[1]], [ureg], sr=[pp[1]], bias=bcol, alpha=0.01)
                ACT(ybuf[0][:, j * 512:(j + 1) * 512],
                    big[0][:, uslc], AF.Square, [ureg], [ybuf[1]])
            for j in range(8):
                jj = half * 8 + j
                grp = wload(WH["W2"][jj * 128:(jj + 1) * 128, :], 1024)
                for fo in range(8):
                    MM(pb[fo][0][0:128, 0:512], grp[0][:, fo * 128:(fo + 1) * 128],
                       ybuf[0][:, j * 512:(j + 1) * 512], j == 0, j == 7,
                       [grp[1], ybuf[1]], [pb[fo][1]])
            for fo in range(8):
                if half < 2:
                    TT("vector", x1[0][:, fo * 512:(fo + 1) * 512],
                       x1[0][:, fo * 512:(fo + 1) * 512], pb[fo][0][0:128, 0:512],
                       ALU_.add, [pb[fo][1], xr], [xr])
                else:
                    STT("vector", x1[0][:, fo * 512:(fo + 1) * 512],
                        pb[fo][0][0:128, 0:512],
                        pp[0][:, PPC["b2a"] + fo:PPC["b2a"] + fo + 1],
                        x1[0][:, fo * 512:(fo + 1) * 512], ALU_.add, ALU_.add,
                        [pb[fo][1], xr], [xr], sr=[pp[1]])
        dump("attnres", x1[0][:], [x1[1]])

    def _ln_rows_and_bcast(sum_pb, ssq_pb):
        TS("vector", rowm[0][0:1, :], sum_pb[0][0:1, 0:512], 1.0 / D, None,
           ALU_.mult, None, [sum_pb[1]], [rowm[1]])
        ACT(rowr[0][0:1, :], rowm[0][0:1, :], AF.Square, [rowm[1]], [rowr[1]])
        STT("vector", rowr[0][0:1, :], ssq_pb[0][0:1, 0:512], 1.0 / D,
            rowr[0][0:1, :], ALU_.mult, ALU_.subtract,
            [ssq_pb[1], rowr[1]], [rowr[1]])
        ACT(rowr[0][0:1, :], rowr[0][0:1, :], AF.Ln, [rowr[1]], [rowr[1]],
            bias=1e-5)
        ACT(rowr[0][0:1, :], rowr[0][0:1, :], AF.Exp, [rowr[1]], [rowr[1]],
            scale=-0.5)
        TT("vector", rowm[0][0:1, :], rowm[0][0:1, :], rowr[0][0:1, :],
           ALU_.mult, [rowm[1], rowr[1]], [rowm[1]])
        MM(sum_pb[0][0:128, 0:512], ones_rf[0][:], rowr[0][0:1, 0:512],
           True, True, [ones_rf[1], rowr[1]], [sum_pb[1]])
        MM(ssq_pb[0][0:128, 0:512], ones_rf[0][:], rowm[0][0:1, 0:512],
           True, True, [ones_rf[1], rowm[1]], [ssq_pb[1]])

    def _ln_apply_fm(in4_ap, half, wname, bname, in_rd, rpb, mpb):
        big4 = big[0][:, 0:2048].rearrange("p (f s i) -> p f s i", f=4, s=2)
        rb4 = rpb[0][0:128, 0:512].rearrange("p (s i) -> p s i", s=2).unsqueeze(
            1).broadcast_to([128, 4, 2, 256])
        mrb4 = mpb[0][0:128, 0:512].rearrange("p (s i) -> p s i", s=2).unsqueeze(
            1).broadcast_to([128, 4, 2, 256])
        TT("vector", big4, in4_ap, rb4, ALU_.mult,
           list(in_rd) + [rpb[1]], [big[1], bigB[1]])
        TT("vector", big4, big4, mrb4, ALU_.subtract,
           [big[1], mpb[1], bigB[1]], [big[1], bigB[1]])
        for f4 in range(4):
            fp = half * 4 + f4
            ACT(abuf[0][:, fp * 512:(fp + 1) * 512],
                big[0][:, f4 * 512:(f4 + 1) * 512], AF.Identity,
                [(big[1] if f4 < 2 else bigB[1]), pp[1]], [abuf[1]],
                scale=pp[0][:, PPC[wname] + fp:PPC[wname] + fp + 1],
                bias=pp[0][:, PPC[bname] + fp:PPC[bname] + fp + 1])

    XV3 = xnxf[0][:].rearrange("p (f x) -> p f x", f=8)

    def _shift_cast(c, site):
        cr, xr = carry[1], xnxf[1]
        for s in range(2):
            VCOPY(XV3[:, :, 257 * s], carry[0][:, site * 16 + s * 8:
                                               site * 16 + s * 8 + 8],
                  [cr], [xr])
        for s in range(2):
            VCOPY(carry[0][:, site * 16 + s * 8:site * 16 + s * 8 + 8],
                  XV3[:, :, 256 + 257 * s], [xr], [cr])
        for half in range(2):
            lo = 1 if half == 0 else 0
            src = XV3[:, half * 4:(half + 1) * 4, lo:lo + 514].rearrange(
                "p f (s i) -> p f s i", s=2, i=257)[:, :, :, 0:256]
            dst = abuf[0][:, half * 2048:(half + 1) * 2048].rearrange(
                "p (f s i) -> p f s i", f=4, s=2, i=256)
            SCOPY(dst, src, [xr], [abuf[1]])

    def st_ffn_layer(c, l):
        lr = "0" if l == 0 else "1"
        ar, xr = abuf[1], xnxf[1]
        _shift_cast(c, l * 2 + 0)
        for p3, dsts in ((0, "q"), (1, "k"), (2, "v")):
            for fin in range(8):
                grp = wload(WH["fwqkv"][l, fin * 128:(fin + 1) * 128,
                                        p3 * 512:(p3 + 1) * 512], 512)
                for tt in range(4):
                    MM(pb[tt][0][0:128, 0:512],
                       abuf[0][:, fin * 512 + tt * 128:fin * 512 + tt * 128 + 128],
                       grp[0][:, 0:512], fin == 0, fin == 7,
                       [grp[1], ar], [pb[tt][1]])
            for tt in range(4):
                s, hi = tt // 2, tt % 2
                if dsts == "q":
                    SCOPY(ftq[0][:, tt * 512:(tt + 1) * 512],
                          pb[tt][0][0:128, 0:512], [pb[tt][1]], [ftq[1]])
                elif dsts == "k":
                    SCOPY(ftk[0][:, tt * 512:(tt + 1) * 512],
                          pb[tt][0][0:128, 0:512], [pb[tt][1]], [ftk[1]])
                else:
                    rr = (c % 3) * 2 + hi
                    SCOPY(vca[l][s][0][:, rr * 512:(rr + 1) * 512],
                          pb[tt][0][0:128, 0:512], [pb[tt][1]], [vca[l][s][1]])
        for qi, nm in enumerate(["tab_qc", "tab_qs", "tab_kc", "tab_ks"]):
            for hi in range(2):
                DMA("qa", tabs[0][:, qi * 128 + hi * 64:qi * 128 + hi * 64 + 64],
                    CH[nm][c * TC + hi * 128: c * TC + hi * 128 + 128, :],
                    (), [tabs[1]])
        for ft, qk in ((ftq, 0), (ftk, 1)):
            fr = ft[1]
            ACT(big[0][:, 0:2048], ft[0][:], AF.Square, [fr], [big[1]])
            RED(stat[0][:, 32:64].rearrange("p (g o) -> p g o", o=1),
                big[0][:, 0:2048].rearrange("p (g d) -> p g d", d=64),
                "sum", [big[1]], [stat[1]])
            TS("vector", stat[0][:, 32:64], stat[0][:, 32:64], 1e-24, None,
               ALU_.max, None, [stat[1]], [stat[1]])
            ACT(stat[0][:, 0:32], stat[0][:, 32:64], AF.Ln, [stat[1]], [stat[1]])
            ACT(stat[0][:, 32:64], stat[0][:, 0:32], AF.Exp, [stat[1]], [stat[1]],
                scale=-0.5)
            f3 = ft[0][:].rearrange("p (g d) -> p g d", d=64)
            TT("vector", f3, f3,
               stat[0][:, 32:64].unsqueeze(2).broadcast_to([128, 32, 64]),
               ALU_.mult, [fr, stat[1]], [fr])
            qs_off = (l * 2 + qk) * 64
            TT("vector", f3, f3,
               qscb[0][:, qs_off:qs_off + 64].unsqueeze(1).broadcast_to(
                   [128, 32, 64]), ALU_.mult, [fr, qscb[1]], [fr])
            tb = tabs[0][:, qk * 256:qk * 256 + 128]
            ts_ = tabs[0][:, qk * 256 + 128:qk * 256 + 256]
            for s in range(2):
                half_ap = ft[0][:, s * 1024:(s + 1) * 1024]
                h4 = half_ap.rearrange("p (hi g d) -> p hi g d", hi=2, d=64)
                cosb = tb.rearrange("p (hi d) -> p hi d", hi=2).unsqueeze(
                    2).broadcast_to([128, 2, 8, 64])
                sinb = ts_.rearrange("p (hi d) -> p hi d", hi=2).unsqueeze(
                    2).broadcast_to([128, 2, 8, 64])
                big4a = big[0][:, 0:1024].rearrange(
                    "p (hi g d) -> p hi g d", hi=2, d=64)
                big4b = big[0][:, 1024:2048].rearrange(
                    "p (hi g d) -> p hi g d", hi=2, d=64)
                TT("vector", big4a, h4, cosb, ALU_.mult,
                   [fr, tabs[1]], [big[1]])
                h4h = half_ap.rearrange("p (b two hd) -> p b two hd",
                                        two=2, hd=32)
                b4h = big[0][:, 1024:2048].rearrange(
                    "p (b two hd) -> p b two hd", two=2, hd=32)
                em.op("scalar", lambda e, o=b4h[:, :, 0, :],
                      i=h4h[:, :, 1, :]: e.mul(o, i, -1.0), [fr], [bigB[1]])
                em.op("scalar", lambda e, o=b4h[:, :, 1, :],
                      i=h4h[:, :, 0, :]: e.copy(o, i), [fr], [bigB[1]])
                TT("vector", big4b, big4b, sinb, ALU_.mult,
                   [bigB[1], tabs[1]], [bigB[1]])
                TT("vector", half_ap, big[0][:, 0:1024], big[0][:, 1024:2048],
                   ALU_.add, [big[1], bigB[1]], [fr])
        for dd in range(4):
            bi = 4 + dd % 2
            for tt in range(4):
                TR(pbf(bi)[0:128, tt * 128:tt * 128 + 128],
                   ftq[0][:, tt * 512 + dd * 128:tt * 512 + dd * 128 + 128],
                   identb[0][:], [ftq[1], identb[1]], [pb[bi][1]])
            SCOPY(fqf[0][:, dd * 512:(dd + 1) * 512], pbf(bi)[0:128, 0:512],
                  [pb[bi][1]], [fqf[1]])
        rslot = (c % 3) * 256
        for dd in range(4):
            bi = 4 + dd % 2
            for tt in range(4):
                TR(pbf(bi)[0:128, tt * 128:tt * 128 + 128],
                   ftk[0][:, tt * 512 + dd * 128:tt * 512 + dd * 128 + 128],
                   identb[0][:], [ftk[1], identb[1]], [pb[bi][1]])
            for s in range(2):
                SCOPY(kca[l][s][0][:, dd * 768 + rslot:dd * 768 + rslot + 256],
                      pbf(bi)[0:128, s * 256:s * 256 + 256], [pb[bi][1]],
                      [kca[l][s][1]])
        if l == 0:
            dump("fqf0", fqf[0][:], [fqf[1]])
            dump("kc00", kca[0][0][0][:], [kca[0][0][1]])
        for s in range(2):
            for hh in range(FFN_HEADS):
                dd, ph = hh // 2, 64 * (hh % 2)
                parm = (s * 8 + hh) % 2
                bA, bB = (pb[6], pb[7]) if parm == 0 else (pb[4], pb[5])
                simo = 0 if parm == 0 else 1024
                simr = big[1] if parm == 0 else bigB[1]
                atnb = atn if parm == 0 else atn2
                sco = 0 if parm == 0 else 8
                for qt in range(2):
                    gq = 2 * c + qt
                    kts = [(g, ("causal" if g == gq else
                                ("edge" if g == gq - 4 else "full")))
                           for g in range(max(0, gq - 4), gq + 1)]
                    nk = len(kts)
                    for ki, (g, kind) in enumerate(kts):
                        bank = bA if ki < 4 else bB
                        col = (ki % 4) * 128
                        rcol = (g // 2 % 3) * 256 + (g % 2) * 128
                        MM(bank[0][0:128, col:col + 128],
                           fqf[0][ph:ph + 64,
                                  dd * 512 + s * 256 + qt * 128:
                                  dd * 512 + s * 256 + qt * 128 + 128],
                           kca[l][s][0][ph:ph + 64,
                                        dd * 768 + rcol:dd * 768 + rcol + 128],
                           True, True, [fqf[1], kca[l][s][1]], [bank[1]])
                        dst = big[0][:, simo + ki * 128:simo + ki * 128 + 128]
                        if kind == "full":
                            SCOPY(dst, bank[0][0:128, col:col + 128],
                                  [bank[1]], [simr])
                        else:
                            moff = 0 if kind == "causal" else 128
                            TT("vector", dst, bank[0][0:128, col:col + 128],
                               ffnm[0][:, moff:moff + 128], ALU_.add,
                               [bank[1], ffnm[1]], [simr])
                    RED(stat[0][:, sco:sco + 1],
                        big[0][:, simo:simo + nk * 128], "max",
                        [simr], [stat[1]])
                    TS("vector", stat[0][:, sco + 1:sco + 2],
                       stat[0][:, sco:sco + 1], -1.0, None,
                       ALU_.mult, None, [stat[1]], [stat[1]])
                    ACT(atnb[0][:, 0:nk * 128], big[0][:, simo:simo + nk * 128],
                        AF.Exp, [simr, stat[1]], [atnb[1]],
                        sr=[stat[1]], bias=stat[0][:, sco + 1:sco + 2])
                    RED(stat[0][:, sco + 2:sco + 3], atnb[0][:, 0:nk * 128],
                        "sum", [atnb[1]], [stat[1]])
                    RECIP(stat[0][:, sco + 3:sco + 4],
                          stat[0][:, sco + 2:sco + 3], [stat[1]], [stat[1]])
                    TS("vector", atnb[0][:, 0:nk * 128], atnb[0][:, 0:nk * 128],
                       stat[0][:, sco + 3:sco + 4], None, ALU_.mult, None,
                       [atnb[1]], [atnb[1]], sr=[stat[1]])
                    if (l, s, hh, qt) == _DBG.get("ffn_probe", (None,) * 4):
                        dump("simdmp", big[0][:], [big[1]])
                        dump("atnpre", atn[0][:], [atn[1]])
                        dump("statdmp", stat[0][:], [stat[1]])
                    bAn = 6 if parm == 0 else 4
                    for ki in range(nk):
                        TR(pbf(bAn)[0:128, ki * 128:(ki + 1) * 128],
                           atnb[0][:, ki * 128:ki * 128 + 128], identb[0][:],
                           [atnb[1], identb[1]], [bA[1]])
                    SCOPY(atnb[0][:, 0:nk * 128], pbf(bAn)[0:128, 0:nk * 128],
                          [bA[1]], [atnb[1]])
                    for ki, (g, kind) in enumerate(kts):
                        rr = (g // 2 % 3) * 2 + (g % 2)
                        MM(bB[0][0:64, 256 + qt * 128:256 + qt * 128 + 128],
                           vca[l][s][0][:, rr * 512 + hh * 64:
                                        rr * 512 + hh * 64 + 64],
                           atnb[0][:, ki * 128:ki * 128 + 128],
                           ki == 0, ki == nk - 1,
                           [vca[l][s][1], atnb[1]], [bB[1]])
                if True:
                    SCOPY(offm[0][ph:ph + 64,
                                  dd * 512 + s * 256:dd * 512 + s * 256 + 256],
                          bB[0][0:64, 256:512], [bB[1]], [offm[1]])
        dump("fq" + lr, ftq[0][:], [ftq[1]])
        dump("fk" + lr, ftk[0][:], [ftk[1]])
        dump("offm" + lr, offm[0][:], [offm[1]])
        for vin in range(4):
            grp = wload(WH["fwo"][l, vin * 128:(vin + 1) * 128, :], 1024)
            for fo in range(8):
                MM(pb[fo][0][0:128, 0:512], grp[0][:, fo * 128:(fo + 1) * 128],
                   offm[0][:, vin * 512:(vin + 1) * 512], vin == 0, vin == 3,
                   [grp[1], offm[1]], [pb[fo][1]])
        for fo in range(8):
            TT("vector", xnxf_seqs(fo),
               pb[fo][0][0:128, 0:512].rearrange("p (s i) -> p s i", s=2),
               xnxf_seqs(fo), ALU_.add, [pb[fo][1], xr], [xr])
        dump("mid" + lr, xnxf[0][:], [xr])
        _shift_cast(c, l * 2 + 1)
        for fp in range(8):
            MM(pb[4][0][0:1, 0:512], onescb[0][:],
               abuf[0][:, fp * 512:(fp + 1) * 512], fp == 0, fp == 7,
               [onescb[1], ar], [pb[4][1]])
        ACT(ybuf[0][:, 0:4096], abuf[0][:, 0:4096], AF.Square, [ar], [ybuf[1]])
        for fp in range(8):
            MM(pb[5][0][0:1, 0:512], onescb[0][:],
               ybuf[0][:, fp * 512:(fp + 1) * 512], fp == 0, fp == 7,
               [onescb[1], ybuf[1]], [pb[5][1]])
        _ln_rows_and_bcast(pb[4], pb[5])
        for half in range(2):
            lo = 1 if half == 0 else 0
            src4 = XV3[:, half * 4:(half + 1) * 4, lo:lo + 514].rearrange(
                "p f (s i) -> p f s i", s=2, i=257)[:, :, :, 0:256]
            _ln_apply_fm(src4, half, "flng" + lr, "flnb" + lr, [xr],
                         pb[4], pb[5])
        for half in range(3):
            npass = (8, 8, 6)[half]
            for j in range(npass):
                jj = half * 8 + j
                ncc = 128 if jj < 21 else 42
                grpA = wload(WH["fw1"][l][:, jj * 128:jj * 128 + ncc].rearrange(
                    "(f p) c -> p f c", p=128), 8 * ncc)
                grpG = wload(WH["fw1"][l][:, HID + jj * 128:
                                          HID + jj * 128 + ncc].rearrange(
                    "(f p) c -> p f c", p=128), 8 * ncc)
                bA, bG = (pb[6], pb[7]) if j % 2 == 0 else (pb[4], pb[5])
                for fin in range(8):
                    MM(bA[0][0:ncc, 0:512],
                       grpA[0][:, fin * ncc:(fin + 1) * ncc],
                       abuf[0][:, fin * 512:(fin + 1) * 512], fin == 0, fin == 7,
                       [grpA[1], ar], [bA[1]])
                for fin in range(8):
                    MM(bG[0][0:ncc, 0:512],
                       grpG[0][:, fin * ncc:(fin + 1) * ncc],
                       abuf[0][:, fin * 512:(fin + 1) * 512], fin == 0, fin == 7,
                       [grpG[1], ar], [bG[1]])
                bga = pp[0][0:ncc, PPC["fb1a" + lr] + jj:PPC["fb1a" + lr] + jj + 1]
                bgg = pp[0][0:ncc, PPC["fb1g" + lr] + jj:PPC["fb1g" + lr] + jj + 1]
                bgs = pp[0][0:ncc, PPC["fb1gs" + lr] + jj:
                            PPC["fb1gs" + lr] + jj + 1]
                eslc = slice(0, 512) if j % 2 == 0 else slice(1024, 1536)
                fslc = slice(512, 1024) if j % 2 == 0 else slice(1536, 2048)
                ACT(big[0][0:ncc, eslc], bG[0][0:ncc, 0:512], AF.Erf,
                    [bG[1]], [big[1]], sr=[pp[1]],
                    scale=0.7071067811865476, bias=bgs)
                ACT(big[0][0:ncc, fslc], big[0][0:ncc, eslc], AF.Identity,
                    [big[1]], [big[1]], scale=0.5, bias=0.5)
                STT("vector", big[0][0:ncc, eslc], bG[0][0:ncc, 0:512],
                    bgg, big[0][0:ncc, fslc], ALU_.add, ALU_.mult,
                    [bG[1], big[1]], [big[1]], sr=[pp[1]])
                STT("vector", ybuf[0][0:ncc, j * 512:(j + 1) * 512],
                    bA[0][0:ncc, 0:512], bga, big[0][0:ncc, eslc],
                    ALU_.add, ALU_.mult, [bA[1], big[1]], [ybuf[1]],
                    sr=[pp[1]])
            for j in range(npass):
                jj = half * 8 + j
                grp = wload(FW2P[l][jj * 128:(jj + 1) * 128, :], 1024)
                for fo in range(8):
                    MM(pb[fo][0][0:128, 0:512],
                       grp[0][:, fo * 128:(fo + 1) * 128],
                       ybuf[0][:, j * 512:(j + 1) * 512], j == 0, j == npass - 1,
                       [grp[1], ybuf[1]], [pb[fo][1]])
            for fo in range(8):
                if half < 2:
                    TT("vector", xnxf_seqs(fo),
                       pb[fo][0][0:128, 0:512].rearrange("p (s i) -> p s i", s=2),
                       xnxf_seqs(fo), ALU_.add, [pb[fo][1], xr], [xr])
                else:
                    STT("vector", xnxf_seqs(fo),
                        pb[fo][0][0:128, 0:512].rearrange("p (s i) -> p s i",
                                                          s=2),
                        pp[0][:, PPC["fb2_" + lr] + fo:PPC["fb2_" + lr] + fo + 1],
                        xnxf_seqs(fo), ALU_.add, ALU_.add,
                        [pb[fo][1], xr], [xr], sr=[pp[1]])
        dump("ffn" + lr, xnxf[0][:], [xr])

    def st_final(c):
        for tt in range(4):
            s, hi = tt // 2, tt % 2
            for half in range(2):
                bA, bB = pb[0 + half], pb[2 + half]
                for f4 in range(4):
                    fp = half * 4 + f4
                    TR(bA[0][0:128, f4 * 128:f4 * 128 + 128],
                       x1[0][:, fp * 512 + tt * 128:fp * 512 + tt * 128 + 128],
                       identf[0][:], [x1[1], identf[1]], [bA[1]])
                    TR(bB[0][0:128, f4 * 128:f4 * 128 + 128],
                       xnxf[0][:, fmc(fp, s, hi * 128):
                               fmc(fp, s, hi * 128) + 128],
                       identf[0][:], [xnxf[1], identf[1]], [bB[1]])
                dst = x_tm[0][:, tt * 1024 + half * 512:
                              tt * 1024 + half * 512 + 512]
                TT("vector", dst, dst, bA[0][0:128, 0:512], ALU_.add,
                   [bA[1], x_tm[1]], [x_tm[1]])
                TT("vector", dst, dst, bB[0][0:128, 0:512], ALU_.add,
                   [bB[1], x_tm[1]], [x_tm[1]])
        for tt in range(4):
            s, hi = tt // 2, tt % 2
            DMA("qa", out_h[s, c * TC + hi * 128:c * TC + hi * 128 + 128, :],
                x_tm[0][:, tt * 1024:(tt + 1) * 1024], [x_tm[1]], ())

    stages = _DBG.get("stages", ("load", "outer", "gla", "mlp", "ffn", "final"))
    if "loadtest" in stages:
        st_load(0)
        for tt in range(4):
            xs = x_tm[0][:, tt * 1024:(tt + 1) * 1024]
            if _DBG.get("lt2"):
                ACT(big[0][:, (tt % 2) * 1024:(tt % 2) * 1024 + 1024], xs,
                    AF.Square, [x_tm[1]], [big[1]])
                RED(stat[0][:, tt:tt + 1],
                    big[0][:, (tt % 2) * 1024:(tt % 2) * 1024 + 1024], "sum",
                    [big[1]], [stat[1]])
            RED(stat[0][:, 4 + tt:5 + tt], xs, "sum",
                [x_tm[1]], [stat[1]])
        dump("statearly", stat[0][:], [stat[1]])
        dump("xtm", x_tm[0][:], [x_tm[1]])
        with es:
            em.finalize()
        return nc
    for c in range(NCH):
        wchunk(c)
        if "load" in stages:
            st_load(c)
        if "outer" in stages:
            st_outer(c)
        if "gla" in stages:
            st_gla(c)
        if "mlp" in stages:
            st_ln2_mlp(c)
        if "ffn" in stages:
            st_ffn_layer(c, 0)
            st_ffn_layer(c, 1)
        if "final" in stages:
            st_final(c)

    with es:
        em.finalize()
    return nc


def kernel(**inputs):
    nmap = {
        "x": None,
        "ln1_w": "L0_attn_ln1_w", "ln1_b": "L0_attn_ln1_b",
        "Wq": "L0_attn_Wq", "Wk": "L0_attn_Wk", "Wv": "L0_attn_Wv",
        "Wg": "L0_attn_Wg", "Wgk1": "L0_attn_Wgk1", "Wgk2": "L0_attn_Wgk2",
        "bgk2": "L0_attn_bgk2", "rms_w": "L0_attn_rms_w", "Wo": "L0_attn_Wo",
        "ln2_w": "L0_attn_ln2_w", "ln2_b": "L0_attn_ln2_b",
        "W1": "L0_attn_W1", "b1": "L0_attn_b1", "W2": "L0_attn_W2",
        "b2": "L0_attn_b2",
        "fwqkv": "L0_ffn_wqkv", "fqs": "L0_ffn_q_scale", "fks": "L0_ffn_k_scale",
        "fwo": "L0_ffn_wo", "flng": "L0_ffn_ln_g", "flnb": "L0_ffn_ln_b",
        "fw1": "L0_ffn_w1", "fb1": "L0_ffn_b1", "fw2": "L0_ffn_w2",
        "fb2": "L0_ffn_b2",
    }
    x = np.ascontiguousarray(np.asarray(inputs["L0_attn_x"], dtype=np.float32))
    shared = {k: np.ascontiguousarray(np.asarray(inputs[v], dtype=np.float32))
              for k, v in nmap.items() if v is not None}
    for k, v in _CONSTS.items():
        shared["c_" + k] = v
    shared["ppk"] = _pack_pp(inputs)
    shared["rowk"] = _pack_rows(inputs)
    fw2 = np.asarray(inputs["L0_ffn_w2"], dtype=np.float32)
    shared["fw2p"] = np.concatenate(
        [fw2, np.zeros((2, 2816 - HID, 1024), np.float32)], axis=1)
    nc = build_program()
    in_maps = []
    for core in range(NCORES):
        m = dict(shared)
        m["x"] = x[core * SEQ_PER_CORE:(core + 1) * SEQ_PER_CORE]
        in_maps.append(m)
    res = run_bass_kernel_spmd(nc, in_maps, core_ids=list(range(NCORES)))
    out = np.concatenate([np.asarray(r["out"]) for r in res.results], axis=0)
    if _DBG["dumps"]:
        kernel._dbg_out = [{k: np.asarray(v) for k, v in r.items()}
                           for r in res.results]
    return out


# version 17
# speedup vs baseline: 1.0122x; 1.0003x over previous
import math
import sys as _sys
import numpy as np
import concourse.bass as bass
import concourse.mybir as mybir
from concourse.bass_utils import run_bass_kernel_spmd
from contextlib import ExitStack

F32 = mybir.dt.float32
BF16 = mybir.dt.bfloat16
AF = mybir.ActivationFunctionType
ALU = mybir.AluOpType
AX = mybir.AxisListType

B, T, D = 16, 2048, 1024
NCORES = 8
SEQ_PER_CORE = B // NCORES
TC = 256
NCHUNK = T // TC
R = SEQ_PER_CORE * TC
WBLK = 516
H_GLA, DK, DV = 4, 256, 256
FFN_HEADS, DIM_HEAD, WINDOW, INNER = 8, 64, 512, 512
HID = 2730
MLP_H = 3072
NEG = -1.0e30

_DBG = {"n_chunks": NCHUNK, "dumps": []}


class Em:
    CENG = ("tensor", "vector", "scalar", "gpsimd", "sync")
    QUEUES = {"qs": "sync", "qa": "scalar", "qg": "gpsimd"}

    def __init__(self, nc):
        self.nc = nc
        self.ops = {e: [] for e in self.CENG}
        self.nq = {q: 0 for q in self.QUEUES}
        self._g = 0
        self.regions = {}
        self._rid = 0
        self.stable = set()
        self.junk = {}

    def region(self, name=None, excl=False):
        name = name or f"r{self._rid}"
        self._rid += 1
        self.regions[name] = {"w": None, "r": [], "x": excl}
        return name

    def _need(self, dep, eng, deps):
        kind = dep[0]
        if kind == "c" and dep[1] == eng:
            return
        key = (kind, dep[1])
        deps[key] = max(deps.get(key, -1), dep[2])

    def op(self, eng, fn, reads=(), writes=(), scalar_reads=(), tag="",
           tail=False, cost=500, actset=None):
        deps = {}
        hazard = 0
        hsrc = []
        idx = len(self.ops[eng])
        for r in list(reads) + list(scalar_reads):
            st = self.regions[r]
            if st.get("x"):
                for rd in st["r"]:
                    if rd[0] != "c" or rd[1] != eng:
                        self._need(rd, eng, deps)
            if st["w"] is not None:
                self._need(st["w"], eng, deps)
                w = st["w"]
                if st.get("wt") and w[0] == "c" and w[1] == eng:
                    hsrc.append(("t", w[1], w[2]))
                    if w[2] >= idx - 2:
                        hazard = max(hazard, 1)
                if r in scalar_reads and r not in self.stable:
                    hsrc.append(("s", None, None))
                    if w[0] != "c" or w[1] != eng or w[2] >= idx - 2:
                        hazard = max(hazard, 1)
        for r in writes:
            st = self.regions[r]
            if st["w"] is not None:
                self._need(st["w"], eng, deps)
            for rd in st["r"]:
                self._need(rd, eng, deps)
        rec = {"fn": fn, "deps": deps, "inc": False, "dma": None,
               "hazard": hazard, "eng": eng, "idx": idx, "tail": tail,
               "cost": cost, "hsrc": hsrc, "g": self._g, "actset": actset,
               "rd": list(reads) + list(scalar_reads), "sr": list(scalar_reads),
               "wr": list(writes),
               "tag": tag or f"L{_sys._getframe(2).f_lineno}"}
        self._g += 1
        self.ops[eng].append(rec)
        me = ("c", eng, idx)
        for r in reads:
            self.regions[r]["r"].append(me)
        for r in scalar_reads:
            self.regions[r]["r"].append(me)
        for r in writes:
            self.regions[r] = {"w": me, "r": [], "wt": tail,
                               "x": self.regions[r].get("x", False)}
        return me

    def dma(self, queue, fn, reads=(), writes=(), tag="", cost=3000):
        eng = self.QUEUES[queue]
        deps = {}
        idx = len(self.ops[eng])
        for r in reads:
            st = self.regions[r]
            if st["w"] is not None:
                self._need(st["w"], eng, deps)
        for r in writes:
            st = self.regions[r]
            if st["w"] is not None:
                self._need(st["w"], eng, deps)
            for rd in st["r"]:
                self._need(rd, eng, deps)
        ordinal = self.nq[queue]
        self.nq[queue] += 1
        rec = {"fn": fn, "deps": deps, "inc": True, "dma": queue, "ord": ordinal,
               "hazard": 0, "eng": eng, "idx": idx, "tail": False,
               "cost": cost, "hsrc": [], "g": self._g,
               "rd": list(reads), "sr": [], "wr": list(writes),
               "tag": tag or f"L{_sys._getframe(2).f_lineno}"}
        self._g += 1
        self.ops[eng].append(rec)
        me = ("q", queue, ordinal)
        for r in reads:
            self.regions[r]["r"].append(me)
        for r in writes:
            self.regions[r] = {"w": me, "r": [], "wt": False,
                               "x": self.regions[r].get("x", False)}
        return me

    def barrier_all(self, region_names):
        pass


    def _reschedule(self):
        allops = sorted((r for ops in self.ops.values() for r in ops),
                        key=lambda r: r["g"])
        nid = {r["g"]: i for i, r in enumerate(allops)}
        edges = [set() for _ in allops]

        regs = {name: {"w": None, "r": [], "x": st.get("x", False),
                       "wt": False}
                for name, st in self.regions.items()}
        for i, r in enumerate(allops):
            for rg in r["rd"]:
                st = regs[rg]
                if st["x"]:
                    for (rdid, rdeng) in st["r"]:
                        if rdeng != r["eng"]:
                            edges[i].add(rdid)
                if st["w"] is not None:
                    edges[i].add(st["w"])
            for rg in r["wr"]:
                st = regs[rg]
                if st["w"] is not None:
                    edges[i].add(st["w"])
                for (rdid, _e) in st["r"]:
                    edges[i].add(rdid)
            edges[i].discard(i)
            for rg in r["rd"]:
                regs[rg]["r"].append((i, r["eng"]))
            for rg in r["wr"]:
                regs[rg] = {"w": i, "r": [], "x": regs[rg]["x"],
                            "wt": r["tail"]}

        ndep = [len(e) for e in edges]
        consumers = [[] for _ in allops]
        for i, e in enumerate(edges):
            for p in e:
                consumers[p].append(i)
        pend = {e: [] for e in self.CENG}
        import heapq
        wait_cnt = list(ndep)
        for i, r in enumerate(allops):
            if wait_cnt[i] == 0:
                heapq.heappush(pend[r["eng"]], (r["g"], i))
        etime = {e: 0.0 for e in self.CENG}
        qtime = {q: 0.0 for q in self.QUEUES}
        fin = [0.0] * len(allops)
        order = {e: [] for e in self.CENG}
        nsched = 0
        import itertools
        cur_set = None
        GWIN = 4000
        while nsched < len(allops):
            cands = [e for e in self.CENG if pend[e]]
            assert cands, "scheduler cycle"
            eng = min(cands, key=lambda e: etime[e])
            if eng == "scalar":
                tmp = []
                best = None
                while pend[eng]:
                    it = heapq.heappop(pend[eng])
                    tmp.append(it)
                    aset = allops[it[1]].get("actset")
                    if aset is None or aset == cur_set:
                        best = it
                        break
                if best is None or best[0] > tmp[0][0] + GWIN:
                    best = tmp[0]
                for it in tmp:
                    if it is not best:
                        heapq.heappush(pend[eng], it)
                g, i = best
            else:
                g, i = heapq.heappop(pend[eng])
            r = allops[i]
            start = max([etime[eng]] + [fin[p] for p in edges[i]])
            cost = r["cost"]
            aset = r.get("actset")
            if aset is not None and aset != cur_set and eng == "scalar":
                cur_set = aset
                cost += 1280.0
            if r["dma"] is not None:
                qs = max(start, qtime[r["dma"]])
                qtime[r["dma"]] = qs + cost
                fin[i] = qs + cost
                etime[eng] = start + 100.0
            else:
                fin[i] = start + cost
                etime[eng] = fin[i]
            order[eng].append(r)
            nsched += 1
            for c in consumers[i]:
                wait_cnt[c] -= 1
                if wait_cnt[c] == 0:
                    heapq.heappush(pend[allops[c]["eng"]],
                                   (allops[c]["g"], c))
        return allops, nid, edges, order

    def finalize(self):
        nc = self.nc
        allops, nid, edges, order = self._reschedule()
        pos = {}
        for e, lst in order.items():
            for k, r in enumerate(lst):
                pos[r["g"]] = k
        gid = {r["g"]: i for i, r in enumerate(allops)}
        qord = {}
        cnt = {q: 0 for q in self.QUEUES}
        for e in self.CENG:
            pass
        for e, lst in order.items():
            for r in lst:
                if r["dma"] is not None:
                    qord[r["g"]] = cnt[r["dma"]]
                    cnt[r["dma"]] += 1
        node_of = {r["g"]: i for i, r in enumerate(allops)}
        need_inc = set()
        dep_nodes = [[] for _ in allops]
        for i, e in enumerate(edges):
            r = allops[i]
            for p in e:
                pr = allops[p]
                if pr["eng"] != r["eng"] or pr["dma"] is not None:
                    dep_nodes[i].append(p)
                    if pr["dma"] is None:
                        need_inc.add(p)
        cum = {}
        for e, lst in order.items():
            c = 0
            cc = []
            for r in lst:
                if r["dma"] is None and node_of[r["g"]] in need_inc:
                    r["inc"] = True
                if r["inc"] and r["dma"] is None:
                    c += 1
                cc.append(c)
            cum[e] = cc
        for e, lst in order.items():
            lastw = {}
            for k, r in enumerate(lst):
                r["hazard"] = 0
                if any(x not in self.stable for x in r["sr"]):
                    r["hazard"] = 1
                for (kind, heng, hidx) in r["hsrc"]:
                    if kind != "t":
                        continue
                for p in dep_nodes[node_of[r["g"]]]:
                    pass
            posk = {r["g"]: k for k, r in enumerate(lst)}
            for k, r in enumerate(lst):
                i = node_of[r["g"]]
                for p in edges[i]:
                    pr = allops[p]
                    if (pr["eng"] == e and pr["dma"] is None and pr["tail"]
                            and k - posk[pr["g"]] <= 2):
                        r["hazard"] = max(r["hazard"], 1)

        es = ExitStack()
        KPOOL = 8
        sems = {}
        pools = {}
        for e in self.CENG:
            sems[e] = es.enter_context(nc.semaphore(f"sem_{e}"))
        for q in self.QUEUES:
            pools[q] = [es.enter_context(nc.semaphore(f"sem_{q}{i}"))
                        for i in range(KPOOL)]
        allsems = list(sems.values()) + [s for p in pools.values() for s in p]
        nums = sorted(h.num for h in allsems)
        rng = range(nums[0], nums[-1] + 1)

        with es:
            nc.all_engine_barrier()
            nc.gpsimd.dma_reset(rng)
            nc.gpsimd.sem_clear(rng)
            nc.all_engine_barrier()
            with nc.Block() as block:
                def section(eng):
                    def body(engobj):
                        water = {}
                        for k, r in enumerate(order[eng]):
                            i = node_of[r["g"]]
                            wts = []
                            for p in dep_nodes[i]:
                                pr = allops[p]
                                if pr["dma"] is not None:
                                    oq = qord[pr["g"]]
                                    lane = oq % KPOOL
                                    val = 16 * (oq // KPOOL + 1)
                                    key = ("q", pr["dma"], lane)
                                    if water.get(key, -1) < val:
                                        wts.append((pools[pr["dma"]][lane],
                                                    val))
                                        water[key] = val
                                else:
                                    val = cum[pr["eng"]][pos[pr["g"]]]
                                    key = ("c", pr["eng"])
                                    if water.get(key, -1) < val:
                                        wts.append((sems[pr["eng"]], val))
                                        water[key] = val
                            for s_, v_ in wts:
                                engobj.wait_ge(s_, v_)
                            if r["hazard"]:
                                j = self.junk.get(eng)
                                for _ in range(2):
                                    if j is None:
                                        engobj.nop()
                                    elif eng == "scalar":
                                        engobj.copy(j, j)
                                    else:
                                        engobj.tensor_copy(j, j)
                            ins = r["fn"](engobj)
                            if r["inc"]:
                                if ins is None:
                                    ins = engobj.nop()
                                if r["dma"]:
                                    q = r["dma"]
                                    lane = qord[r["g"]] % KPOOL
                                    ins.then_inc(pools[q][lane], 16)
                                else:
                                    ins.then_inc(sems[eng], 1)
                        for q, qeng in self.QUEUES.items():
                            if qeng == eng and self.nq[q]:
                                for lane in range(min(KPOOL, self.nq[q])):
                                    n = (self.nq[q] - 1 - lane) // KPOOL + 1
                                    engobj.wait_ge(pools[q][lane], 16 * n)
                    return body

                block.tensor(section("tensor"))
                block.vector(section("vector"))
                block.scalar(section("scalar"))
                block.gpsimd(section("gpsimd"))
                block.sync(section("sync"))
            nc.all_engine_barrier()
            nc.gpsimd.dma_reset(rng)
            nc.gpsimd.sem_clear(rng)
            nc.all_engine_barrier()


def _host_consts():
    c = {}
    c["ident_f32"] = np.eye(128, dtype=np.float32)
    c["ident_bf"] = np.eye(128, dtype=np.float32)
    c["ones_col_bf"] = np.ones((128, 1), dtype=np.float32)
    c["ones_row_bf"] = np.ones((1, 512), dtype=np.float32)
    c["ones_1x128_f32"] = np.ones((1, 128), dtype=np.float32)
    m0 = np.zeros((128, 256), dtype=np.float32)
    m1 = np.zeros((128, 256), dtype=np.float32)
    s_idx = np.arange(128)[:, None]
    t_idx = np.arange(256)[None, :]
    m0[:, :] = (s_idx <= t_idx)
    m1[:, :] = (s_idx + 128 <= t_idx)
    c["gla_mask0"] = m0
    c["gla_mask1"] = m1
    q_i = np.arange(128)[:, None]
    k_j = np.arange(128)[None, :]
    c["mask_causal"] = np.where(k_j <= q_i, 0.0, NEG).astype(np.float32)
    c["mask_edge"] = np.where(k_j >= q_i, 0.0, NEG).astype(np.float32)
    half = np.arange(0, DIM_HEAD, 2, dtype=np.float64)
    inv_freq = 1.0 / 10000.0 ** (half / DIM_HEAD)
    t = np.arange(T, dtype=np.float64)
    freqs = t[:, None] * inv_freq[None, :]
    freqs = np.concatenate([freqs, freqs], axis=-1)
    base = (half + 0.4 * DIM_HEAD) / (1.4 * DIM_HEAD)
    power = (t - T // 2) / float(WINDOW // 2)
    scale = base[None, :] ** power[:, None]
    scale = np.concatenate([scale, scale], axis=-1)
    cos, sin = np.cos(freqs), np.sin(freqs)
    c["tab_qc"] = (8.0 * cos * scale).astype(np.float32)
    c["tab_qs"] = (8.0 * sin * scale).astype(np.float32)
    c["tab_kc"] = (cos / scale).astype(np.float32)
    c["tab_ks"] = (sin / scale).astype(np.float32)
    fc = np.zeros((128, 8), dtype=np.float32)
    fc[:, 0] = 1e-5
    fc[:, 1] = 1.0
    fc[:, 2] = 0.5
    fc[:, 3] = 0.01
    c["fconst"] = fc
    return c


_CONSTS = _host_consts()


def _pp_layout():
    cols = {}
    cur = 0
    for name, n in [("ln1_w", 8), ("ln1_b", 8), ("ln2_w", 8), ("ln2_b", 8),
                    ("bgk2", 8), ("b1a", 24), ("b2a", 8),
                    ("flng0", 8), ("flnb0", 8), ("flng1", 8), ("flnb1", 8),
                    ("fb2_0", 8), ("fb2_1", 8),
                    ("fb1a0", 22), ("fb1g0", 22), ("fb1a1", 22), ("fb1g1", 22),
                    ("fb1gs0", 22), ("fb1gs1", 22)]:
        cols[name] = cur
        cur += n
    assert cur <= 256
    return cols


_PPCOLS = _pp_layout()


def _pack_pp(ins):
    pp = np.zeros((128, 256), dtype=np.float32)

    def put(name, vec):
        c = _PPCOLS[name]
        v = np.asarray(vec, dtype=np.float32).reshape(-1)
        nt = (len(v) + 127) // 128
        for t in range(nt):
            seg = v[t * 128:(t + 1) * 128]
            pp[0:len(seg), c + t] = seg
    put("ln1_w", ins["L0_attn_ln1_w"]); put("ln1_b", ins["L0_attn_ln1_b"])
    put("ln2_w", ins["L0_attn_ln2_w"]); put("ln2_b", ins["L0_attn_ln2_b"])
    put("bgk2", ins["L0_attn_bgk2"])
    put("b1a", ins["L0_attn_b1"]); put("b2a", ins["L0_attn_b2"])
    for l in range(2):
        put(f"flng{l}", ins["L0_ffn_ln_g"][l]); put(f"flnb{l}", ins["L0_ffn_ln_b"][l])
        put(f"fb2_{l}", ins["L0_ffn_b2"][l])
        put(f"fb1a{l}", ins["L0_ffn_b1"][l][:HID])
        put(f"fb1g{l}", ins["L0_ffn_b1"][l][HID:])
    return pp


def _pack_rows(ins):
    r = np.zeros((1, 512), dtype=np.float32)
    r[0, 0:256] = np.asarray(ins["L0_attn_rms_w"], np.float32)
    r[0, 256:320] = np.asarray(ins["L0_ffn_q_scale"][0], np.float32)
    r[0, 320:384] = np.asarray(ins["L0_ffn_k_scale"][0], np.float32)
    r[0, 384:448] = np.asarray(ins["L0_ffn_q_scale"][1], np.float32)
    r[0, 448:512] = np.asarray(ins["L0_ffn_k_scale"][1], np.float32)
    return r


def build_program():
    nc = bass.Bass("TRN2", target_bir_lowering=False)
    em = Em(nc)
    es = ExitStack()
    NCH = _DBG["n_chunks"]

    def din(name, shape, dt=F32):
        return nc.dram_tensor(name, list(shape), dt, kind="ExternalInput").ap()

    x_h = din("x", (SEQ_PER_CORE, T, D))
    WH = {}
    for nm, shp in [
        ("ln1_w", (D,)), ("ln1_b", (D,)), ("Wq", (D, D)), ("Wk", (D, D)),
        ("Wv", (D, D)), ("Wg", (D, D)), ("Wgk1", (D, 16)), ("Wgk2", (16, D)),
        ("bgk2", (D,)), ("rms_w", (DV,)), ("Wo", (D, D)), ("ln2_w", (D,)),
        ("ln2_b", (D,)), ("W1", (D, MLP_H)), ("b1", (MLP_H,)),
        ("W2", (MLP_H, D)), ("b2", (D,)),
        ("fwqkv", (2, D, 3 * INNER)), ("fqs", (2, DIM_HEAD)), ("fks", (2, DIM_HEAD)),
        ("fwo", (2, INNER, D)), ("flng", (2, D)), ("flnb", (2, D)),
        ("fw1", (2, D, 2 * HID)), ("fb1", (2, 2 * HID)), ("fw2", (2, HID, D)),
        ("fb2", (2, D)),
    ]:
        WH[nm] = din(nm, shp)
    CH = {k: din("c_" + k, v.shape) for k, v in _CONSTS.items()}
    PPH = din("ppk", (128, 256))
    ROWH = din("rowk", (1, 512))
    FW2P = din("fw2p", (2, 2816, 1024))
    WSC = nc.dram_tensor("wscratch", [128 * 400000], BF16,
                         kind="Internal").ap()
    wsc_reg = em.region("wscratch")
    out_h = nc.dram_tensor("out", [SEQ_PER_CORE, T, D], F32,
                           kind="ExternalOutput").ap()
    dbg_h = {nm: nc.dram_tensor("dbg_" + nm, list(shape), F32,
                                kind="ExternalOutput").ap()
             for nm, shape in _DBG["dumps"]}

    def S(name, p, f, dt=F32):
        t = es.enter_context(nc.sbuf_tensor(name, [p, f], dt))
        return (t, em.region(name))

    x_tm = S("x_tm", 128, 4 * 1024)
    xnxf = S("xnxf", 128, 8 * WBLK)
    x1 = S("x1", 128, 8 * 512)
    abuf = S("abuf", 128, 8 * 512, BF16)
    ybuf = S("ybuf", 128, 8 * 512, BF16)
    ofm = S("ofm", 128, 8 * 256, BF16)
    offm = S("offm", 128, 4 * 512, BF16)
    hst = [S(f"hst{s}", 128, 8 * 256, BF16) for s in range(2)]
    kca = [[S(f"kc{l}{s}", 128, 4 * 768, BF16) for s in range(2)] for l in range(2)]
    vca = [[S(f"vc{l}{s}", 128, 6 * 512, BF16) for s in range(2)] for l in range(2)]
    g_q = S("g_q", 128, 512); g_k = S("g_k", 128, 512)
    g_m = S("g_m", 128, 512); g_l = S("g_l", 128, 512); g_s = S("g_s", 128, 512)
    g_s2 = S("g_s2", 128, 512)
    g_qb = S("g_qb", 128, 512, BF16); g_qt = S("g_qt", 128, 512, BF16)
    g_kh = S("g_kh", 128, 512, BF16); g_kt = S("g_kt", 128, 512, BF16)
    g_gg = S("g_gg", 128, 512); g_sg = S("g_sg", 128, 512)
    g_v = S("g_v", 128, 512, BF16); g_at = S("g_at", 128, 512, BF16)
    g_ob = S("g_ob", 128, 512, BF16)
    big = S("big", 128, 2048)
    bigB = (big[0], em.region("bigB"))
    xntt = S("xntt", 128, 1024)
    xnrtt = S("xnrtt", 128, 1024, BF16)
    ftq = S("ftq", 128, 4 * 512, BF16)
    ftk = S("ftk", 128, 4 * 512, BF16)
    fqf = S("fqf", 128, 4 * 512, BF16)
    atn = S("atn", 128, 640, BF16)
    atn2 = S("atn2", 128, 640, BF16)
    rowr = S("rowr", 1, 512)
    rowm = S("rowm", 1, 512)
    stat = S("stat", 128, 64)
    sch = S("sch", 128, 32)
    carry = S("carry", 128, 64)
    ones_rf = S("ones_rf", 1, 128)
    pp = S("pp", 128, 256)
    rmswb = S("rmswb", 128, 256)
    tabs = S("tabs", 128, 512)
    qscb = S("qscb", 128, 256)
    identf = S("identf", 128, 128)
    identb = S("identb", 128, 128, BF16)
    onescb = S("onescb", 128, 1, BF16)
    onesrb = S("onesrb", 1, 512, BF16)
    glam = S("glam", 128, 512, BF16)
    ffnm = S("ffnm", 128, 256, BF16)
    u_bf = S("u_bf", 16, 512, BF16)
    fcv = S("fcv", 128, 8)
    junkt = S("junkt", 128, 4)
    em.junk["vector"] = junkt[0][:, 0:2]
    em.junk["scalar"] = junkt[0][:, 2:4]
    NSLOT = 4
    wsl = [S(f"wsl{i}", 128, 1024, BF16) for i in range(NSLOT)]
    pb = []
    for i in range(8):
        t = es.enter_context(nc.psum_tensor(f"pb{i}", [128, 512], F32))
        pb.append((t, em.region(f"pb{i}", excl=True)))

    def pbf(i):
        return pb[i][0][:, :].bitcast(BF16)

    def TT(eng, out, in0, in1, op, rd, wr, dt_note=""):
        em.op(eng, lambda e, o=out, a=in0, b=in1, p=op:
              e.tensor_tensor(out=o, in0=a, in1=b, op=p), rd, wr)

    def TS(eng, out, in0, s1, s2, op0, op1, rd, wr, sr=()):
        if op1 is None:
            em.op(eng, lambda e, o=out, a=in0, x=s1, p0=op0:
                  e.tensor_scalar(out=o, in0=a, scalar1=x, scalar2=None, op0=p0),
                  rd, wr, sr)
        else:
            em.op(eng, lambda e, o=out, a=in0, x=s1, y=s2, p0=op0, p1=op1:
                  e.tensor_scalar(out=o, in0=a, scalar1=x, scalar2=y,
                                  op0=p0, op1=p1), rd, wr, sr)

    def STT(eng, out, in0, sc, in1, op0, op1, rd, wr, sr=()):
        em.op(eng, lambda e, o=out, a=in0, x=sc, b=in1, p0=op0, p1=op1:
              e.scalar_tensor_tensor(out=o, in0=a, scalar=x, in1=b,
                                     op0=p0, op1=p1), rd, wr, sr)

    _ASET = {AF.Exp: "E", AF.Ln: "E", AF.Erf: "S", AF.Sigmoid: "S"}

    def ACT(out, in_, func, rd, wr, sr=(), accum=None, **kw):
        aset = _ASET.get(func)
        if accum is not None:
            em.op("scalar", lambda e, o=out, i=in_, f=func, k=dict(kw), a=accum:
                  e.activation(o, i, f, accum_out=a, **k), rd, wr, sr,
                  tail=True, actset=aset)
        else:
            em.op("scalar", lambda e, o=out, i=in_, f=func, k=dict(kw):
                  e.activation(o, i, f, **k), rd, wr, sr, actset=aset)

    def SCOPY(out, in_, rd, wr):
        em.op("scalar", lambda e, o=out, i=in_: e.copy(o, i), rd, wr, cost=430)

    def VCOPY(out, in_, rd, wr):
        em.op("vector", lambda e, o=out, i=in_: e.tensor_copy(o, i), rd, wr,
              cost=150)

    def RED(out, in_, op, rd, wr, **kw):
        if op == "sum":
            em.op("vector", lambda e, o=out, i=in_, k=dict(kw):
                  e.reduce_sum(o, i, axis=AX.X, **k), rd, wr, tail=True,
                  cost=650)
        else:
            em.op("vector", lambda e, o=out, i=in_, k=dict(kw):
                  e.reduce_max(o, i, axis=AX.X, **k), rd, wr, tail=True,
                  cost=650)

    def SCAN(out, d0, d1, init, op0, op1, rd, wr, sr=()):
        em.op("vector", lambda e, o=out, a=d0, b=d1, i=init, p0=op0, p1=op1:
              e.tensor_tensor_scan(out=o, data0=a, data1=b, initial=i,
                                   op0=p0, op1=p1), rd, wr, sr, tail=True)

    def MM(out, lhsT, rhs, start, stop, rd, wr):
        em.op("tensor", lambda e, o=out, l=lhsT, r=rhs, a=start, b=stop:
              e.matmul(o, l, r, start=a, stop=b), rd, wr, cost=150)

    def TR(out, in_, ident, rd, wr):
        em.op("tensor", lambda e, o=out, i=in_, d=ident:
              e.transpose(o, i, d), rd, wr, cost=120)

    def RECIP(out, in_, rd, wr):
        em.op("vector", lambda e, o=out, i=in_: e.reciprocal(o, i), rd, wr,
              tail=True, cost=150)


    def DMA(queue, out, in_, rd, wr, slow=False, cost=3000):
        em.dma(queue, lambda e, o=out, i=in_, sl=slow:
               e.dma_start(out=o, in_=i, allow_slow_non_contiguous=sl), rd, wr,
               cost=cost)

    wstate = {"i": 0, "off": 0, "offs": [], "c": 0}

    def wchunk(c):
        wstate["c"] = c
        wstate["i"] = 0
        if c == 0:
            wstate["offs"] = []
        wstate["gi"] = 0

    def wload(in_ap, cols, parts=128):
        assert parts == 128
        sl = wsl[wstate["i"] % NSLOT]
        wstate["i"] += 1
        gi = wstate["gi"]
        if wstate["c"] == 0:
            off = wstate["off"]
            greg = em.region(f"wg{gi}")
            wstate["offs"].append((off, cols, greg))
            wstate["off"] += cols
            assert wstate["off"] <= 400000
            blk = WSC[off * 128:(off + cols) * 128].rearrange(
                "(p c) -> p c", p=128)
            DMA("qg", blk, in_ap, rd=(), wr=[greg])
        off, cols0, greg = wstate["offs"][gi]
        assert cols0 == cols
        blk = WSC[off * 128:(off + cols) * 128].rearrange("(p c) -> p c", p=128)
        DMA("qs", sl[0][:, 0:cols], blk,
            rd=[greg], wr=[sl[1]], cost=4 * cols)
        wstate["gi"] = gi + 1
        return sl

    def wtiles(arr2d, r0, nr, c0, ncols):
        return arr2d[r0:r0 + nr, c0:c0 + ncols]

    def wgroup_rows(arr2d, c0, ncols):
        return arr2d[:, c0:c0 + ncols].rearrange("(f p) c -> p f c", p=128)

    def fmc(fp, s, i):
        return fp * WBLK + 1 + s * 257 + i

    def fms(fp, s, i):
        return fp * WBLK + s * 257 + i

    def plc(fp, tt, i=0):
        return fp * 512 + tt * 128 + i

    XT, XR = xnxf
    A_T, A_R = abuf
    X1T, X1R = x1
    YT, YR = ybuf
    PBT = pb

    def xnxf_seqs(fp, shifted=False):
        off = fp * WBLK + (0 if shifted else 1)
        return XT[:, off:off + 514].rearrange("p (s i) -> p s i", s=2,
                                              i=257)[:, :, 0:256]

    def dump(name, ap, rd):
        if name in dbg_h:
            DMA("qg", dbg_h[name], ap, rd=list(rd), wr=())

    DMA("qa", identf[0][:], CH["ident_f32"], (), [identf[1]])
    DMA("qg", identb[0][:], CH["ident_bf"], (), [identb[1]])
    DMA("qg", onescb[0][:], CH["ones_col_bf"], (), [onescb[1]])
    DMA("qg", onesrb[0][0:1, :], CH["ones_row_bf"], (), [onesrb[1]])
    DMA("qg", glam[0][:, 0:256], CH["gla_mask0"], (), [glam[1]])
    DMA("qg", glam[0][:, 256:512], CH["gla_mask1"], (), [glam[1]])
    DMA("qg", ffnm[0][:, 0:128], CH["mask_causal"], (), [ffnm[1]])
    DMA("qg", ffnm[0][:, 128:256], CH["mask_edge"], (), [ffnm[1]])
    DMA("qa", ones_rf[0][:], CH["ones_1x128_f32"], (), [ones_rf[1]])
    em.stable.add(ones_rf[1])

    DMA("qa", fcv[0][:], CH["fconst"], (), [fcv[1]])
    for _i, _v in enumerate([1e-5, 1.0, 0.5, 0.01]):
        nc.const_aps.register(F32, _v, fcv[0][:, _i:_i + 1])
    em.stable.add(fcv[1])
    em.op("scalar", lambda e: e.copy(sch[0][0:1, 31:32], fcv[0][0:1, 0:1]),
          [fcv[1]], [sch[1]])
    em.op("vector", lambda e: e.tensor_copy(sch[0][0:1, 30:31],
                                            fcv[0][0:1, 0:1]),
          [fcv[1]], [sch[1]])

    PPC = dict(_PPCOLS)
    DMA("qa", pp[0][:], PPH, (), [pp[1]])
    crows = S("crows", 1, 512)
    DMA("qa", crows[0][:], ROWH, (), [crows[1]])
    em.stable.add(crows[1])
    for srcn, dstn in (("fb1g0", "fb1gs0"), ("fb1g1", "fb1gs1")):
        em.op("scalar", lambda e, a=PPC[srcn], b=PPC[dstn]:
              e.mul(pp[0][:, b:b + 22], pp[0][:, a:a + 22],
                    0.7071067811865476), [pp[1]], [pp[1]])
    wgk1t = S("wgk1t", 128, 128, BF16)
    wgk2t = S("wgk2t", 16, 1024, BF16)
    DMA("qg", wgk1t[0][:], wgroup_rows(WH["Wgk1"], 0, 16), (), [wgk1t[1]])
    DMA("qa", big[0][0:16, 0:1024], WH["Wgk2"][0:16, :], (), [big[1]])
    SCOPY(wgk2t[0][0:16, :], big[0][0:16, 0:1024], [big[1]], [wgk2t[1]])
    em.stable.add(wgk1t[1]); em.stable.add(wgk2t[1])
    MM(pb[6][0][0:128, 0:256], ones_rf[0][:], crows[0][0:1, 0:256], True, True,
       [ones_rf[1], crows[1]], [pb[6][1]])
    VCOPY(rmswb[0][:], pb[6][0][0:128, 0:256], [pb[6][1]], [rmswb[1]])
    for i in range(4):
        MM(pb[7][0][0:128, 0:64], ones_rf[0][:],
           crows[0][0:1, 256 + i * 64:256 + (i + 1) * 64], True, True,
           [ones_rf[1], crows[1]], [pb[7][1]])
        VCOPY(qscb[0][:, i * 64:(i + 1) * 64], pb[7][0][0:128, 0:64],
              [pb[7][1]], [qscb[1]])

    em.op("gpsimd", lambda e: e.memset(hst[0][0][:], 0.0), (), [hst[0][1]])
    em.op("gpsimd", lambda e: e.memset(hst[1][0][:], 0.0), (), [hst[1][1]])
    em.op("gpsimd", lambda e: e.memset(carry[0][:], 0.0), (), [carry[1]])
    em.op("gpsimd", lambda e: e.memset(ybuf[0][:], 0.0), (), [ybuf[1]])
    em.op("gpsimd", lambda e: e.memset(xnxf[0][:], 0.0), (), [xnxf[1]])

    for rname in [identf[1], identb[1], onescb[1], onesrb[1], glam[1], ffnm[1],
                  pp[1], rmswb[1], qscb[1]]:
        em.stable.add(rname)
    em.op("scalar", lambda e: e.nop(),
          [], [fcv[1], identf[1], pp[1], crows[1], wgk2t[1]])
    em.op("gpsimd", lambda e: e.nop(),
          [], [identb[1], onescb[1], onesrb[1], glam[1], ffnm[1],
               wgk1t[1]])

    ALU_ = ALU

    def st_load(c):
        for tt in range(4):
            s, hi = tt // 2, tt % 2
            DMA("qa", x_tm[0][:, tt * 1024:(tt + 1) * 1024],
                x_h[s, c * TC + hi * 128: c * TC + hi * 128 + 128, :],
                (), [x_tm[1]], cost=8000)

    def st_outer(c):
        xr, sr_, br = x_tm[1], stat[1], big[1]
        for tt in range(4):
            xs = x_tm[0][:, tt * 1024:(tt + 1) * 1024]
            br_t = br if tt % 2 == 0 else bigB[1]
            ACT(big[0][:, (tt % 2) * 1024:(tt % 2) * 1024 + 1024], xs,
                AF.Square, [xr], [br_t])
            RED(stat[0][:, tt:tt + 1],
                big[0][:, (tt % 2) * 1024:(tt % 2) * 1024 + 1024], "sum",
                [br_t], [sr_])
            RED(stat[0][:, 4 + tt:5 + tt], xs, "sum", [xr], [sr_])
        TS("vector", stat[0][:, 8:12], stat[0][:, 4:8], 1.0 / D, None,
           ALU_.mult, None, [sr_], [sr_])
        ACT(stat[0][:, 24:28], stat[0][:, 8:12], AF.Square, [sr_], [sr_])
        STT("vector", stat[0][:, 28:32], stat[0][:, 0:4], 1.0 / D,
            stat[0][:, 24:28], ALU_.mult, ALU_.subtract, [sr_], [sr_])
        ACT(stat[0][:, 32:36], stat[0][:, 28:32], AF.Ln, [sr_], [sr_], bias=1e-5)
        ACT(stat[0][:, 12:16], stat[0][:, 32:36], AF.Exp, [sr_], [sr_], scale=-0.5)
        TT("vector", stat[0][:, 16:20], stat[0][:, 8:12], stat[0][:, 12:16],
           ALU_.mult, [sr_], [sr_])
        ACT(stat[0][:, 36:40], stat[0][:, 12:16], AF.Square, [sr_], [sr_])
        TT("vector", stat[0][:, 40:44], stat[0][:, 28:32], stat[0][:, 36:40],
           ALU_.mult, [sr_], [sr_])
        ACT(stat[0][:, 44:48], stat[0][:, 40:44], AF.Ln, [sr_], [sr_], bias=1e-5)
        ACT(stat[0][:, 20:24], stat[0][:, 44:48], AF.Exp, [sr_], [sr_], scale=-0.5)
        dump("statearly", stat[0][:], [sr_])
        if _DBG.get("outer_stats_only"):
            dump("xtm", x_tm[0][:], [x_tm[1]])
            return
        for s in range(2):
            for hi in range(2):
                tt = 2 * s + hi
                xs = x_tm[0][:, tt * 1024:(tt + 1) * 1024]
                br_h = br if hi == 0 else bigB[1]
                TS("vector", big[0][:, hi * 1024:(hi + 1) * 1024], xs,
                   stat[0][:, 12 + tt:13 + tt], stat[0][:, 16 + tt:17 + tt],
                   ALU_.mult, ALU_.subtract, [xr], [br_h], sr=[sr_])
                TS("vector", xnrtt[0][:], big[0][:, hi * 1024:(hi + 1) * 1024],
                   stat[0][:, 20 + tt:21 + tt], None, ALU_.mult, None,
                   [br_h], [xnrtt[1]], sr=[sr_])
                for fp in range(8):
                    bi = 2 + fp % 2
                    b = pb[bi]
                    TR(pbf(bi)[0:128, 0:128], xnrtt[0][:, fp * 128:(fp + 1) * 128],
                       identb[0][:], [xnrtt[1], identb[1]], [b[1]])
                    ACT(abuf[0][:, fp * 512 + tt * 128: fp * 512 + tt * 128 + 128],
                        pbf(bi)[0:128, 0:128], AF.Identity, [b[1], pp[1]], [abuf[1]],
                        scale=pp[0][:, PPC["ln1_w"] + fp:PPC["ln1_w"] + fp + 1],
                        bias=pp[0][:, PPC["ln1_b"] + fp:PPC["ln1_b"] + fp + 1])
            for fp in range(8):
                b = pb[fp % 2]
                for hi in range(2):
                    TR(b[0][0:128, hi * 128:hi * 128 + 128],
                       big[0][:, hi * 1024 + fp * 128: hi * 1024 + fp * 128 + 128],
                       identf[0][:], [br if hi == 0 else bigB[1], identf[1]],
                       [b[1]])
                SCOPY(xnxf[0][:, fmc(fp, s, 0):fmc(fp, s, 0) + 256],
                      b[0][0:128, 0:256], [b[1]], [xnxf[1]])
        dump("xn", xnxf[0][:], [xnxf[1]])
        dump("stat", stat[0][:], [stat[1]])
        dump("xn1", abuf[0][:], [abuf[1]])

    def st_gla(c):
        ar, ppr = abuf[1], pp[1]
        CUTS = ["u", "gk", "proj", "gate", "gv", "at", "o", "otr", "state", "wo"]
        cut = _DBG.get("gla_cut", "wo")
        nsh = _DBG.get("gla_sh", 8)

        def upto(name):
            return CUTS.index(cut) >= CUTS.index(name)
        for fin in range(8):
            MM(pb[4][0][0:16, 0:512], wgk1t[0][:, fin * 16:(fin + 1) * 16],
               abuf[0][:, fin * 512:(fin + 1) * 512], fin == 0, fin == 7,
               [wgk1t[1], ar], [pb[4][1]])
        SCOPY(u_bf[0][0:16, :], pb[4][0][0:16, 0:512], [pb[4][1]], [u_bf[1]])
        if not upto("gk"):
            return
        for s in range(2):
            for h in range(H_GLA):
                if s * 4 + h >= nsh:
                    continue
                par = (s * 4 + h) % 2
                BZ0, BZ1, BP, BG0, BG1, BO, BT1, BT2 = (
                    (4, 5, 6, 2, 3, 7, 4, 5) if par == 0 else
                    (0, 1, 2, 0, 1, 3, 0, 1))
                g_lx, g_mx = g_l, g_m
                g_sx = g_s if par == 0 else g_s2
                for kk in range(2):
                    zb = pb[BZ0 + kk * (BZ1 - BZ0)]
                    MM(zb[0][0:128, 0:256],
                       wgk2t[0][0:16, h * 256 + kk * 128:h * 256 + kk * 128 + 128],
                       u_bf[0][0:16, s * 256:(s + 1) * 256], True, True,
                       [wgk2t[1], u_bf[1]], [zb[1]])
                    ksl = slice(kk * 256, (kk + 1) * 256)
                    bg = pp[0][:, PPC["bgk2"] + 2 * h + kk:PPC["bgk2"] + 2 * h + kk + 1]
                    ACT(g_lx[0][:, ksl], zb[0][0:128, 0:256], AF.Abs,
                        [zb[1]], [g_lx[1]], sr=[ppr], bias=bg, scale=1.0)
                    ACT(g_sx[0][:, ksl], g_lx[0][:, ksl], AF.Exp,
                        [g_lx[1]], [g_sx[1]], scale=-1.0)
                    ACT(g_lx[0][:, ksl], g_sx[0][:, ksl], AF.Ln,
                        [g_sx[1]], [g_lx[1]], bias=1.0)
                    TS("vector", g_mx[0][:, ksl], zb[0][0:128, 0:256], bg, 0.0,
                       ALU_.add, ALU_.min, [zb[1]], [g_mx[1]], sr=[ppr])
                    SCAN(g_sx[0][:, ksl], g_mx[0][:, ksl], g_lx[0][:, ksl], 0.0,
                         ALU_.add, ALU_.subtract, [g_mx[1], g_lx[1]], [g_sx[1]])
                for kk in range(2):
                    VCOPY(sch[0][:, kk:kk + 1],
                          g_sx[0][:, kk * 256 + 255:kk * 256 + 256],
                          [g_sx[1]], [sch[1]])
                ACT(sch[0][:, 2:4], sch[0][:, 0:2], AF.Exp, [sch[1]], [sch[1]],
                    scale=1.0 / 16.0)
                ACT(sch[0][:, 4:6], sch[0][:, 0:2], AF.Exp, [sch[1]], [sch[1]],
                    scale=-1.0 / 16.0)
                TT("vector", g_mx[0][:].rearrange("p (c i) -> p c i", c=2),
                   g_sx[0][:].rearrange("p (c i) -> p c i", c=2),
                   sch[0][:, 0:2].unsqueeze(2).broadcast_to([128, 2, 256]),
                   ALU_.subtract, [g_sx[1], sch[1]], [g_mx[1]])
                ACT(g_lx[0][:], g_sx[0][:], AF.Exp, [g_sx[1]], [g_lx[1]],
                    scale=1.0 / 16.0)
                ACT(g_sx[0][:], g_mx[0][:], AF.Exp, [g_mx[1]], [g_sx[1]],
                    scale=-1.0 / 16.0)
                if not upto("proj"):
                    continue
                for Wn, dst in (("Wq", g_q), ("Wk", g_k)):
                    for kk in range(2):
                        grp = wload(wgroup_rows(WH[Wn], h * 256 + kk * 128, 128),
                                    1024)
                        bk = pb[BP]
                        for fin in range(8):
                            MM(bk[0][0:128, 0:256],
                               grp[0][:, fin * 128:(fin + 1) * 128],
                               abuf[0][:, fin * 512 + s * 256:
                                        fin * 512 + s * 256 + 256],
                               fin == 0, fin == 7, [grp[1], ar], [bk[1]])
                        VCOPY(dst[0][:, kk * 256:(kk + 1) * 256],
                              bk[0][0:128, 0:256], [bk[1]], [dst[1]])
                if not upto("gate"):
                    continue
                STT("vector", g_qb[0][:], g_q[0][:], 1.0 / 16.0, g_lx[0][:],
                    ALU_.mult, ALU_.mult, [g_q[1], g_lx[1]], [g_qb[1]])
                TT("vector", g_qt[0][:].rearrange("p (c i) -> p c i", c=2),
                   g_qb[0][:].rearrange("p (c i) -> p c i", c=2),
                   sch[0][:, 4:6].unsqueeze(2).broadcast_to([128, 2, 256]),
                   ALU_.mult, [g_qb[1], sch[1]], [g_qt[1]])
                TT("vector", g_kh[0][:], g_k[0][:], g_sx[0][:], ALU_.mult,
                   [g_k[1], g_sx[1]], [g_kh[1]])
                if not upto("gv"):
                    continue
                for Wn, dst in (("Wg", g_gg), ("Wv", g_v)):
                    for half in range(2):
                        grp = wload(
                            WH[Wn][half * 512:(half + 1) * 512,
                                   h * 256:(h + 1) * 256].rearrange(
                                "(f p) c -> p f c", p=128), 1024)
                        for hi in range(2):
                            tt = 2 * s + hi
                            bgv = pb[BG0 if hi == 0 else BG1]
                            for f4 in range(4):
                                fin = half * 4 + f4
                                MM(bgv[0][0:128, 256:512],
                                   abuf[0][:, fin * 512 + tt * 128:
                                            fin * 512 + tt * 128 + 128],
                                   grp[0][:, f4 * 256:(f4 + 1) * 256],
                                   half == 0 and f4 == 0, half == 1 and f4 == 3,
                                   [grp[1], ar], [bgv[1]])
                    for hi in range(2):
                        bgv = pb[BG0 if hi == 0 else BG1]
                        if dst is g_v:
                            SCOPY(dst[0][:, hi * 256:(hi + 1) * 256],
                                  bgv[0][0:128, 256:512],
                                  [bgv[1]], [dst[1]])
                        else:
                            VCOPY(dst[0][:, hi * 256:(hi + 1) * 256],
                                  bgv[0][0:128, 256:512],
                                  [bgv[1]], [dst[1]])
                if not upto("at"):
                    continue
                for sh in range(2):
                    for kk in range(2):
                        MM(pb[BP][0][0:128, sh * 256:(sh + 1) * 256],
                           g_kh[0][:, kk * 256 + sh * 128:kk * 256 + sh * 128 + 128],
                           g_qt[0][:, kk * 256:(kk + 1) * 256],
                           kk == 0, kk == 1, [g_kh[1], g_qt[1]], [pb[BP][1]])
                    TT("vector", g_at[0][:, sh * 256:(sh + 1) * 256],
                       pb[BP][0][0:128, sh * 256:(sh + 1) * 256],
                       glam[0][:, sh * 256:(sh + 1) * 256],
                       ALU_.mult, [pb[BP][1], glam[1]], [g_at[1]])
                if not upto("o"):
                    continue
                for hi in range(2):
                    ob = pb[BO]
                    MM(ob[0][0:128, 0:256],
                       g_at[0][:, 0 * 256 + hi * 128:0 * 256 + hi * 128 + 128],
                       g_v[0][:, 0:256], True, False,
                       [g_at[1], g_v[1]], [ob[1]])
                    MM(ob[0][0:128, 0:256],
                       g_at[0][:, 1 * 256 + hi * 128:1 * 256 + hi * 128 + 128],
                       g_v[0][:, 256:512], False, False,
                       [g_at[1], g_v[1]], [ob[1]])
                    MM(ob[0][0:128, 0:256],
                       g_qb[0][:, 0 * 256 + hi * 128:0 * 256 + hi * 128 + 128],
                       hst[s][0][:, (2 * h) * 256:(2 * h) * 256 + 256],
                       False, False, [g_qb[1], hst[s][1]], [ob[1]])
                    MM(ob[0][0:128, 0:256],
                       g_qb[0][:, 1 * 256 + hi * 128:1 * 256 + hi * 128 + 128],
                       hst[s][0][:, (2 * h + 1) * 256:(2 * h + 1) * 256 + 256],
                       False, True, [g_qb[1], hst[s][1]], [ob[1]])
                    hsl = slice(hi * 256, (hi + 1) * 256)
                    ACT(big[0][:, hi * 256:(hi + 1) * 256], ob[0][0:128, 0:256],
                        AF.Square, [ob[1]], [big[1]])
                    RED(sch[0][:, 8 + hi:9 + hi],
                        big[0][:, hi * 256:(hi + 1) * 256], "sum",
                        [big[1]], [sch[1]])
                    ACT(sch[0][:, 10 + hi:11 + hi], sch[0][:, 8 + hi:9 + hi],
                        AF.Ln, [sch[1]], [sch[1]], scale=1.0 / 256.0, bias=1e-5)
                    ACT(sch[0][:, 12 + hi:13 + hi], sch[0][:, 10 + hi:11 + hi],
                        AF.Exp, [sch[1]], [sch[1]], scale=-0.5)
                    ACT(g_sg[0][:, hsl], g_gg[0][:, hsl], AF.Sigmoid,
                        [g_gg[1]], [g_sg[1]])
                    TT("vector", g_sg[0][:, hsl], g_gg[0][:, hsl],
                       g_sg[0][:, hsl], ALU_.mult, [g_gg[1], g_sg[1]], [g_sg[1]])
                    STT("vector", g_gg[0][:, hsl], ob[0][0:128, 0:256],
                        sch[0][:, 12 + hi:13 + hi], g_sg[0][:, hsl],
                        ALU_.mult, ALU_.mult, [ob[1], g_sg[1]], [g_gg[1]],
                        sr=[sch[1]])
                    TT("vector", g_ob[0][:, hsl], g_gg[0][:, hsl],
                       rmswb[0][:, 0:256], ALU_.mult, [g_gg[1], rmswb[1]],
                       [g_ob[1]])
                if not upto("otr"):
                    continue
                for vv in range(2):
                    for hi in range(2):
                        TR(pbf(BT1)[0:128, 512 + hi * 128:512 + hi * 128 + 128],
                           g_ob[0][:, hi * 256 + vv * 128:hi * 256 + vv * 128 + 128],
                           identb[0][:], [g_ob[1], identb[1]], [pb[BT1][1]])
                    SCOPY(ofm[0][:, (2 * h + vv) * 256:(2 * h + vv) * 256 + 256],
                          pbf(BT1)[0:128, 512:768], [pb[BT1][1]], [ofm[1]])
                if not upto("state"):
                    continue
                for sh in range(2):
                    for kk in range(2):
                        TR(pbf(BT2)[0:128, 512 + kk * 128:512 + kk * 128 + 128],
                           g_kh[0][:, kk * 256 + sh * 128:kk * 256 + sh * 128 + 128],
                           identb[0][:], [g_kh[1], identb[1]], [pb[BT2][1]])
                    SCOPY(g_kt[0][:, sh * 256:(sh + 1) * 256],
                          pbf(BT2)[0:128, 512:768], [pb[BT2][1]], [g_kt[1]])
                for kk in range(2):
                    for sh in range(2):
                        MM(pb[BP][0][0:128, kk * 256:(kk + 1) * 256],
                           g_kt[0][:, sh * 256 + kk * 128:sh * 256 + kk * 128 + 128],
                           g_v[0][:, sh * 256:(sh + 1) * 256],
                           sh == 0, sh == 1, [g_kt[1], g_v[1]], [pb[BP][1]])
                    TS("vector", g_sg[0][:, kk * 256:(kk + 1) * 256],
                       hst[s][0][:, (2 * h + kk) * 256:(2 * h + kk) * 256 + 256],
                       sch[0][:, 2 + kk:3 + kk], None, ALU_.mult, None,
                       [hst[s][1]], [g_sg[1]], sr=[sch[1]])
                    TT("vector",
                       hst[s][0][:, (2 * h + kk) * 256:(2 * h + kk) * 256 + 256],
                       g_sg[0][:, kk * 256:(kk + 1) * 256],
                       pb[BP][0][0:128, kk * 256:(kk + 1) * 256], ALU_.add,
                       [g_sg[1], pb[BP][1]], [hst[s][1]])
            if not upto("wo") or nsh < 8:
                continue
            for vin in range(8):
                grp = wload(WH["Wo"][vin * 128:(vin + 1) * 128, :], 1024)
                for fo in range(8):
                    MM(pb[fo][0][0:128, 0:256],
                       grp[0][:, fo * 128:(fo + 1) * 128],
                       ofm[0][:, vin * 256:(vin + 1) * 256],
                       vin == 0, vin == 7, [grp[1], ofm[1]], [pb[fo][1]])
            for fo in range(8):
                TT("vector", x1[0][:, fo * 512 + s * 256:fo * 512 + s * 256 + 256],
                   pb[fo][0][0:128, 0:256],
                   xnxf[0][:, fmc(fo, s, 0):fmc(fo, s, 0) + 256],
                   ALU_.add, [pb[fo][1], xnxf[1]], [x1[1]])
        dump("x1", x1[0][:], [x1[1]])

    def st_ln2_mlp(c):
        xr = x1[1]
        SCOPY(ybuf[0][:, 0:4096], x1[0][:, 0:4096], [xr], [ybuf[1]])
        for fp in range(8):
            MM(pb[4][0][0:1, 0:512], onescb[0][:],
               ybuf[0][:, fp * 512:(fp + 1) * 512], fp == 0, fp == 7,
               [onescb[1], ybuf[1]], [pb[4][1]])
        ACT(ybuf[0][:, 0:4096], x1[0][:, 0:4096], AF.Square, [xr], [ybuf[1]])
        for fp in range(8):
            MM(pb[5][0][0:1, 0:512], onescb[0][:],
               ybuf[0][:, fp * 512:(fp + 1) * 512], fp == 0, fp == 7,
               [onescb[1], ybuf[1]], [pb[5][1]])
        _ln_rows_and_bcast(pb[4], pb[5])
        for half in range(2):
            _ln_apply_fm(x1[0][:, half * 2048:(half + 1) * 2048].rearrange(
                "p (f s i) -> p f s i", f=4, s=2), half, "ln2_w", "ln2_b", [xr],
                pb[4], pb[5])
        for half in range(3):
            for j in range(8):
                jj = half * 8 + j
                grp = wload(wgroup_rows(WH["W1"], jj * 128, 128), 1024)
                ub = pb[4 + j % 4]
                for fin in range(8):
                    MM(ub[0][0:128, 0:512], grp[0][:, fin * 128:(fin + 1) * 128],
                       abuf[0][:, fin * 512:(fin + 1) * 512], fin == 0, fin == 7,
                       [grp[1], abuf[1]], [ub[1]])
                bcol = pp[0][:, PPC["b1a"] + jj:PPC["b1a"] + jj + 1]
                uslc = slice((j % 4) * 512, (j % 4) * 512 + 512)
                ureg = big[1] if j % 4 < 2 else bigB[1]
                ACT(big[0][:, uslc], ub[0][0:128, 0:512], AF.Prelu,
                    [ub[1]], [ureg], sr=[pp[1]], bias=bcol, alpha=0.01)
                ACT(ybuf[0][:, j * 512:(j + 1) * 512],
                    big[0][:, uslc], AF.Square, [ureg], [ybuf[1]])
            for j in range(8):
                jj = half * 8 + j
                grp = wload(WH["W2"][jj * 128:(jj + 1) * 128, :], 1024)
                for fo in range(8):
                    MM(pb[fo][0][0:128, 0:512], grp[0][:, fo * 128:(fo + 1) * 128],
                       ybuf[0][:, j * 512:(j + 1) * 512], j == 0, j == 7,
                       [grp[1], ybuf[1]], [pb[fo][1]])
            for fo in range(8):
                if half < 2:
                    TT("vector", x1[0][:, fo * 512:(fo + 1) * 512],
                       x1[0][:, fo * 512:(fo + 1) * 512], pb[fo][0][0:128, 0:512],
                       ALU_.add, [pb[fo][1], xr], [xr])
                else:
                    STT("vector", x1[0][:, fo * 512:(fo + 1) * 512],
                        pb[fo][0][0:128, 0:512],
                        pp[0][:, PPC["b2a"] + fo:PPC["b2a"] + fo + 1],
                        x1[0][:, fo * 512:(fo + 1) * 512], ALU_.add, ALU_.add,
                        [pb[fo][1], xr], [xr], sr=[pp[1]])
        dump("attnres", x1[0][:], [x1[1]])

    def _ln_rows_and_bcast(sum_pb, ssq_pb):
        TS("vector", rowm[0][0:1, :], sum_pb[0][0:1, 0:512], 1.0 / D, None,
           ALU_.mult, None, [sum_pb[1]], [rowm[1]])
        ACT(rowr[0][0:1, :], rowm[0][0:1, :], AF.Square, [rowm[1]], [rowr[1]])
        STT("vector", rowr[0][0:1, :], ssq_pb[0][0:1, 0:512], 1.0 / D,
            rowr[0][0:1, :], ALU_.mult, ALU_.subtract,
            [ssq_pb[1], rowr[1]], [rowr[1]])
        ACT(rowr[0][0:1, :], rowr[0][0:1, :], AF.Ln, [rowr[1]], [rowr[1]],
            bias=1e-5)
        ACT(rowr[0][0:1, :], rowr[0][0:1, :], AF.Exp, [rowr[1]], [rowr[1]],
            scale=-0.5)
        TT("vector", rowm[0][0:1, :], rowm[0][0:1, :], rowr[0][0:1, :],
           ALU_.mult, [rowm[1], rowr[1]], [rowm[1]])
        MM(sum_pb[0][0:128, 0:512], ones_rf[0][:], rowr[0][0:1, 0:512],
           True, True, [ones_rf[1], rowr[1]], [sum_pb[1]])
        MM(ssq_pb[0][0:128, 0:512], ones_rf[0][:], rowm[0][0:1, 0:512],
           True, True, [ones_rf[1], rowm[1]], [ssq_pb[1]])

    def _ln_apply_fm(in4_ap, half, wname, bname, in_rd, rpb, mpb):
        big4 = big[0][:, 0:2048].rearrange("p (f s i) -> p f s i", f=4, s=2)
        rb4 = rpb[0][0:128, 0:512].rearrange("p (s i) -> p s i", s=2).unsqueeze(
            1).broadcast_to([128, 4, 2, 256])
        mrb4 = mpb[0][0:128, 0:512].rearrange("p (s i) -> p s i", s=2).unsqueeze(
            1).broadcast_to([128, 4, 2, 256])
        TT("vector", big4, in4_ap, rb4, ALU_.mult,
           list(in_rd) + [rpb[1]], [big[1], bigB[1]])
        TT("vector", big4, big4, mrb4, ALU_.subtract,
           [big[1], mpb[1], bigB[1]], [big[1], bigB[1]])
        for f4 in range(4):
            fp = half * 4 + f4
            ACT(abuf[0][:, fp * 512:(fp + 1) * 512],
                big[0][:, f4 * 512:(f4 + 1) * 512], AF.Identity,
                [(big[1] if f4 < 2 else bigB[1]), pp[1]], [abuf[1]],
                scale=pp[0][:, PPC[wname] + fp:PPC[wname] + fp + 1],
                bias=pp[0][:, PPC[bname] + fp:PPC[bname] + fp + 1])

    XV3 = xnxf[0][:].rearrange("p (f x) -> p f x", f=8)

    def _shift_cast(c, site):
        cr, xr = carry[1], xnxf[1]
        for s in range(2):
            VCOPY(XV3[:, :, 257 * s], carry[0][:, site * 16 + s * 8:
                                               site * 16 + s * 8 + 8],
                  [cr], [xr])
        for s in range(2):
            VCOPY(carry[0][:, site * 16 + s * 8:site * 16 + s * 8 + 8],
                  XV3[:, :, 256 + 257 * s], [xr], [cr])
        for half in range(2):
            lo = 1 if half == 0 else 0
            src = XV3[:, half * 4:(half + 1) * 4, lo:lo + 514].rearrange(
                "p f (s i) -> p f s i", s=2, i=257)[:, :, :, 0:256]
            dst = abuf[0][:, half * 2048:(half + 1) * 2048].rearrange(
                "p (f s i) -> p f s i", f=4, s=2, i=256)
            SCOPY(dst, src, [xr], [abuf[1]])

    def st_ffn_layer(c, l):
        lr = "0" if l == 0 else "1"
        ar, xr = abuf[1], xnxf[1]
        _shift_cast(c, l * 2 + 0)
        for p3, dsts in ((0, "q"), (1, "k"), (2, "v")):
            for fin in range(8):
                grp = wload(WH["fwqkv"][l, fin * 128:(fin + 1) * 128,
                                        p3 * 512:(p3 + 1) * 512], 512)
                for tt in range(4):
                    MM(pb[tt][0][0:128, 0:512],
                       abuf[0][:, fin * 512 + tt * 128:fin * 512 + tt * 128 + 128],
                       grp[0][:, 0:512], fin == 0, fin == 7,
                       [grp[1], ar], [pb[tt][1]])
            for tt in range(4):
                s, hi = tt // 2, tt % 2
                if dsts == "q":
                    SCOPY(ftq[0][:, tt * 512:(tt + 1) * 512],
                          pb[tt][0][0:128, 0:512], [pb[tt][1]], [ftq[1]])
                elif dsts == "k":
                    SCOPY(ftk[0][:, tt * 512:(tt + 1) * 512],
                          pb[tt][0][0:128, 0:512], [pb[tt][1]], [ftk[1]])
                else:
                    rr = (c % 3) * 2 + hi
                    SCOPY(vca[l][s][0][:, rr * 512:(rr + 1) * 512],
                          pb[tt][0][0:128, 0:512], [pb[tt][1]], [vca[l][s][1]])
        for qi, nm in enumerate(["tab_qc", "tab_qs", "tab_kc", "tab_ks"]):
            for hi in range(2):
                DMA("qa", tabs[0][:, qi * 128 + hi * 64:qi * 128 + hi * 64 + 64],
                    CH[nm][c * TC + hi * 128: c * TC + hi * 128 + 128, :],
                    (), [tabs[1]])
        for ft, qk in ((ftq, 0), (ftk, 1)):
            fr = ft[1]
            ACT(big[0][:, 0:2048], ft[0][:], AF.Square, [fr], [big[1]])
            RED(stat[0][:, 32:64].rearrange("p (g o) -> p g o", o=1),
                big[0][:, 0:2048].rearrange("p (g d) -> p g d", d=64),
                "sum", [big[1]], [stat[1]])
            TS("vector", stat[0][:, 32:64], stat[0][:, 32:64], 1e-24, None,
               ALU_.max, None, [stat[1]], [stat[1]])
            ACT(stat[0][:, 0:32], stat[0][:, 32:64], AF.Ln, [stat[1]], [stat[1]])
            ACT(stat[0][:, 32:64], stat[0][:, 0:32], AF.Exp, [stat[1]], [stat[1]],
                scale=-0.5)
            f3 = ft[0][:].rearrange("p (g d) -> p g d", d=64)
            TT("vector", f3, f3,
               stat[0][:, 32:64].unsqueeze(2).broadcast_to([128, 32, 64]),
               ALU_.mult, [fr, stat[1]], [fr])
            qs_off = (l * 2 + qk) * 64
            TT("vector", f3, f3,
               qscb[0][:, qs_off:qs_off + 64].unsqueeze(1).broadcast_to(
                   [128, 32, 64]), ALU_.mult, [fr, qscb[1]], [fr])
            tb = tabs[0][:, qk * 256:qk * 256 + 128]
            ts_ = tabs[0][:, qk * 256 + 128:qk * 256 + 256]
            for s in range(2):
                half_ap = ft[0][:, s * 1024:(s + 1) * 1024]
                h4 = half_ap.rearrange("p (hi g d) -> p hi g d", hi=2, d=64)
                cosb = tb.rearrange("p (hi d) -> p hi d", hi=2).unsqueeze(
                    2).broadcast_to([128, 2, 8, 64])
                sinb = ts_.rearrange("p (hi d) -> p hi d", hi=2).unsqueeze(
                    2).broadcast_to([128, 2, 8, 64])
                big4a = big[0][:, 0:1024].rearrange(
                    "p (hi g d) -> p hi g d", hi=2, d=64)
                big4b = big[0][:, 1024:2048].rearrange(
                    "p (hi g d) -> p hi g d", hi=2, d=64)
                TT("vector", big4a, h4, cosb, ALU_.mult,
                   [fr, tabs[1]], [big[1]])
                h4h = half_ap.rearrange("p (b two hd) -> p b two hd",
                                        two=2, hd=32)
                b4h = big[0][:, 1024:2048].rearrange(
                    "p (b two hd) -> p b two hd", two=2, hd=32)
                em.op("scalar", lambda e, o=b4h[:, :, 0, :],
                      i=h4h[:, :, 1, :]: e.mul(o, i, -1.0), [fr], [bigB[1]])
                em.op("scalar", lambda e, o=b4h[:, :, 1, :],
                      i=h4h[:, :, 0, :]: e.copy(o, i), [fr], [bigB[1]])
                TT("vector", big4b, big4b, sinb, ALU_.mult,
                   [bigB[1], tabs[1]], [bigB[1]])
                TT("vector", half_ap, big[0][:, 0:1024], big[0][:, 1024:2048],
                   ALU_.add, [big[1], bigB[1]], [fr])
        for dd in range(4):
            bi = 4 + dd % 2
            for tt in range(4):
                TR(pbf(bi)[0:128, tt * 128:tt * 128 + 128],
                   ftq[0][:, tt * 512 + dd * 128:tt * 512 + dd * 128 + 128],
                   identb[0][:], [ftq[1], identb[1]], [pb[bi][1]])
            SCOPY(fqf[0][:, dd * 512:(dd + 1) * 512], pbf(bi)[0:128, 0:512],
                  [pb[bi][1]], [fqf[1]])
        rslot = (c % 3) * 256
        for dd in range(4):
            bi = 4 + dd % 2
            for tt in range(4):
                TR(pbf(bi)[0:128, tt * 128:tt * 128 + 128],
                   ftk[0][:, tt * 512 + dd * 128:tt * 512 + dd * 128 + 128],
                   identb[0][:], [ftk[1], identb[1]], [pb[bi][1]])
            for s in range(2):
                SCOPY(kca[l][s][0][:, dd * 768 + rslot:dd * 768 + rslot + 256],
                      pbf(bi)[0:128, s * 256:s * 256 + 256], [pb[bi][1]],
                      [kca[l][s][1]])
        if l == 0:
            dump("fqf0", fqf[0][:], [fqf[1]])
            dump("kc00", kca[0][0][0][:], [kca[0][0][1]])
        for s in range(2):
            for hh in range(FFN_HEADS):
                dd, ph = hh // 2, 64 * (hh % 2)
                parm = (s * 8 + hh) % 2
                bA, bB = (pb[6], pb[7]) if parm == 0 else (pb[4], pb[5])
                simo = 0 if parm == 0 else 1024
                simr = big[1] if parm == 0 else bigB[1]
                atnb = atn if parm == 0 else atn2
                sco = 0 if parm == 0 else 8
                for qt in range(2):
                    gq = 2 * c + qt
                    kts = [(g, ("causal" if g == gq else
                                ("edge" if g == gq - 4 else "full")))
                           for g in range(max(0, gq - 4), gq + 1)]
                    nk = len(kts)
                    for ki, (g, kind) in enumerate(kts):
                        bank = bA if ki < 4 else bB
                        col = (ki % 4) * 128
                        rcol = (g // 2 % 3) * 256 + (g % 2) * 128
                        MM(bank[0][0:128, col:col + 128],
                           fqf[0][ph:ph + 64,
                                  dd * 512 + s * 256 + qt * 128:
                                  dd * 512 + s * 256 + qt * 128 + 128],
                           kca[l][s][0][ph:ph + 64,
                                        dd * 768 + rcol:dd * 768 + rcol + 128],
                           True, True, [fqf[1], kca[l][s][1]], [bank[1]])
                        dst = big[0][:, simo + ki * 128:simo + ki * 128 + 128]
                        if kind == "full":
                            SCOPY(dst, bank[0][0:128, col:col + 128],
                                  [bank[1]], [simr])
                        else:
                            moff = 0 if kind == "causal" else 128
                            TT("vector", dst, bank[0][0:128, col:col + 128],
                               ffnm[0][:, moff:moff + 128], ALU_.add,
                               [bank[1], ffnm[1]], [simr])
                    RED(stat[0][:, sco:sco + 1],
                        big[0][:, simo:simo + nk * 128], "max",
                        [simr], [stat[1]])
                    TS("vector", stat[0][:, sco + 1:sco + 2],
                       stat[0][:, sco:sco + 1], -1.0, None,
                       ALU_.mult, None, [stat[1]], [stat[1]])
                    ACT(atnb[0][:, 0:nk * 128], big[0][:, simo:simo + nk * 128],
                        AF.Exp, [simr, stat[1]], [atnb[1]],
                        sr=[stat[1]], bias=stat[0][:, sco + 1:sco + 2])
                    RED(stat[0][:, sco + 2:sco + 3], atnb[0][:, 0:nk * 128],
                        "sum", [atnb[1]], [stat[1]])
                    RECIP(stat[0][:, sco + 3:sco + 4],
                          stat[0][:, sco + 2:sco + 3], [stat[1]], [stat[1]])
                    TS("vector", atnb[0][:, 0:nk * 128], atnb[0][:, 0:nk * 128],
                       stat[0][:, sco + 3:sco + 4], None, ALU_.mult, None,
                       [atnb[1]], [atnb[1]], sr=[stat[1]])
                    if (l, s, hh, qt) == _DBG.get("ffn_probe", (None,) * 4):
                        dump("simdmp", big[0][:], [big[1]])
                        dump("atnpre", atn[0][:], [atn[1]])
                        dump("statdmp", stat[0][:], [stat[1]])
                    bAn = 6 if parm == 0 else 4
                    for ki in range(nk):
                        TR(pbf(bAn)[0:128, ki * 128:(ki + 1) * 128],
                           atnb[0][:, ki * 128:ki * 128 + 128], identb[0][:],
                           [atnb[1], identb[1]], [bA[1]])
                    SCOPY(atnb[0][:, 0:nk * 128], pbf(bAn)[0:128, 0:nk * 128],
                          [bA[1]], [atnb[1]])
                    for ki, (g, kind) in enumerate(kts):
                        rr = (g // 2 % 3) * 2 + (g % 2)
                        MM(bB[0][0:64, 256 + qt * 128:256 + qt * 128 + 128],
                           vca[l][s][0][:, rr * 512 + hh * 64:
                                        rr * 512 + hh * 64 + 64],
                           atnb[0][:, ki * 128:ki * 128 + 128],
                           ki == 0, ki == nk - 1,
                           [vca[l][s][1], atnb[1]], [bB[1]])
                if True:
                    SCOPY(offm[0][ph:ph + 64,
                                  dd * 512 + s * 256:dd * 512 + s * 256 + 256],
                          bB[0][0:64, 256:512], [bB[1]], [offm[1]])
        dump("fq" + lr, ftq[0][:], [ftq[1]])
        dump("fk" + lr, ftk[0][:], [ftk[1]])
        dump("offm" + lr, offm[0][:], [offm[1]])
        for vin in range(4):
            grp = wload(WH["fwo"][l, vin * 128:(vin + 1) * 128, :], 1024)
            for fo in range(8):
                MM(pb[fo][0][0:128, 0:512], grp[0][:, fo * 128:(fo + 1) * 128],
                   offm[0][:, vin * 512:(vin + 1) * 512], vin == 0, vin == 3,
                   [grp[1], offm[1]], [pb[fo][1]])
        for fo in range(8):
            TT("vector", xnxf_seqs(fo),
               pb[fo][0][0:128, 0:512].rearrange("p (s i) -> p s i", s=2),
               xnxf_seqs(fo), ALU_.add, [pb[fo][1], xr], [xr])
        dump("mid" + lr, xnxf[0][:], [xr])
        _shift_cast(c, l * 2 + 1)
        for fp in range(8):
            MM(pb[4][0][0:1, 0:512], onescb[0][:],
               abuf[0][:, fp * 512:(fp + 1) * 512], fp == 0, fp == 7,
               [onescb[1], ar], [pb[4][1]])
        ACT(ybuf[0][:, 0:4096], abuf[0][:, 0:4096], AF.Square, [ar], [ybuf[1]])
        for fp in range(8):
            MM(pb[5][0][0:1, 0:512], onescb[0][:],
               ybuf[0][:, fp * 512:(fp + 1) * 512], fp == 0, fp == 7,
               [onescb[1], ybuf[1]], [pb[5][1]])
        _ln_rows_and_bcast(pb[4], pb[5])
        for half in range(2):
            lo = 1 if half == 0 else 0
            src4 = XV3[:, half * 4:(half + 1) * 4, lo:lo + 514].rearrange(
                "p f (s i) -> p f s i", s=2, i=257)[:, :, :, 0:256]
            _ln_apply_fm(src4, half, "flng" + lr, "flnb" + lr, [xr],
                         pb[4], pb[5])
        for half in range(3):
            npass = (8, 8, 6)[half]
            for j in range(npass):
                jj = half * 8 + j
                ncc = 128 if jj < 21 else 42
                grpA = wload(WH["fw1"][l][:, jj * 128:jj * 128 + ncc].rearrange(
                    "(f p) c -> p f c", p=128), 8 * ncc)
                grpG = wload(WH["fw1"][l][:, HID + jj * 128:
                                          HID + jj * 128 + ncc].rearrange(
                    "(f p) c -> p f c", p=128), 8 * ncc)
                bA, bG = (pb[6], pb[7]) if j % 2 == 0 else (pb[4], pb[5])
                for fin in range(8):
                    MM(bA[0][0:ncc, 0:512],
                       grpA[0][:, fin * ncc:(fin + 1) * ncc],
                       abuf[0][:, fin * 512:(fin + 1) * 512], fin == 0, fin == 7,
                       [grpA[1], ar], [bA[1]])
                for fin in range(8):
                    MM(bG[0][0:ncc, 0:512],
                       grpG[0][:, fin * ncc:(fin + 1) * ncc],
                       abuf[0][:, fin * 512:(fin + 1) * 512], fin == 0, fin == 7,
                       [grpG[1], ar], [bG[1]])
                bga = pp[0][0:ncc, PPC["fb1a" + lr] + jj:PPC["fb1a" + lr] + jj + 1]
                bgg = pp[0][0:ncc, PPC["fb1g" + lr] + jj:PPC["fb1g" + lr] + jj + 1]
                bgs = pp[0][0:ncc, PPC["fb1gs" + lr] + jj:
                            PPC["fb1gs" + lr] + jj + 1]
                eslc = slice(0, 512) if j % 2 == 0 else slice(1024, 1536)
                fslc = slice(512, 1024) if j % 2 == 0 else slice(1536, 2048)
                ACT(big[0][0:ncc, eslc], bG[0][0:ncc, 0:512], AF.Erf,
                    [bG[1]], [big[1]], sr=[pp[1]],
                    scale=0.7071067811865476, bias=bgs)
                ACT(big[0][0:ncc, fslc], big[0][0:ncc, eslc], AF.Identity,
                    [big[1]], [big[1]], scale=0.5, bias=0.5)
                STT("vector", big[0][0:ncc, eslc], bG[0][0:ncc, 0:512],
                    bgg, big[0][0:ncc, fslc], ALU_.add, ALU_.mult,
                    [bG[1], big[1]], [big[1]], sr=[pp[1]])
                STT("vector", ybuf[0][0:ncc, j * 512:(j + 1) * 512],
                    bA[0][0:ncc, 0:512], bga, big[0][0:ncc, eslc],
                    ALU_.add, ALU_.mult, [bA[1], big[1]], [ybuf[1]],
                    sr=[pp[1]])
            for j in range(npass):
                jj = half * 8 + j
                grp = wload(FW2P[l][jj * 128:(jj + 1) * 128, :], 1024)
                for fo in range(8):
                    MM(pb[fo][0][0:128, 0:512],
                       grp[0][:, fo * 128:(fo + 1) * 128],
                       ybuf[0][:, j * 512:(j + 1) * 512], j == 0, j == npass - 1,
                       [grp[1], ybuf[1]], [pb[fo][1]])
            for fo in range(8):
                if half < 2:
                    TT("vector", xnxf_seqs(fo),
                       pb[fo][0][0:128, 0:512].rearrange("p (s i) -> p s i", s=2),
                       xnxf_seqs(fo), ALU_.add, [pb[fo][1], xr], [xr])
                else:
                    STT("vector", xnxf_seqs(fo),
                        pb[fo][0][0:128, 0:512].rearrange("p (s i) -> p s i",
                                                          s=2),
                        pp[0][:, PPC["fb2_" + lr] + fo:PPC["fb2_" + lr] + fo + 1],
                        xnxf_seqs(fo), ALU_.add, ALU_.add,
                        [pb[fo][1], xr], [xr], sr=[pp[1]])
        dump("ffn" + lr, xnxf[0][:], [xr])

    def st_final(c):
        for tt in range(4):
            s, hi = tt // 2, tt % 2
            for half in range(2):
                bA, bB = pb[0 + half], pb[2 + half]
                for f4 in range(4):
                    fp = half * 4 + f4
                    TR(bA[0][0:128, f4 * 128:f4 * 128 + 128],
                       x1[0][:, fp * 512 + tt * 128:fp * 512 + tt * 128 + 128],
                       identf[0][:], [x1[1], identf[1]], [bA[1]])
                    TR(bB[0][0:128, f4 * 128:f4 * 128 + 128],
                       xnxf[0][:, fmc(fp, s, hi * 128):
                               fmc(fp, s, hi * 128) + 128],
                       identf[0][:], [xnxf[1], identf[1]], [bB[1]])
                dst = x_tm[0][:, tt * 1024 + half * 512:
                              tt * 1024 + half * 512 + 512]
                TT("vector", dst, dst, bA[0][0:128, 0:512], ALU_.add,
                   [bA[1], x_tm[1]], [x_tm[1]])
                TT("vector", dst, dst, bB[0][0:128, 0:512], ALU_.add,
                   [bB[1], x_tm[1]], [x_tm[1]])
        for tt in range(4):
            s, hi = tt // 2, tt % 2
            DMA("qa", out_h[s, c * TC + hi * 128:c * TC + hi * 128 + 128, :],
                x_tm[0][:, tt * 1024:(tt + 1) * 1024], [x_tm[1]], ())

    stages = _DBG.get("stages", ("load", "outer", "gla", "mlp", "ffn", "final"))
    if "loadtest" in stages:
        st_load(0)
        for tt in range(4):
            xs = x_tm[0][:, tt * 1024:(tt + 1) * 1024]
            if _DBG.get("lt2"):
                ACT(big[0][:, (tt % 2) * 1024:(tt % 2) * 1024 + 1024], xs,
                    AF.Square, [x_tm[1]], [big[1]])
                RED(stat[0][:, tt:tt + 1],
                    big[0][:, (tt % 2) * 1024:(tt % 2) * 1024 + 1024], "sum",
                    [big[1]], [stat[1]])
            RED(stat[0][:, 4 + tt:5 + tt], xs, "sum",
                [x_tm[1]], [stat[1]])
        dump("statearly", stat[0][:], [stat[1]])
        dump("xtm", x_tm[0][:], [x_tm[1]])
        with es:
            em.finalize()
        return nc
    for c in range(NCH):
        wchunk(c)
        if "load" in stages:
            st_load(c)
        if "outer" in stages:
            st_outer(c)
        if "gla" in stages:
            st_gla(c)
        if "mlp" in stages:
            st_ln2_mlp(c)
        if "ffn" in stages:
            st_ffn_layer(c, 0)
            st_ffn_layer(c, 1)
        if "final" in stages:
            st_final(c)

    with es:
        em.finalize()
    return nc


def kernel(**inputs):
    nmap = {
        "x": None,
        "ln1_w": "L0_attn_ln1_w", "ln1_b": "L0_attn_ln1_b",
        "Wq": "L0_attn_Wq", "Wk": "L0_attn_Wk", "Wv": "L0_attn_Wv",
        "Wg": "L0_attn_Wg", "Wgk1": "L0_attn_Wgk1", "Wgk2": "L0_attn_Wgk2",
        "bgk2": "L0_attn_bgk2", "rms_w": "L0_attn_rms_w", "Wo": "L0_attn_Wo",
        "ln2_w": "L0_attn_ln2_w", "ln2_b": "L0_attn_ln2_b",
        "W1": "L0_attn_W1", "b1": "L0_attn_b1", "W2": "L0_attn_W2",
        "b2": "L0_attn_b2",
        "fwqkv": "L0_ffn_wqkv", "fqs": "L0_ffn_q_scale", "fks": "L0_ffn_k_scale",
        "fwo": "L0_ffn_wo", "flng": "L0_ffn_ln_g", "flnb": "L0_ffn_ln_b",
        "fw1": "L0_ffn_w1", "fb1": "L0_ffn_b1", "fw2": "L0_ffn_w2",
        "fb2": "L0_ffn_b2",
    }
    x = np.ascontiguousarray(np.asarray(inputs["L0_attn_x"], dtype=np.float32))
    shared = {k: np.ascontiguousarray(np.asarray(inputs[v], dtype=np.float32))
              for k, v in nmap.items() if v is not None}
    for k, v in _CONSTS.items():
        shared["c_" + k] = v
    shared["ppk"] = _pack_pp(inputs)
    shared["rowk"] = _pack_rows(inputs)
    fw2 = np.asarray(inputs["L0_ffn_w2"], dtype=np.float32)
    shared["fw2p"] = np.concatenate(
        [fw2, np.zeros((2, 2816 - HID, 1024), np.float32)], axis=1)
    nc = build_program()
    in_maps = []
    for core in range(NCORES):
        m = dict(shared)
        m["x"] = x[core * SEQ_PER_CORE:(core + 1) * SEQ_PER_CORE]
        in_maps.append(m)
    res = run_bass_kernel_spmd(nc, in_maps, core_ids=list(range(NCORES)))
    out = np.concatenate([np.asarray(r["out"]) for r in res.results], axis=0)
    if _DBG["dumps"]:
        kernel._dbg_out = [{k: np.asarray(v) for k, v in r.items()}
                           for r in res.results]
    return out
